# version 14
# baseline (speedup 1.0000x reference)
import contextlib
import numpy as np
import ml_dtypes

import concourse.bass as bass
import concourse.mybir as mybir
from concourse.bass_utils import run_bass_kernel_spmd
from concourse import library_config

BF = ml_dtypes.bfloat16
F32 = mybir.dt.float32
BF16 = mybir.dt.bfloat16
I16 = mybir.dt.int16
U8 = mybir.dt.uint8
Alu = mybir.AluOpType
Act = mybir.ActivationFunctionType

HID = 2048
PROJ = 8512
CONVD = 4352
INTER = 4096
NH = 64
HD = 64
NST = 128
LCH = 128
NKT = HID // 128
NGT = INTER // 128
NHT = CONVD // 128
OC = 67
NCORE = 8
QUAD_GROUPS = [[0, 1, 2, 3], [4, 5, 6, 7]]
PAIR_GROUPS = [[0, 4], [1, 5], [2, 6], [3, 7]]
EPS = 1e-5


def _perm_hid():
    perm = np.empty(HID, np.int64)
    for tb in range(8):
        for par in range(2):
            for p in range(128):
                perm[128 * (2 * tb + par) + p] = 256 * tb + 2 * p + par
    return perm


class Sem:
    __slots__ = ("h", "n", "_cm")

    def __init__(self, nc, name):
        self._cm = nc.semaphore(name)
        self.h = self._cm.__enter__()
        self.n = 0


class Builder:
    def __init__(self, T, debug=False):
        self.T = T
        self.NCH = T // LCH
        self.TSEG = min(512, T)
        self.NTS = T // self.TSEG
        self.debug = debug
        self.nc = bass.Bass("TRN2", target_bir_lowering=False)
        self.q = {"sync": [], "scalar": [], "vector": [], "tensor": [], "gpsimd": []}
        self._stack = contextlib.ExitStack()
        self.dbg_outs = {}
        self._phase_marks = []

    def mark(self, name):
        self._phase_marks.append((name, {k: len(v) for k, v in self.q.items()}))

    def sem(self, name):
        s = Sem(self.nc, name)
        self._sems.append(s)
        return s

    def sb(self, name, shape, dtype):
        return self._cur_stack.enter_context(self.nc.sbuf_tensor(name, list(shape), dtype))

    def ps(self, name, shape, dtype):
        return self._cur_stack.enter_context(self.nc.psum_tensor(name, list(shape), dtype))

    def scope(self):
        return contextlib.ExitStack()

    def _do(self, eng, fn):
        self.q[eng].append(fn)

    def W(self, eng, sem, val=None):
        v = sem.n if val is None else val
        assert v <= sem.n, f"wait on future sem value {v} > {sem.n}"
        if v <= 0:
            return
        self._do(eng, lambda e, h=sem.h, v=v: e.wait_ge(h, v))

    def I(self, eng, fn, sem=None, k=1):
        if sem is not None:
            sem.n += k

        def run(e, fn=fn, sem=sem, k=k):
            ins = fn(e)
            if sem is not None:
                ins.then_inc(sem.h, k)
        self._do(eng, run)

    def D(self, eng):
        self._do(eng, lambda e: e.drain())

    def dbg(self, name, ap_fn, shape, dtype):
        if not self.debug:
            return
        t = self.nc.dram_tensor("dbg_" + name, list(shape), dtype, kind="ExternalOutput").ap()
        self.dbg_outs[name] = "dbg_" + name
        for sem in [self.sDV, self.sAC, self.sPE, self.sGP]:
            self.W("scalar", sem)
        self.I("scalar", lambda e, t=t, f=ap_fn: e.dma_start(t, f()), self.sDP, 16)
        self.W("scalar", self.sDP)

    def dbgsnap(self, name, ap_fn, shape, dtype):
        if not self.debug:
            return
        t = self.nc.dram_tensor("dbg_" + name, list(shape), dtype, kind="ExternalOutput").ap()
        self.dbg_outs[name] = "dbg_" + name
        buf = self.sb("dbgbuf_" + name, shape, dtype)
        self.D("vector")
        self.I("vector", lambda e, b=buf, f=ap_fn: e.tensor_copy(b[:], f()), self.sDV)
        self.W("scalar", self.sDV)
        self.I("scalar", lambda e, t=t, b=buf: e.dma_start(t, b[:]), self.sDP, 16)
        self.W("scalar", self.sDP)

    def build(self):
        nc = self.nc
        T = self.T
        self._sems = []
        with self._stack as top:
            self._cur_stack = top
            self._declare_io()
            s = self
            s.sDP = self.sem("dP")
            s.sPE = self.sem("pe")
            s.sDV = self.sem("dv")
            s.sAC = self.sem("ac")
            s.sGP = self.sem("gp")
            s.sAG = self.sem("ag")
            self._alloc_global()
            self.mark("const")
            self.ph_const()
            with self.scope() as sc:
                self._cur_stack = sc
                self.mark("ln")
                self.ph_ln()
            with self.scope() as sc:
                self._cur_stack = sc
                self.mark("dequant")
                self.ph_dequant()
            self._alloc_mixer_globals()
            with self.scope() as sc:
                self._cur_stack = sc
                self.mark("inproj")
                self.ph_inproj()
            with self.scope() as sc:
                self._cur_stack = sc
                self.mark("mixer")
                self.ph_mixer()
            with self.scope() as sc:
                self._cur_stack = sc
                self.mark("correction")
                self.ph_correction()
            self._grp_state.close()
            with self.scope() as sc:
                self._cur_stack = sc
                self.mark("gating")
                self.ph_gating()
            self._grp_hsy.close()
            with self.scope() as sc:
                self._cur_stack = sc
                self.mark("ffn")
                self.ph_ffn()
            with self.scope() as sc:
                self._cur_stack = sc
                self.mark("combine")
                self.ph_combine()
            self._cur_stack = top
            marks = self._phase_marks + [("end", {k: len(v) for k, v in self.q.items()})]
            with nc.Block() as blk:
                for name in ["sync", "scalar", "vector", "tensor", "gpsimd"]:
                    ops = self.q[name]

                    def body(e, ops=ops, name=name):
                        for mi in range(len(marks) - 1):
                            pname, a = marks[mi][0], marks[mi][1][name]
                            b = marks[mi + 1][1][name]
                            if b <= a:
                                continue
                            with nc.named_scope(pname):
                                for f in ops[a:b]:
                                    f(e)
                        for f in ops[marks[-1][1][name]:]:
                            f(e)
                    getattr(blk, name)(body)
        return nc

    def _declare_io(self):
        nc = self.nc
        T = self.T
        ap = lambda n, s, d, k="ExternalInput": nc.dram_tensor(n, list(s), d, kind=k).ap()
        self.d_x = ap("xin", (HID, 3 + T), F32)
        self.d_win = ap("win", (HID, PROJ), F32)
        self.d_wout = ap("wout", (INTER, HID), F32)
        self.d_convw = ap("convw", (128, NHT * 4), F32)
        self.d_convb = ap("convb", (128, NHT), F32)
        self.d_dtb = ap("dtb", (64, 1), F32)
        self.d_alog = ap("alog", (64, 1), F32)
        self.d_dcol = ap("dcol", (128, NGT), F32)
        self.d_nw = ap("nw", (128, NGT), F32)
        self.d_cg = ap("cg", (512, 4096), U8)
        self.d_cu = ap("cu", (512, 4096), U8)
        self.d_cd = ap("cd", (1024, HID), U8)
        self.d_sg = ap("sg", (32, 4096), F32)
        self.d_su = ap("su", (32, 4096), F32)
        self.d_sd = ap("sd", (64, HID), F32)
        self.d_msumT = ap("msumT", (8, 8), F32)
        self.d_pmask = ap("pmask", (8, 1), F32)
        self.d_out = ap("out", (HID, T), F32, "ExternalOutput")
        ia = lambda n, s, d: nc.dram_tensor(n, list(s), d, kind="Internal").ap()
        self.d_gsp = ia("gate_spill", (INTER, T), BF16)
        self.d_ssp = ia("s_spill", (HID, T), BF16)
        self.d_wsh_g = ia("wsh_g", (1024, 4096), BF16)
        self.d_wsh_u = ia("wsh_u", (1024, 4096), BF16)
        self.d_wsh_d = ia("wsh_d", (2048, HID), BF16)
        self.d_wg = ia("wg_q", (HID, 4096), BF16)
        self.d_wu = ia("wu_q", (HID, 4096), BF16)
        self.d_wd = ia("wd_q", (4096, HID), BF16)
        self.d_xnsh = ia("xn_sh", (HID, T), BF16)
        self.d_xnq = ia("xn_q", (4 * HID, T), BF16)
        self.d_yp = ia("y_part", (4 * HID, T), BF16)
        self.d_yrs = ia("y_rs", (HID, T), BF16)
        self.d_stin = ia("st_in", (129, NH * HD), F32)
        self.d_stout = ia("st_out", (129 * NCORE, NH * HD), F32)
        self.d_aL = ia("aL_d", (64, self.T), F32)
        self.d_rb = ia("rowb_d", (4, 4096), F32)
        self.d_w8 = ia("w8_d", (8, 64), F32)

    def _alloc_global(self):
        T = self.T
        self.idnB = self.sb("idnB", (128, 128), BF16)
        self.idnF = self.sb("idnF", (128, 128), F32)
        self.trimask = self.sb("trimask", (128, 128), F32)
        self.ones1 = self.sb("ones1", (128, 1), F32)
        self.onesT = self.sb("onesT", (64, T), F32)
        self.zc64 = self.sb("zc64", (64, 1), F32)
        self.dcol = self.sb("dcolS", (128, NGT), F32)
        self.dtb = self.sb("dtbS", (64, 1), F32)
        self.nA = self.sb("nA", (64, 1), F32)
        self.convb = self.sb("convbS", (128, NHT), F32)
        self.convw = self.sb("convwS", (128, NHT * 4), F32)
        self.nwcol = self.sb("nwS", (128, NGT), F32)
        self.xn = self.sb("xn", (128, NKT * T), BF16)
        self.xnh = self.sb("xnh", (128, NKT * 3), BF16)

    def _alloc_mixer_globals(self):
        T = self.T
        self._grp_hsy = contextlib.ExitStack()
        st = self._grp_hsy
        self.hsy = st.enter_context(self.nc.sbuf_tensor("hsy", [128, NGT * T], BF16))
        self._grp_state = contextlib.ExitStack()
        st = self._grp_state
        self.scanstack = st.enter_context(self.nc.sbuf_tensor("scanstack", [128, T], F32))
        self.aG = st.enter_context(self.nc.sbuf_tensor("aG", [64, T], F32))
        self.Bt_ = st.enter_context(self.nc.sbuf_tensor("Bbuf", [128, T], BF16))
        self.Ct_ = st.enter_context(self.nc.sbuf_tensor("Cbuf", [128, T], BF16))
        self.Sst = st.enter_context(self.nc.sbuf_tensor("Sstate", [128, NH * HD], F32))
        self.Sst16 = st.enter_context(self.nc.sbuf_tensor("Sstate16", [128, NH * HD], BF16))
        self.I("gpsimd", lambda e: e.memset(self.Sst[:], 0.0))
        self.I("gpsimd", lambda e: e.memset(self.Sst16[:], 0.0), self.sGP)

    def ph_const(self):
        s = self
        nc = self.nc
        s.I("gpsimd", lambda e: e.memset(s.idnB[:], 1.0))
        s.I("gpsimd", lambda e: e.memset(s.idnF[:], 1.0))
        s.I("gpsimd", lambda e: e.memset(s.onesT[:], 1.0))
        s.D("gpsimd")
        s.I("gpsimd", lambda e: e.affine_select(
            s.idnB[:], s.idnB[:], [[-1, 128]], Alu.is_equal, 0.0, channel_multiplier=1))
        s.D("gpsimd")
        s.I("gpsimd", lambda e: e.affine_select(
            s.idnF[:], s.idnF[:], [[-1, 128]], Alu.is_equal, 0.0, channel_multiplier=1))
        s.D("gpsimd")
        s.I("gpsimd", lambda e: e.memset(s.trimask[:], 1.0))
        s.D("gpsimd")
        s.I("gpsimd", lambda e: e.affine_select(
            s.trimask[:], s.trimask[:], [[1, 128]], Alu.is_ge, 0.0, channel_multiplier=-1))
        s.D("gpsimd")
        s.I("gpsimd", lambda e: e.memset(s.ones1[:], 1.0))
        s.I("gpsimd", lambda e: e.memset(s.zc64[:], 0.0), s.sGP)
        s.dCst = self.sem("dCst")
        for dst, src in [(s.dcol, s.d_dcol), (s.dtb, s.d_dtb), (s.convb, s.d_convb),
                         (s.convw, s.d_convw), (s.nwcol, s.d_nw)]:
            s.I("sync", lambda e, d=dst, sr=src: e.dma_start(d[:], sr[:]), s.dCst, 16)
        s.I("sync", lambda e: e.dma_start(s.nA[:], s.d_alog[:]), s.dCst, 16)
        s.W("scalar", s.dCst)
        s.I("scalar", lambda e: e.activation(s.nA[:], s.nA[:], Act.Exp))
        s.D("scalar")
        s.I("scalar", lambda e: e.mul(s.nA[:], s.nA[:], -1.0), s.sAC)
        s.D("scalar")

    def ph_ln(self):
        s = self
        for eng in ["tensor", "scalar", "vector"]:
            s.W(eng, s.sGP)
        T, TS, NTS = self.T, self.TSEG, self.NTS
        xbuf = s.sb("xbuf", (128, NKT * T), F32)
        xh = s.sb("xh", (128, NKT * 3), F32)
        xsq = [s.sb(f"xsq{i}", (128, TS), F32) for i in range(2)]
        srow0 = s.sb("srow0", (1, 3 + T), F32)
        srow1 = s.sb("srow1", (1, 3 + T), F32)
        mrow = s.sb("mrow", (1, 3 + T), F32)
        vrow = s.sb("vrow", (1, 3 + T), F32)
        inv = s.sb("invrow", (1, 3 + T), F32)
        nmi = s.sb("nmirow", (1, 3 + T), F32)
        invb = s.sb("invb", (128, T), F32)
        nmb = s.sb("nmb", (128, T), F32)
        invbH = s.sb("invbH", (128, 3), F32)
        nmbH = s.sb("nmbH", (128, 3), F32)
        xsq_mm = [0, 0]
        dX = s.sem("dX")
        with s.nc.psum_tensor("lnp0", [1, TS], F32) as p0, \
             s.nc.psum_tensor("lnp1", [1, TS], F32) as p1:
            for k in range(NKT):
                s.I("sync", lambda e, k=k: e.dma_start(
                    xbuf[:, k * T:(k + 1) * T], s.d_x[128 * k:128 * (k + 1), 3:3 + T]), dX, 16)
                s.I("sync", lambda e, k=k: e.dma_start(
                    xh[:, 3 * k:3 * k + 3], s.d_x[128 * k:128 * (k + 1), 0:3]), dX, 16)
            ldS = dX.n
            segs = [(3 + i * TS, TS, False) for i in range(NTS)] + [(0, 3, True)]
            for (c0, w, ishalo) in segs:
                for k in range(NKT):
                    src = (lambda k=k, c0=c0, w=w: xh[:, 3 * k:3 * k + w]) if ishalo \
                        else (lambda k=k, c0=c0, w=w: xbuf[:, k * T + (c0 - 3):k * T + (c0 - 3) + w])
                    s.W("tensor", dX, ldS)
                    s.I("tensor", lambda e, src=src, k=k, w=w: e.matmul(
                        p0[0:1, 0:w], s.ones1[:], src(), start=(k == 0), stop=(k == NKT - 1)))
                for k in range(NKT):
                    src = (lambda k=k, c0=c0, w=w: xh[:, 3 * k:3 * k + w]) if ishalo \
                        else (lambda k=k, c0=c0, w=w: xbuf[:, k * T + (c0 - 3):k * T + (c0 - 3) + w])
                    s.W("scalar", dX, ldS)
                    s.W("scalar", s.sPE, xsq_mm[k % 2])
                    s.I("scalar", lambda e, src=src, k=k, w=w: e.activation(
                        xsq[k % 2][:, 0:w], src(), Act.Square), s.sAC)
                    s.W("tensor", s.sAC)
                    s.I("tensor", lambda e, k=k, w=w: e.matmul(
                        p1[0:1, 0:w], s.ones1[:], xsq[k % 2][:, 0:w],
                        start=(k == 0), stop=(k == NKT - 1)), s.sPE)
                    xsq_mm[k % 2] = s.sPE.n
                s.W("vector", s.sPE)
                s.I("vector", lambda e, c0=c0, w=w: e.tensor_copy(srow0[0:1, c0:c0 + w], p0[0:1, 0:w]))
                s.I("vector", lambda e, c0=c0, w=w: e.tensor_copy(srow1[0:1, c0:c0 + w], p1[0:1, 0:w]), s.sDV)
                s.W("tensor", s.sDV)
        V, A = "vector", "scalar"
        s.D(V)
        s.I(V, lambda e: e.tensor_scalar_mul(mrow[:], srow0[:], 1.0 / HID))
        s.I(V, lambda e: e.tensor_scalar_mul(vrow[:], srow1[:], 1.0 / HID), s.sDV)
        s.D(V)
        s.W(A, s.sDV)
        s.I(A, lambda e: e.activation(inv[:], mrow[:], Act.Square), s.sAC)
        s.W(V, s.sAC)
        s.I(V, lambda e: e.tensor_sub(vrow[:], vrow[:], inv[:]))
        s.D(V)
        s.I(V, lambda e: e.tensor_scalar_add(vrow[:], vrow[:], EPS), s.sDV)
        s.D(V)
        s.W(A, s.sDV)
        s.I(A, lambda e: e.activation(inv[:], vrow[:], Act.Sqrt), s.sAC)
        s.W(V, s.sAC)
        s.I(V, lambda e: e.reciprocal(nmi[:], inv[:]))
        s.D(V)
        s.I(V, lambda e: e.tensor_mul(inv[:], nmi[:], nmi[:]))
        s.D(V)
        s.I(V, lambda e: e.tensor_mul(inv[:], inv[:], vrow[:]))
        s.D(V)
        s.I(V, lambda e: e.tensor_scalar(inv[:], inv[:], -0.5, 1.5, Alu.mult, Alu.add))
        s.D(V)
        s.I(V, lambda e: e.tensor_mul(inv[:], inv[:], nmi[:]))
        s.D(V)
        s.I(V, lambda e: e.scalar_tensor_tensor(
            nmi[:], mrow[:], -1.0, inv[:], Alu.mult, Alu.mult), s.sDV)
        dRb = s.sem("dRbLN")
        s.W("scalar", s.sDV)
        s.I("scalar", lambda e: e.dma_start(s.d_rb[0:1, 0:3 + T], inv[:]), dRb, 16)
        s.I("scalar", lambda e: e.dma_start(s.d_rb[1:2, 0:3 + T], nmi[:]), dRb, 16)
        s.W("sync", dRb)
        s.I("sync", lambda e: e.dma_start(invb[:], s.d_rb[0:1, 3:3 + T].broadcast_to([128, T])), dRb, 16)
        s.I("sync", lambda e: e.dma_start(nmb[:], s.d_rb[1:2, 3:3 + T].broadcast_to([128, T])), dRb, 16)
        s.I("sync", lambda e: e.dma_start(invbH[:], s.d_rb[0:1, 0:3].broadcast_to([128, 3])), dRb, 16)
        s.I("sync", lambda e: e.dma_start(nmbH[:], s.d_rb[1:2, 0:3].broadcast_to([128, 3])), dRb, 16)
        s.W(V, dRb)
        for k in range(NKT):
            s.I(V, lambda e, k=k: e.tensor_mul(xbuf[:, k * T:(k + 1) * T], xbuf[:, k * T:(k + 1) * T], invb[:]))
            s.I(V, lambda e, k=k: e.tensor_mul(xh[:, 3 * k:3 * k + 3], xh[:, 3 * k:3 * k + 3], invbH[:]))
        s.D(V)
        for k in range(NKT):
            s.I(V, lambda e, k=k: e.tensor_add(s.xn[:, k * T:(k + 1) * T], xbuf[:, k * T:(k + 1) * T], nmb[:]))
            s.I(V, lambda e, k=k: e.tensor_add(s.xnh[:, 3 * k:3 * k + 3], xh[:, 3 * k:3 * k + 3], nmbH[:]),
                s.sDV if k == NKT - 1 else None)
        self.sem_xn = s.sDV.n
        s.dbg("xn", lambda: s.xn[:, 0:self.T], (128, self.T), BF16)
        dXst = s.sem("dXst")
        s.W("scalar", s.sDV, self.sem_xn)
        for k in range(NKT):
            s.I("scalar", lambda e, k=k: e.dma_start(
                s.d_xnsh[128 * k:128 * (k + 1), :],
                s.xn[:, k * self.T:(k + 1) * self.T]), dXst, 16)
        self.ln_dXst = dXst

    def ph_dequant(self):
        s = self
        for eng in ["sync", "vector", "gpsimd", "scalar"]:
            s.W(eng, s.sDV, self.sem_xn)
        Q = 4096
        cbuf = [s.sb(f"dq_c{i}", (128, Q), U8) for i in range(2)]
        iu = [s.sb(f"dq_iu{i}", (128, Q // 4), mybir.dt.int32) for i in range(2)]
        iu1 = [s.sb(f"dq_iv{i}", (128, Q // 4), mybir.dt.int32) for i in range(2)]
        ipn = [s.sb(f"dq_pn{i}", (128, Q // 4), mybir.dt.int32) for i in range(2)]
        pn1 = [s.sb(f"dq_pm{i}", (128, Q // 4), mybir.dt.int32) for i in range(2)]
        sgn = [s.sb(f"dq_sn{i}", (128, Q // 4), mybir.dt.int32) for i in range(2)]
        crl = [s.sb(f"dq_cr{i}", (128, Q // 2), BF16) for i in range(2)]
        mbf = [s.sb(f"dq_mb{i}", (128, Q // 2), BF16) for i in range(2)]
        sct = [s.sb(f"dq_sc{i}", (128, Q), F32) for i in range(2)]
        wq = [s.sb(f"dq_w{i}", (128, 2 * Q), BF16) for i in range(2)]
        items = []
        for m, (csrc, ssrc, dsh) in enumerate([
                (s.d_cg, s.d_sg, s.d_wsh_g), (s.d_cu, s.d_su, s.d_wsh_u)]):
            for tbl in range(4):
                items.append(dict(
                    codes=lambda tbl=tbl, c=csrc: c[128 * tbl:128 * (tbl + 1), :],
                    srow=lambda jl, tbl=tbl, sc=ssrc: sc[8 * tbl + jl:8 * tbl + jl + 1, :],
                    dsh=dsh, re=256 * tbl, ro=256 * tbl + 128, c0=0, q=4096))
        for ti in range(8):
            items.append(dict(codes=lambda ti=ti: s.d_cd[128 * ti:128 * (ti + 1), :],
                              srow=lambda jl, ti=ti: s.d_sd[8 * ti + jl:8 * ti + jl + 1, :],
                              dsh=s.d_wsh_d, re=256 * ti, ro=256 * ti + 128, c0=0, q=2048))
        V, G = "vector", "gpsimd"
        I32 = mybir.dt.int32
        SGNMASK = 0x80008000 - (1 << 32)
        gp_done = [0, 0]
        g1n_prev = 0
        v4_prev = 0
        dCb = [s.sem("dCb0"), s.sem("dCb1")]
        dWq = [s.sem("dWq0"), s.sem("dWq1")]
        s.dq_dWq = dWq
        for n, it in enumerate(items):
            r = n % 2
            q = it["q"]
            s.W("sync", s.sGP, gp_done[r])
            s.I("sync", lambda e, it=it, r=r, q=q: e.dma_start(
                cbuf[r][:, 0:q], it["codes"]()), dCb[r], 16)
            for jl in range(8):
                s.I("sync", lambda e, it=it, jl=jl, r=r, q=q: e.dma_start(
                    sct[r][16 * jl:16 * (jl + 1), 0:q],
                    it["srow"](jl).broadcast_to([16, q])), dCb[r], 16)
            ld = dCb[r].n
            s.W(V, dCb[r], ld)
            s.W(V, s.sGP, gp_done[r])
            s.W(G, dCb[r], ld)
            for pl, (sh, pd, dlt) in enumerate([(0, 0, 0), (4, 1, 0), (8, 0, 1), (12, 1, 1)]):
                s.I(V, lambda e, r=r, sh=sh, q=q: e.tensor_scalar(
                    iu[r][:, 0:q // 4], cbuf[r][:, 0:q].bitcast(I32), sh, 0x00000007,
                    Alu.logical_shift_right, Alu.bitwise_and))
                s.I(V, lambda e, r=r, sh=sh, q=q: e.tensor_scalar(
                    iu1[r][:, 0:q // 4], cbuf[r][:, 0:q].bitcast(I32), sh, 0x00070000,
                    Alu.logical_shift_right, Alu.bitwise_and))
                s.D(V)
                s.I(V, lambda e, r=r, q=q: e.tensor_scalar(
                    ipn[r][:, 0:q // 4], iu[r][:, 0:q // 4], 64, 0x00003F00, Alu.mult, Alu.add))
                s.I(V, lambda e, r=r, q=q: e.tensor_scalar(
                    pn1[r][:, 0:q // 4], iu1[r][:, 0:q // 4], 64, 0x3F000000, Alu.mult, Alu.add))
                s.D(V)
                s.I(V, lambda e, r=r, q=q: e.tensor_tensor(
                    ipn[r][:, 0:q // 4], ipn[r][:, 0:q // 4], pn1[r][:, 0:q // 4], Alu.bitwise_or))
                s.I(V, lambda e, r=r, sh=sh, q=q: e.tensor_scalar(
                    sgn[r][:, 0:q // 4], cbuf[r][:, 0:q].bitcast(I32), 12 - sh, SGNMASK,
                    Alu.logical_shift_left, Alu.bitwise_and), s.sDV)
                dvv = s.sDV.n
                s.D(V)
                s.W("scalar", s.sGP, g1n_prev)
                s.W("scalar", s.sDV, dvv)
                s.I("scalar", lambda e, r=r, q=q: e.activation(
                    crl[r][:, 0:q // 2], ipn[r][:, 0:q // 4].bitcast(BF16), Act.Relu,
                    bias=1.0, scale=-1.0), s.sAC)
                acn = s.sAC.n
                s.W(G, s.sDV, v4_prev)
                s.W(G, s.sAC, acn)
                s.I(G, lambda e, r=r, q=q: e.tensor_tensor(
                    mbf[r][:, 0:q // 2], ipn[r][:, 0:q // 4].bitcast(BF16),
                    crl[r][:, 0:q // 2], Alu.subtract), s.sGP)
                g1n_prev = s.sGP.n
                s.W(V, s.sGP, g1n_prev)
                s.I(V, lambda e, r=r, q=q: e.tensor_tensor(
                    mbf[r][:, 0:q // 2].bitcast(I32), mbf[r][:, 0:q // 2].bitcast(I32),
                    sgn[r][:, 0:q // 4], Alu.bitwise_or), s.sDV)
                v4_prev = s.sDV.n
                s.D(V)
                if pl == 0:
                    s.W(G, dWq[r])
                s.W(G, s.sDV, v4_prev)
                s.I(G, lambda e, r=r, pd=pd, dlt=dlt, q=q: e.tensor_tensor(
                    wq[r][:, pd * q + dlt:pd * q + q:2], mbf[r][:, 0:q // 2],
                    sct[r][:, dlt:q:2], Alu.mult), s.sGP)
                s.D(G)
            gp_done[r] = s.sGP.n
            s.W("scalar", s.sGP, gp_done[r])
            s.I("scalar", lambda e, it=it, r=r, q=q: e.dma_start(
                it["dsh"][it["re"]:it["re"] + 128, it["c0"]:it["c0"] + q], wq[r][:, 0:q]), dWq[r], 16)
            s.I("scalar", lambda e, it=it, r=r, q=q: e.dma_start(
                it["dsh"][it["ro"]:it["ro"] + 128, it["c0"]:it["c0"] + q], wq[r][:, q:2 * q]), dWq[r], 16)

    def ph_inproj(self):
        s = self
        for eng in ["sync", "scalar", "vector", "tensor"]:
            s.W(eng, s.sGP)
            s.W(eng, s.dq_dWq[0])
            s.W(eng, s.dq_dWq[1])
        s.W("gpsimd", s.dq_dWq[0])
        s.W("gpsimd", s.dq_dWq[1])
        s.W("gpsimd", s.sDV)
        s.W("gpsimd", s.sAC)
        T, TS, NTS = self.T, self.TSEG, self.NTS
        GRP = 3
        wgrp = [s.sb(f"ip_wb{i}", (128, NKT * GRP * 128), BF16) for i in range(2)]
        hbcb = [s.sb(f"ip_hbc{i}", (128, 3 + T), BF16) for i in range(6)]
        cacc = [s.sb(f"ip_cacc{i}", (128, T), F32) for i in range(2)]
        gst = [s.sb(f"ip_gst{i}", (128, TS), BF16) for i in range(2)]
        dtraw = s.sb("dtraw", (64, 3 + T), F32)
        dtA = s.sb("dtA", (64, T), F32)
        groups = [list(range(g0, min(g0 + GRP, OC))) for g0 in range(0, OC, GRP)]
        V, A = "vector", "scalar"
        hbc_slot = {}
        slot_free = [0] * 6
        slot_next = [0]
        pend_conv = []
        dWgr = [s.sem("dWgr0"), s.sem("dWgr1")]
        s.dGst = [s.sem("dGst0"), s.sem("dGst1")]

        def emit_conv(oc, slot):
            th = oc - 32
            r2 = th % 2
            src = hbcb[slot]
            s.W(V, s.sAC)
            s.I(V, lambda e, src=src, th=th, r2=r2: e.tensor_scalar(
                cacc[r2][:], src[:, 0:T], s.convw[:, th * 4:th * 4 + 1], None, Alu.mult))
            s.D(V)
            for rr in range(1, 4):
                s.I(V, lambda e, src=src, th=th, r2=r2, rr=rr: e.scalar_tensor_tensor(
                    cacc[r2][:], src[:, rr:rr + T], s.convw[:, th * 4 + rr:th * 4 + rr + 1],
                    cacc[r2][:], Alu.mult, Alu.add), s.sDV if rr == 3 else None)
                s.D(V)
            cv = s.sDV.n
            if th < 32:
                dst = lambda th=th: s.hsy[:, th * T:(th + 1) * T]
            elif th == 32:
                dst = lambda: s.Bt_[:, 0:T]
            else:
                dst = lambda: s.Ct_[:, 0:T]
            s.W(A, s.sDV, cv)
            s.I(A, lambda e, dst=dst, th=th, r2=r2: e.activation(
                dst(), cacc[r2][:], Act.Silu, bias=s.convb[:, th:th + 1]), s.sAC)
            slot_free[slot] = s.sDV.n
            return cv

        with s.nc.psum_tensor("ipp", [128, GRP * TS], F32) as pp, \
             s.nc.psum_tensor("iph", [128, 512], F32) as ph:
            mm_grp = [0, 0]
            for gi, grp in enumerate(groups):
                o0 = grp[0]
                gw = sum(128 if o < 66 else 64 for o in grp)
                wb = gi % 2
                s.W("gpsimd", s.sPE, mm_grp[wb])
                for k in range(NKT):
                    s.I("gpsimd", lambda e, k=k, o0=o0, gw=gw, wb=wb: e.dma_start(
                        wgrp[wb][:, k * GRP * 128:k * GRP * 128 + gw],
                        s.d_win[128 * k:128 * (k + 1), 128 * o0:128 * o0 + gw]), dWgr[wb], 16)
                s.W("tensor", dWgr[wb])
                s.W("tensor", s.sDV, self.sem_xn)
                for th in range(NTS):
                    for oi, o in enumerate(grp):
                        ow = 128 if o < 66 else 64
                        last = (oi == len(grp) - 1)
                        for k in range(NKT):
                            s.I("tensor", lambda e, k=k, oi=oi, o=o, ow=ow, th=th, wb=wb: e.matmul(
                                pp[0:ow, oi * TS:(oi + 1) * TS],
                                wgrp[wb][:, k * GRP * 128 + oi * 128:k * GRP * 128 + oi * 128 + ow],
                                s.xn[:, k * T + th * TS:k * T + (th + 1) * TS],
                                start=(k == 0), stop=(k == NKT - 1)),
                                s.sPE if (last and k == NKT - 1) else None)
                    mmv = s.sPE.n
                    if th == 0:
                        halo_os = [oi for oi, o in enumerate(grp) if o >= 32]
                        for oi in halo_os:
                            o = grp[oi]
                            ow = 128 if o < 66 else 64
                            last = (oi == halo_os[-1])
                            for k in range(NKT):
                                s.I("tensor", lambda e, k=k, oi=oi, ow=ow, wb=wb: e.matmul(
                                    ph[0:ow, oi * 4:oi * 4 + 3],
                                    wgrp[wb][:, k * GRP * 128 + oi * 128:k * GRP * 128 + oi * 128 + ow],
                                    s.xnh[:, 3 * k:3 * k + 3],
                                    start=(k == 0), stop=(k == NKT - 1)),
                                    s.sPE if (last and k == NKT - 1) else None)
                    hmv = s.sPE.n
                    for oi, o in enumerate(grp):
                        if o < 32:
                            r = oi % 2
                            s.W(A, s.sPE, mmv)
                            s.W(A, s.dGst[r])
                            s.I(A, lambda e, oi=oi, r=r, th=th: e.activation(
                                gst[r][:], pp[:, oi * TS:(oi + 1) * TS], Act.Copy), s.sAC)
                            s.W("scalar", s.sAC)
                            s.I("scalar", lambda e, o=o, th=th, r=r: e.dma_start(
                                s.d_gsp[128 * o:128 * (o + 1), th * TS:(th + 1) * TS], gst[r][:]), s.dGst[r], 16)
                        elif o < 66:
                            if th == 0:
                                slot = slot_next[0] % 6
                                slot_next[0] += 1
                                hbc_slot[o] = slot
                                s.W(A, s.sDV, slot_free[slot])
                            slot = hbc_slot[o]
                            s.W(A, s.sPE, mmv)
                            s.I(A, lambda e, oi=oi, slot=slot, th=th: e.activation(
                                hbcb[slot][:, 3 + th * TS:3 + (th + 1) * TS],
                                pp[:, oi * TS:(oi + 1) * TS], Act.Copy), s.sAC)
                            if th == 0:
                                s.W(A, s.sPE, hmv)
                                s.I(A, lambda e, oi=oi, slot=slot: e.activation(
                                    hbcb[slot][:, 0:3], ph[:, oi * 4:oi * 4 + 3], Act.Copy), s.sAC)
                            if th == NTS - 1:
                                pend_conv.append((o, slot))
                        else:
                            s.W(V, s.sPE, mmv)
                            if th == 0:
                                s.I(V, lambda e, oi=oi, th=th: e.tensor_copy(
                                    dtraw[:, 3 + th * TS:3 + (th + 1) * TS], pp[0:64, oi * TS:(oi + 1) * TS]))
                                s.W(V, s.sPE, hmv)
                                s.I(V, lambda e, oi=oi: e.tensor_copy(
                                    dtraw[:, 0:3], ph[0:64, oi * 4:oi * 4 + 3]), s.sDV)
                            else:
                                s.I(V, lambda e, oi=oi, th=th: e.tensor_copy(
                                    dtraw[:, 3 + th * TS:3 + (th + 1) * TS], pp[0:64, oi * TS:(oi + 1) * TS]),
                                    s.sDV)
                    s.W("tensor", s.sAC)
                    s.W("tensor", s.sDV)
                mm_grp[wb] = s.sPE.n
                for (oc, slot) in pend_conv:
                    emit_conv(oc, slot)
                pend_conv = []
        s.dbg("Br", lambda: s.Bt_[:, 0:T], (128, T), BF16)
        s.dbg("Cr", lambda: s.Ct_[:, 0:T], (128, T), BF16)
        s.dbg("hs0", lambda: s.hsy[:, 0:T], (128, T), BF16)
        s.dbg("dtraw", lambda: dtraw[:], (64, 3 + T), F32)
        V, A = "vector", "scalar"
        ab = s.sb("dt_ab", (64, T), F32)
        s.D(V)
        s.I(V, lambda e: e.tensor_scalar(
            dtA[:], dtraw[:, 3:3 + T], s.dtb[:], 0.0, Alu.add, Alu.max), s.sDV)
        s.W(A, s.sDV)
        s.I(A, lambda e: e.activation(ab[:], dtraw[:, 3:3 + T], Act.Abs, bias=s.dtb[:]))
        s.D(A)
        s.I(A, lambda e: e.activation(ab[:], ab[:], Act.Exp, scale=-1.0))
        s.D(A)
        s.I(A, lambda e: e.activation(ab[:], ab[:], Act.Ln, bias=1.0), s.sAC)
        s.W(V, s.sAC)
        s.I(V, lambda e: e.tensor_add(s.scanstack[0:64, :], dtA[:], ab[:]))
        s.D(V)
        s.I(V, lambda e: e.tensor_scalar_mul(dtA[:], s.scanstack[0:64, :], s.nA[:]))
        s.D(V)
        s.I(V, lambda e: e.tensor_tensor_scan(
            s.aG[:], s.onesT[:], dtA[:], 0.0, Alu.mult, Alu.add))
        s.D(V)
        for c in range(self.NCH):
            cs = slice(c * LCH, (c + 1) * LCH)
            off = (lambda: s.zc64[:]) if c == 0 else (lambda c=c: s.aG[:, c * LCH - 1:c * LCH])
            s.I(V, lambda e, cs=cs, off=off: e.tensor_scalar(
                s.scanstack[64:128, cs], s.aG[:, cs], off(), None, Alu.subtract),
                s.sDV if c == self.NCH - 1 else None)
        self.sem_scan = s.sDV.n
        s.daL = s.sem("daL")
        s.W("scalar", s.sDV, self.sem_scan)
        s.I("scalar", lambda e: e.dma_start(s.d_aL[:], s.scanstack[64:128, :]), s.daL, 16)
        s.dbg("aG", lambda: s.aG[:], (64, T), F32)
        s.dbg("scanstack", lambda: s.scanstack[:], (128, T), F32)

    def ph_mixer(self):
        s = self
        for eng in ["vector", "scalar", "gpsimd", "tensor"]:
            s.W(eng, s.dGst[0])
            s.W(eng, s.dGst[1])
            s.W(eng, s.sAC)
        T = self.T
        V, A, G, TE = "vector", "scalar", "gpsimd", "tensor"
        scanT = s.sb("scanT", (128, 128), F32)
        lr = s.sb("lr", (128, 64), F32)
        lamE = s.sb("lamE", (128, 64), F32)
        wE = s.sb("wEx", (128, 64), F32)
        gamT = s.sb("gamT", (128, 64), F32)
        Btr = s.sb("Btr", (128, 128), BF16)
        cbm = s.sb("cbm", (128, 128), F32)
        xTc = s.sb("xTc", (128, NGT * 128), BF16)
        xTd = s.sb("xTd", (128, NGT * 128), BF16)
        bcA = [s.sb(f"bcA{i}", (128, 1024), F32) for i in range(2)]
        eMA = [s.sb(f"eMA{i}", (128, 1024), BF16) for i in range(2)]
        mhA = [s.sb(f"mhA{i}", (128, 1024), BF16) for i in range(2)]
        bwA = [s.sb(f"bwA{i}", (128, 1024), BF16) for i in range(2)]
        tsb = [s.sb(f"tsb{i}", (128, 512), BF16) for i in range(2)]
        t1s = s.sb("t1s", (128, 512), F32)
        dLr = s.sem("dLr")
        dBc = [s.sem("dBc0"), s.sem("dBc1")]
        with s.nc.psum_tensor("mxA0", [128, 512], F32) as pA0, \
             s.nc.psum_tensor("mxA1", [128, 512], F32) as pA1, \
             s.nc.psum_tensor("mxB0", [128, 512], F32) as pB0, \
             s.nc.psum_tensor("mxB1", [128, 512], F32) as pB1, \
             s.nc.psum_tensor("mxC0", [128, 512], F32) as pC0, \
             s.nc.psum_tensor("mxC1", [128, 512], F32) as pC1, \
             s.nc.psum_tensor("mxT0", [128, 1024], BF16) as pT0, \
             s.nc.psum_tensor("mxT1", [128, 1024], BF16) as pT1:
            pA, pB, pC, pT = [pA0, pA1], [pB0, pB1], [pC0, pC1], [pT0, pT1]
            evA = [0, 0]; evB = [0, 0]; evC = [0, 0]; evT = [0, 0]
            mmA = [0, 0]; mmB = [0, 0]; mmC = [0, 0]; mmT = [0, 0]
            emv_h = [0, 0]; mhv_h = [0, 0]
            pre_dv = [0]
            supd = {}
            s.W(TE, s.sDV, self.sem_scan)
            s.W(TE, s.sGP)
            for c in range(self.NCH):
                cs = slice(c * LCH, (c + 1) * LCH)
                pe_c0 = s.sPE.n
                dv_c0 = s.sDV.n
                ac_c0 = s.sAC.n
                s.W(TE, s.sDV, evB[0])
                s.I(TE, lambda e, cs=cs: e.matmul(
                    pB0[:, 0:128], s.scanstack[:, cs], s.idnF[:],
                    start=True, stop=True, is_transpose=True), s.sPE)
                mmB[0] = s.sPE.n
                s.W(V, s.sPE, mmB[0])
                s.I(V, lambda e: e.tensor_copy(scanT[:], pB0[:, 0:128]), s.sDV)
                s.D(V)
                evB[0] = s.sDV.n
                scT = s.sDV.n
                s.W(TE, s.sDV, evB[1])
                s.I(TE, lambda e, cs=cs: e.matmul(
                    pB1[:, 0:128], s.Bt_[:, cs], s.Ct_[:, cs], start=True, stop=True), s.sPE)
                mmB[1] = s.sPE.n
                s.W(V, s.sPE, mmB[1])
                s.W(V, s.sGP, mhv_h[1] if c > 0 else 0)
                s.W(V, s.sGP, mhv_h[0] if c > 0 else 0)
                s.I(V, lambda e: e.tensor_mul(cbm[:], pB1[:, 0:128], s.trimask[:]), s.sDV)
                evB[1] = s.sDV.n
                s.W(TE, s.sDV, evT[0])
                s.I(TE, lambda e, cs=cs: e.matmul(
                    pT0[:, 0:128], s.Bt_[:, cs], s.idnB[:],
                    start=True, stop=True, is_transpose=True), s.sPE)
                mmT[0] = s.sPE.n
                s.W(V, s.sPE, mmT[0])
                s.I(V, lambda e: e.tensor_copy(Btr[:], pT0[:, 0:128]), s.sDV)
                evT[0] = s.sDV.n
                s.W("sync", s.daL)
                s.W("sync", s.sAC, ac_c0)
                s.W("sync", s.sDV, dv_c0)
                ce = (c + 1) * LCH - 1
                s.I("sync", lambda e, ce=ce: e.dma_start(
                    lr[:], s.d_aL[:, ce:ce + 1].rearrange("h 1 -> 1 h").broadcast_to([128, 64]),
                    allow_slow_non_contiguous=True), dLr, 16)
                lrv = dLr.n
                s.W(A, s.sDV, dv_c0)
                s.W(A, dLr, lrv)
                s.I(A, lambda e: e.activation(lamE[:], lr[:], Act.Exp), s.sAC)
                s.W(V, dLr, lrv)
                s.I(V, lambda e: e.tensor_sub(wE[:], lr[:], scanT[:, 64:128]), s.sDV)
                wev = s.sDV.n
                s.W(A, s.sDV, wev)
                s.I(A, lambda e: e.activation(wE[:], wE[:], Act.Exp))
                s.D(A)
                s.I(A, lambda e: e.activation(gamT[:], scanT[:, 64:128], Act.Exp), s.sAC)
                pre_ac = s.sAC.n
                for b8 in range(4):
                    pt = b8 % 2
                    s.W(TE, s.sDV, evT[pt])
                    for i in range(8):
                        tau = b8 * 8 + i
                        s.I(TE, lambda e, tau=tau, i=i, pt=pt, c=c: e.matmul(
                            pT[pt][:, i * 128:(i + 1) * 128], s.hsy[:, tau * T + c * LCH:tau * T + (c + 1) * LCH],
                            s.idnB[:], start=True, stop=True, is_transpose=True),
                            s.sPE if i == 7 else None)
                    mmT[pt] = s.sPE.n
                    s.W(V, s.sPE, mmT[pt])
                    s.W(V, s.sPE, pe_c0)
                    s.I(V, lambda e, b8=b8, pt=pt: e.tensor_copy(
                        xTc[:, b8 * 1024:(b8 + 1) * 1024], pT[pt][:]), s.sDV)
                    evT[pt] = s.sDV.n
                s.D(V)
                s.I(V, lambda e: e.tensor_tensor(
                    xTd[:].rearrange("p (h d) -> p h d", d=HD), xTc[:].rearrange("p (h d) -> p h d", d=HD),
                    scanT[:, 0:64].unsqueeze(2).broadcast_to([128, 64, HD]), Alu.mult), s.sDV)
                s.D(V)
                s.I(V, lambda e, c=c: e.tensor_tensor(
                    s.hsy[:].rearrange("p (t x) -> p t x", x=T)[:, :, c * LCH:(c + 1) * LCH],
                    s.hsy[:].rearrange("p (t x) -> p t x", x=T)[:, :, c * LCH:(c + 1) * LCH],
                    s.dcol[:, 0:NGT].unsqueeze(2).broadcast_to([128, NGT, LCH]),
                    Alu.mult), s.sDV)
                s.D(V)
                pre_dv[0] = s.sDV.n
                for g in range(8):
                    r = g % 2
                    cg = c * 8 + g
                    s.W("sync", s.sDV, pre_dv[0])
                    s.W("sync", s.sAC, emv_h[r])
                    for i in range(8):
                        h = 8 * g + i
                        s.I("sync", lambda e, h=h, i=i, r=r, c=c: e.dma_start(
                            bcA[r][:, i * 128:(i + 1) * 128],
                            s.d_aL[h:h + 1, c * LCH:(c + 1) * LCH].broadcast_to([128, LCH])),
                            dBc[r], 16)
                    gbv = dBc[r].n
                    s.W(V, dBc[r], gbv)
                    s.I(V, lambda e, r=r, g=g: e.tensor_tensor(
                        bcA[r][:].rearrange("p (h t) -> p h t", t=128),
                        bcA[r][:].rearrange("p (h t) -> p h t", t=128),
                        scanT[:, 64 + 8 * g:64 + 8 * g + 8].unsqueeze(2).broadcast_to([128, 8, 128]),
                        Alu.subtract))
                    s.D(V)
                    s.I(V, lambda e, r=r: e.tensor_scalar_min(bcA[r][:], bcA[r][:], 0.0), s.sDV)
                    dcv = s.sDV.n
                    s.W(A, s.sDV, dcv)
                    s.W(A, s.sGP, mhv_h[r])
                    s.I(A, lambda e, r=r: e.activation(eMA[r][:], bcA[r][:], Act.Exp), s.sAC)
                    emv = s.sAC.n
                    emv_h[r] = emv
                    s.W(G, s.sAC, emv)
                    s.W(G, s.sPE, mmA[r])
                    s.I(G, lambda e, r=r: e.tensor_tensor(
                        mhA[r][:].rearrange("p (h t) -> p h t", t=128),
                        eMA[r][:].rearrange("p (h t) -> p h t", t=128),
                        cbm[:].unsqueeze(1).broadcast_to([128, 8, 128]), Alu.mult), s.sGP)
                    mhv = s.sGP.n
                    mhv_h[r] = mhv
                    s.W(V, s.sAC, pre_ac)
                    s.W(V, s.sPE, mmC[r])
                    s.I(V, lambda e, r=r, g=g: e.tensor_tensor(
                        bwA[r][:].rearrange("p (h n) -> p h n", n=128),
                        Btr[:].unsqueeze(1).broadcast_to([128, 8, 128]),
                        wE[:, 8 * g:8 * g + 8].unsqueeze(2).broadcast_to([128, 8, 128]),
                        Alu.mult), s.sDV)
                    bwv = s.sDV.n
                    s.W(TE, s.sDV, evB[r])
                    if c > 0:
                        s.W(TE, s.sDV, supd[(c - 1, g)])
                    s.I(TE, lambda e, r=r, g=g, cs=cs: e.matmul(
                        pB[r][:], s.Ct_[:, cs], s.Sst16[:, g * 512:(g + 1) * 512],
                        start=True, stop=True), s.sPE)
                    mmB[r] = s.sPE.n
                    s.W(V, s.sPE, mmB[r])
                    s.W(V, s.sPE, mmA[r])
                    s.I(V, lambda e, r=r, g=g: e.tensor_tensor(
                        tsb[r][:].rearrange("p (h d) -> p h d", d=HD),
                        pB[r][:].rearrange("p (h d) -> p h d", d=HD),
                        gamT[:, 8 * g:8 * g + 8].unsqueeze(2).broadcast_to([128, 8, HD]),
                        Alu.mult), s.sDV)
                    evB[r] = s.sDV.n
                    s.W(TE, s.sGP, mhv)
                    s.W(TE, s.sDV, evA[r])
                    for i in range(8):
                        h = 8 * g + i
                        s.I(TE, lambda e, r=r, h=h, i=i: e.matmul(
                            pA[r][64 * (i % 2):64 * (i % 2) + 64, (i // 2) * 128:(i // 2) * 128 + 128],
                            xTd[:, h * HD:(h + 1) * HD], mhA[r][:, i * 128:(i + 1) * 128],
                            start=True, stop=True),
                            s.sPE if i == 7 else None)
                    mmA[r] = s.sPE.n
                    pt2 = g % 2
                    s.W(TE, s.sDV, evB[r])
                    s.W(TE, s.sDV, evT[pt2])
                    for j in range(4):
                        s.I(TE, lambda e, pt2=pt2, r=r, j=j: e.matmul(
                            pT[pt2][:, j * 128:(j + 1) * 128],
                            tsb[r][:, j * 128:(j + 1) * 128], s.idnB[:],
                            start=True, stop=True, is_transpose=True),
                            s.sPE if j == 3 else None)
                    mmT[pt2] = s.sPE.n
                    s.W(V, s.sPE, mmA[r])
                    s.W(V, s.sPE, mmT[pt2])
                    s.I(V, lambda e, r=r, g=g, c=c: e.tensor_tensor(
                        s.hsy[:].rearrange("p (t x) -> p t x", x=T)[:, 4 * g:4 * g + 4, c * LCH:(c + 1) * LCH],
                        s.hsy[:].rearrange("p (t x) -> p t x", x=T)[:, 4 * g:4 * g + 4, c * LCH:(c + 1) * LCH],
                        pA[r][:, 0:512].rearrange("p (j x) -> p j x", x=128), Alu.add))
                    s.D(V)
                    s.I(V, lambda e, pt2=pt2, g=g, c=c: e.tensor_tensor(
                        s.hsy[:].rearrange("p (t x) -> p t x", x=T)[:, 4 * g:4 * g + 4, c * LCH:(c + 1) * LCH],
                        s.hsy[:].rearrange("p (t x) -> p t x", x=T)[:, 4 * g:4 * g + 4, c * LCH:(c + 1) * LCH],
                        pT[pt2][:, 0:512].rearrange("p (j x) -> p j x", x=128), Alu.add), s.sDV)
                    evA[r] = s.sDV.n
                    evT[pt2] = s.sDV.n
                    s.W(TE, s.sDV, bwv)
                    s.W(TE, s.sDV, evC[r])
                    for i in range(8):
                        h = 8 * g + i
                        s.I(TE, lambda e, r=r, h=h, i=i: e.matmul(
                            pC[r][:, i * HD:(i + 1) * HD],
                            bwA[r][:, i * 128:(i + 1) * 128], xTd[:, h * HD:(h + 1) * HD],
                            start=(i == 0), stop=(i == 7)), s.sPE if i == 7 else None)
                    mmC[r] = s.sPE.n
                    s.W(V, s.sPE, mmC[r])
                    s.I(V, lambda e, g=g: e.tensor_tensor(
                        t1s[:].rearrange("p (h d) -> p h d", d=HD),
                        s.Sst[:, g * 512:(g + 1) * 512].rearrange("p (h d) -> p h d", d=HD),
                        lamE[:, 8 * g:8 * g + 8].unsqueeze(2).broadcast_to([128, 8, HD]),
                        Alu.mult))
                    s.D(V)
                    s.I(V, lambda e, r=r, g=g: e.tensor_add(
                        s.Sst[:, g * 512:(g + 1) * 512], t1s[:], pC[r][:]), s.sDV)
                    evC[r] = s.sDV.n
                    s.D(V)
                    s.I(V, lambda e, g=g: e.tensor_copy(
                        s.Sst16[:, g * 512:(g + 1) * 512], s.Sst[:, g * 512:(g + 1) * 512]), s.sDV)
                    supd[(c, g)] = s.sDV.n
                    if c == 0 and g == 0:
                        s.dbgsnap("mh00", lambda: mhA[0][:, 0:128], (128, 128), BF16)
                        s.dbgsnap("xtd0", lambda: xTd[:, 0:128], (128, 128), BF16)
                        s.dbgsnap("tsb0", lambda: tsb[0][:, 0:128], (128, 128), F32)
                        s.dbgsnap("xtc0", lambda: xTc[:, 0:128], (128, 128), BF16)
                        s.dbgsnap("scT0", lambda: scanT[:], (128, 128), F32)
                if c == 0:
                    s.dbg("ych0", lambda: s.hsy[:, 0:LCH], (128, LCH), BF16)
        dStI = s.sem("dStI")
        s.mx_dStI = dStI
        s.W("scalar", s.sDV)
        s.I("scalar", lambda e: e.dma_start(s.d_stin[0:128, :], s.Sst[:]), dStI, 16)
        with s.nc.psum_tensor("afp", [1, 64], F32) as pf:
            s.I(TE, lambda e: e.matmul(
                pf[:], s.aG[:, T - 1:T], s.idnF[0:64, 0:64],
                start=True, stop=True, is_transpose=True), s.sPE)
            afr = s.sb("afrow", (1, NH * HD), F32)
            s.I(V, lambda e: e.memset(afr[:], 0.0))
            s.D(V)
            s.W(V, s.sPE)
            s.I(V, lambda e: e.tensor_copy(afr[:, 0:64], pf[:]), s.sDV)
            s.W("scalar", s.sDV)
            s.I("scalar", lambda e: e.dma_start(s.d_stin[128:129, :], afr[:]), dStI, 16)
        s.W("gpsimd", dStI)
        s.I("gpsimd", lambda e: e.collective_compute(
            "AllGather", Alu.bypass, replica_groups=[list(range(NCORE))],
            ins=[s.d_stin[:]], outs=[s.d_stout[:]]), s.sAG)
        self.sem_ag4 = s.sAG.n
        s.W("gpsimd", s.ln_dXst)
        s.I("gpsimd", lambda e: e.collective_compute(
            "AllGather", Alu.bypass, replica_groups=QUAD_GROUPS,
            ins=[s.d_xnsh[:]], outs=[s.d_xnq[:]]), s.sAG)
        self.sem_agx = s.sAG.n
        s.W("gpsimd", s.dq_dWq[0])
        s.W("gpsimd", s.dq_dWq[1])
        for src_, dst in [(s.d_wsh_g, s.d_wg), (s.d_wsh_u, s.d_wu),
                          (s.d_wsh_d, s.d_wd)]:
            s.I("gpsimd", lambda e, src_=src_, dst=dst: e.collective_compute(
                "AllGather", Alu.bypass, replica_groups=PAIR_GROUPS,
                ins=[src_[:]], outs=[dst[:]]), s.sAG)
        self.sem_ag3 = s.sAG.n
        s.dbg("Sloc", lambda: s.Sst[:], (128, NH * HD), F32)
        s.dbg("ymix", lambda: s.hsy[:, 0:T], (128, T), BF16)

    def ph_correction(self):
        s = self
        for eng in ["vector", "scalar", "gpsimd", "tensor", "sync"]:
            s.W(eng, s.sPE)
            s.W(eng, s.sAC)
            s.W(eng, s.sGP)
            s.W(eng, s.mx_dStI)
        T = self.T
        V, A, G, TE = "vector", "scalar", "gpsimd", "tensor"
        Sin = s.sb("Sinit", (128, NH * HD), BF16)
        Sld = [s.sb(f"Sld{i}", (128, NH * HD), F32) for i in range(2)]
        afr8 = s.sb("afr8", (8, 64), F32)
        msT = s.sb("msT", (8, 8), F32)
        pmk = s.sb("pmk", (8, 1), F32)
        w8 = s.sb("w8", (8, 64), F32)
        wb = s.sb("wb", (128, 64), F32)
        aGT = s.sb("aGT", (128, 64), F32)
        ggT = s.sb("ggT", (128, 64), F32)
        tmc = [s.sb(f"tmc{i}", (128, 512), F32) for i in range(2)]
        s.I(V, lambda e: e.memset(Sin[:], 0.0), s.sDV)
        dAfr = s.sem("dAfr")
        dSld = [s.sem("dSld0"), s.sem("dSld1")]
        s.W("sync", s.sAG, self.sem_ag4)
        for r in range(NCORE):
            s.I("sync", lambda e, r=r: e.dma_start(
                afr8[r:r + 1, :], s.d_stout[129 * r + 128:129 * r + 129, 0:64]), dAfr, 16)
        s.I("sync", lambda e: e.dma_start(msT[:], s.d_msumT[:]), dAfr, 16)
        s.I("sync", lambda e: e.dma_start(pmk[:], s.d_pmask[:]), dAfr, 16)
        ldv = dAfr.n
        with s.nc.psum_tensor("cwp", [8, 64], F32) as pw:
            s.W(TE, dAfr, ldv)
            s.I(TE, lambda e: e.matmul(pw[:], msT[:], afr8[:], start=True, stop=True), s.sPE)
            s.W(A, s.sPE)
            s.I(A, lambda e: e.activation(w8[:], pw[:], Act.Exp), s.sAC)
            s.W(V, s.sAC)
            s.I(V, lambda e: e.tensor_scalar_mul(w8[:], w8[:], pmk[:]), s.sDV)
            wsv = s.sDV.n
        dW8 = s.sem("dW8")
        dWb = s.sem("dWb")
        s.W("scalar", s.sDV, wsv)
        s.I("scalar", lambda e: e.dma_start(s.d_w8[:], w8[:]), dW8, 16)
        for r in range(NCORE):
            rr = r % 2
            s.W("sync", s.sDV)
            s.I("sync", lambda e, r=r, rr=rr: e.dma_start(
                Sld[rr][:], s.d_stout[129 * r:129 * r + 128, :]), dSld[rr], 16)
            ldr = dSld[rr].n
            s.W("sync", dW8)
            s.I("sync", lambda e, r=r: e.dma_start(
                wb[:], s.d_w8[r:r + 1, :].broadcast_to([128, 64])), dWb, 16)
            gv = dWb.n
            s.W(V, dSld[rr], ldr)
            s.W(V, dWb, gv)
            s.I(V, lambda e, rr=rr: e.tensor_tensor(
                Sld[rr][:].rearrange("p (h d) -> p h d", d=HD),
                Sld[rr][:].rearrange("p (h d) -> p h d", d=HD),
                wb[:].unsqueeze(2).broadcast_to([128, 64, HD]), Alu.mult))
            s.D(V)
            s.I(V, lambda e, rr=rr: e.tensor_add(Sin[:], Sin[:], Sld[rr][:]), s.sDV)
            s.D(V)
        sinv = s.sDV.n
        s.dbg("Sinit", lambda: Sin[:], (128, NH * HD), BF16)
        with s.nc.psum_tensor("cpD0", [128, 512], F32) as pD0, \
             s.nc.psum_tensor("cpD1", [128, 512], F32) as pD1, \
             s.nc.psum_tensor("cpE0", [128, 512], F32) as pE0, \
             s.nc.psum_tensor("cpE1", [128, 512], F32) as pE1, \
             s.nc.psum_tensor("cpG", [128, 64], F32) as pG:
            pD, pE = [pD0, pD1], [pE0, pE1]
            mmD = [0, 0]; evD = [0, 0]; mmE = [0, 0]; evE = [0, 0]
            aGTv = 0
            for c in range(self.NCH):
                cs = slice(c * LCH, (c + 1) * LCH)
                s.W(TE, s.sAC, aGTv)
                s.I(TE, lambda e, cs=cs: e.matmul(
                    pG[:], s.aG[:, cs], s.idnF[0:64, 0:64],
                    start=True, stop=True, is_transpose=True), s.sPE)
                mgv = s.sPE.n
                s.W(V, s.sPE, mgv)
                s.I(V, lambda e: e.tensor_copy(aGT[:], pG[:]), s.sDV)
                agv = s.sDV.n
                s.W(A, s.sDV, agv)
                s.I(A, lambda e: e.activation(ggT[:], aGT[:], Act.Exp), s.sAC)
                aGTv = s.sAC.n
                for g in range(8):
                    r = g % 2
                    s.W(TE, s.sDV, evD[r])
                    s.W(TE, s.sDV, sinv)
                    s.I(TE, lambda e, r=r, g=g, cs=cs: e.matmul(
                        pD[r][:], s.Ct_[:, cs], Sin[:, g * 512:(g + 1) * 512],
                        start=True, stop=True), s.sPE)
                    mmD[r] = s.sPE.n
                    s.W(V, s.sPE, mmD[r])
                    s.W(V, s.sPE, mmE[r])
                    s.W(V, s.sAC, aGTv)
                    s.I(V, lambda e, r=r, g=g: e.tensor_tensor(
                        tmc[r][:].rearrange("p (h d) -> p h d", d=HD),
                        pD[r][:].rearrange("p (h d) -> p h d", d=HD),
                        ggT[:, 8 * g:8 * g + 8].unsqueeze(2).broadcast_to([128, 8, HD]),
                        Alu.mult), s.sDV)
                    evD[r] = s.sDV.n
                    s.W(TE, s.sDV, evD[r])
                    s.W(TE, s.sDV, evE[r])
                    for j in range(4):
                        s.I(TE, lambda e, r=r, j=j: e.matmul(
                            pE[r][:, j * 128:(j + 1) * 128],
                            tmc[r][:, j * 128:(j + 1) * 128], s.idnF[:],
                            start=True, stop=True, is_transpose=True),
                            s.sPE if j == 3 else None)
                    mmE[r] = s.sPE.n
                    s.W(V, s.sPE, mmE[r])
                    for j in range(4):
                        tau = 4 * g + j
                        s.I(V, lambda e, r=r, tau=tau, j=j, c=c: e.tensor_add(
                            s.hsy[:, tau * T + c * LCH:tau * T + (c + 1) * LCH],
                            s.hsy[:, tau * T + c * LCH:tau * T + (c + 1) * LCH],
                            pE[r][:, j * 128:(j + 1) * 128]), s.sDV if j == 3 else None)
                    evE[r] = s.sDV.n
        s.dbg("ycorr", lambda: s.hsy[:, 0:T], (128, T), BF16)

    def ph_gating(self):
        s = self
        for eng in ["sync", "scalar", "vector", "tensor", "gpsimd"]:
            s.W(eng, s.sPE)
            s.W(eng, s.sDV)
        T, TS, NTS = self.T, self.TSEG, self.NTS
        V, A, G, TE = "vector", "scalar", "gpsimd", "tensor"
        _gsc = contextlib.ExitStack()
        _sb1 = lambda n, sh, dt_: _gsc.enter_context(s.nc.sbuf_tensor(n, list(sh), dt_))
        gb = _sb1("gateb", (128, NGT * TS), BF16)
        sg = _sb1("sgb", (128, NGT * TS), BF16)
        ygt = [_sb1(f"ygt{i}", (128, TS), F32) for i in range(2)]
        ysq = [_sb1(f"ysq{i}", (128, TS), F32) for i in range(2)]
        rr0 = _sb1("rr0", (1, TS), F32)
        rr1 = _sb1("rr1", (1, TS), F32)
        rr2 = _sb1("rr2", (1, TS), F32)
        rr3 = _sb1("rr3", (1, TS), F32)
        rsqb = _sb1("rsqb", (128, TS), F32)
        s.dRbG = s.sem("dRbG")
        dGb = s.sem("dGb")
        s.W("sync", s.dGst[0])
        s.W("sync", s.dGst[1])
        for th in range(NTS):
            with s.nc.psum_tensor(f"gstat{th}", [1, TS], F32) as pst:
                s.W("sync", s.sDV)
                for tau in range(NGT):
                    s.I("sync", lambda e, tau=tau, th=th: e.dma_start(
                        gb[:, tau * TS:(tau + 1) * TS],
                        s.d_gsp[128 * tau:128 * (tau + 1), th * TS:(th + 1) * TS]), dGb, 16)
                ldv = dGb.n
                for tau in range(NGT):
                    r = tau % 2
                    s.W(A, dGb, ldv)
                    s.W(A, s.sDV)
                    s.I(A, lambda e, tau=tau: e.activation(
                        sg[:, tau * TS:(tau + 1) * TS], gb[:, tau * TS:(tau + 1) * TS], Act.Silu), s.sAC)
                    s.W(V, s.sAC)
                    s.W(V, s.sPE)
                    s.I(V, lambda e, tau=tau, r=r, th=th: e.tensor_mul(
                        ygt[r][:], s.hsy[:, tau * T + th * TS:tau * T + (th + 1) * TS],
                        sg[:, tau * TS:(tau + 1) * TS]), s.sDV)
                    s.W(A, s.sDV)
                    s.I(A, lambda e, r=r: e.activation(ysq[r][:], ygt[r][:], Act.Square), s.sAC)
                    s.W(TE, s.sAC)
                    s.I(TE, lambda e, tau=tau, r=r: e.matmul(
                        pst[:], s.ones1[:], ysq[r][:], start=(tau == 0), stop=(tau == NGT - 1)), s.sPE)
                s.W(V, s.sPE)
                s.I(V, lambda e: e.tensor_copy(rr0[:], pst[:]), s.sDV)
                s.D(V)
            s.I(V, lambda e: e.tensor_scalar(rr1[:], rr0[:], 1.0 / INTER, EPS, Alu.mult, Alu.add), s.sDV)
            s.D(V)
            s.W(A, s.sDV)
            s.I(A, lambda e: e.activation(rr2[:], rr1[:], Act.Sqrt), s.sAC)
            s.W(V, s.sAC)
            s.I(V, lambda e: e.reciprocal(rr3[:], rr2[:]))
            s.D(V)
            s.I(V, lambda e: e.tensor_mul(rr2[:], rr3[:], rr3[:]))
            s.D(V)
            s.I(V, lambda e: e.tensor_mul(rr2[:], rr2[:], rr1[:]))
            s.D(V)
            s.I(V, lambda e: e.tensor_scalar(rr2[:], rr2[:], -0.5, 1.5, Alu.mult, Alu.add))
            s.D(V)
            s.I(V, lambda e: e.tensor_mul(rr3[:], rr3[:], rr2[:]), s.sDV)
            s.W("scalar", s.sDV)
            s.I("scalar", lambda e, th=th: e.dma_start(s.d_rb[2:3, 0:TS], rr3[:]), s.dRbG, 16)
            s.W("sync", s.dRbG)
            s.I("sync", lambda e, th=th: e.dma_start(
                rsqb[:], s.d_rb[2:3, 0:TS].broadcast_to([128, TS])), s.dRbG, 16)
            s.W(V, s.dRbG)
            for tau in range(NGT):
                r = tau % 2
                s.I(V, lambda e, tau=tau, r=r, th=th: e.tensor_mul(
                    ygt[r][:], s.hsy[:, tau * T + th * TS:tau * T + (th + 1) * TS],
                    sg[:, tau * TS:(tau + 1) * TS]))
                s.D(V)
                s.I(V, lambda e, tau=tau, r=r, th=th: e.scalar_tensor_tensor(
                    s.hsy[:, tau * T + th * TS:tau * T + (th + 1) * TS],
                    ygt[r][:], s.nwcol[:, tau:tau + 1], rsqb[:], Alu.mult, Alu.mult),
                    s.sDV if tau == NGT - 1 else None)
        yn_done = s.sDV.n
        s.dbg("yn", lambda: s.hsy[:, 0:T], (128, T), BF16)
        _gsc.close()
        for eng in ["sync", "scalar", "vector", "tensor"]:
            s.W(eng, s.sPE)
            s.W(eng, s.sDV)
            s.W(eng, s.sAC)
        HOG = 2
        wob = [s.sb(f"op_wb{i}", (128, NGT * HOG * 128), BF16) for i in range(2)]
        sst = [s.sb(f"op_s{i}", (128, TS), BF16) for i in range(2)]
        with s.nc.psum_tensor("opp0", [128, 512], F32) as q0, \
             s.nc.psum_tensor("opp1", [128, 512], F32) as q1:
            qq = [q0, q1]
            evq = [0, 0]
            dOw = [s.sem("dOw0"), s.sem("dOw1")]
            mm_hg = [0, 0]
            s.dSst = [s.sem("dSst0"), s.sem("dSst1")]
            s.W("gpsimd", s.sDV, yn_done)
            for hg in range(16 // HOG):
                wb = hg % 2
                s.W("gpsimd", s.sPE, mm_hg[wb])
                for ch in range(NGT):
                    s.I("gpsimd", lambda e, ch=ch, hg=hg, wb=wb: e.dma_start(
                        wob[wb][:, ch * HOG * 128:(ch + 1) * HOG * 128],
                        s.d_wout[128 * ch:128 * (ch + 1), HOG * 128 * hg:HOG * 128 * (hg + 1)]), dOw[wb], 16)
                s.W(TE, dOw[wb])
                s.W(TE, s.sDV, yn_done)
                for th in range(NTS):
                    for oi in range(HOG):
                        r = (th * HOG + oi) % 2
                        s.W(TE, s.sDV, evq[r])
                        for ch in range(NGT):
                            s.I(TE, lambda e, ch=ch, oi=oi, th=th, r=r, wb=wb: e.matmul(
                                qq[r][:, 0:TS], wob[wb][:, ch * HOG * 128 + oi * 128:ch * HOG * 128 + (oi + 1) * 128],
                                s.hsy[:, ch * T + th * TS:ch * T + (th + 1) * TS],
                                start=(ch == 0), stop=(ch == NGT - 1)),
                                s.sPE if ch == NGT - 1 else None)
                        mmq = s.sPE.n
                        s.W(V, s.sPE, mmq)
                        s.W(V, s.dSst[r])
                        s.I(V, lambda e, r=r: e.tensor_copy(sst[r][:], qq[r][:, 0:TS]), s.sDV)
                        evq[r] = s.sDV.n
                        ho = hg * HOG + oi
                        s.W("scalar", s.sDV, evq[r])
                        s.I("scalar", lambda e, ho=ho, th=th, r=r: e.dma_start(
                            s.d_ssp[128 * ho:128 * (ho + 1), th * TS:(th + 1) * TS], sst[r][:]), s.dSst[r], 16)
                mm_hg[wb] = s.sPE.n

    def ph_ffn(self):
        s = self
        for eng in ["sync", "scalar", "vector", "tensor"]:
            s.W(eng, s.sPE)
            s.W(eng, s.dSst[0])
            s.W(eng, s.dSst[1])
            if s.debug:
                s.W(eng, s.sDP)
        T, TS, NTS = self.T, self.TSEG, self.NTS
        V, A, TE = "vector", "scalar", "tensor"
        NB = 512
        NBLK = 4096 // NB
        wgb = s.sb("f_wg", (128, NKT * NB), BF16)
        wub = s.sb("f_wu", (128, NKT * NB), BF16)
        wdb = s.sb("f_wd", (128, (NB // 128) * HID), BF16)
        itb = s.sb("f_it", (128, (NB // 128) * T), BF16)
        glt = [s.sb(f"f_gl{i}", (128, TS), BF16) for i in range(2)]
        dab = s.sb("f_dac", (128, 16 * T), F32)
        xnb = s.sb("f_xnb", (128, NKT * T), BF16)
        dWg = s.sem("dWg")
        dWd = s.sem("dWd")
        dXnb = s.sem("dXnb")
        dYp = s.sem("dYp")
        s.W("sync", s.sAG, self.sem_agx)
        s.W("sync", s.sAG, self.sem_ag3)
        with s.nc.psum_tensor("fp0", [128, 512], F32) as f0, \
             s.nc.psum_tensor("fp1", [128, 512], F32) as f1, \
             s.nc.psum_tensor("fp2", [128, 512], F32) as f2, \
             s.nc.psum_tensor("fp3", [128, 512], F32) as f3, \
             s.nc.psum_tensor("fp4", [128, 512], F32) as f4, \
             s.nc.psum_tensor("fp5", [128, 512], F32) as f5:
            pg = [f0, f2]
            pu = [f1, f3]
            pdn = [f4, f5]
            ev_g = [0, 0]
            ev_d = [0, 0]
            mm_gu_done = 0
            mm_dn_done = 0
            yp_wait = 0
            for bb in range(4):
                s.W("sync", s.sPE, mm_gu_done)
                for k in range(NKT):
                    s.I("sync", lambda e, k=k, bb=bb: e.dma_start(
                        xnb[:, k * T:(k + 1) * T],
                        s.d_xnq[2048 * bb + 128 * k:2048 * bb + 128 * (k + 1), :]), dXnb, 16)
                ld_xn = dXnb.n
                for b in range(NBLK):
                    s.W("sync", s.sPE, mm_gu_done)
                    for k in range(NKT):
                        s.I("sync", lambda e, k=k, b=b: e.dma_start(
                            wgb[:, k * NB:(k + 1) * NB], s.d_wg[128 * k:128 * (k + 1), NB * b:NB * (b + 1)]), dWg, 16)
                        s.I("sync", lambda e, k=k, b=b: e.dma_start(
                            wub[:, k * NB:(k + 1) * NB], s.d_wu[128 * k:128 * (k + 1), NB * b:NB * (b + 1)]), dWg, 16)
                    ld_gu = dWg.n
                    s.W("sync", s.sPE, mm_dn_done)
                    for jc in range(NB // 128):
                        s.I("sync", lambda e, jc=jc, b=b: e.dma_start(
                            wdb[:, jc * HID:(jc + 1) * HID],
                            s.d_wd[NB * b + 128 * jc:NB * b + 128 * (jc + 1), :]), dWd, 16)
                    ld_d = dWd.n
                    s.W(TE, dWg, ld_gu)
                    if b == 0:
                        s.W(TE, dXnb, ld_xn)
                    for j in range(NB // 128):
                        par, w0 = j % 2, 256 * (j // 2)
                        for th in range(NTS):
                            r = (j * NTS + th) % 2
                            s.W(TE, s.sDV, ev_g[r])
                            for k in range(NKT):
                                s.I(TE, lambda e, k=k, j=j, th=th, r=r, par=par, w0=w0: e.matmul(
                                    pg[r][:, 0:TS],
                                    wgb[:, k * NB + w0 + par:k * NB + w0 + par + 255:2],
                                    xnb[:, k * T + th * TS:k * T + (th + 1) * TS],
                                    start=(k == 0), stop=(k == NKT - 1)))
                            for k in range(NKT):
                                s.I(TE, lambda e, k=k, j=j, th=th, r=r, par=par, w0=w0: e.matmul(
                                    pu[r][:, 0:TS],
                                    wub[:, k * NB + w0 + par:k * NB + w0 + par + 255:2],
                                    xnb[:, k * T + th * TS:k * T + (th + 1) * TS],
                                    start=(k == 0), stop=(k == NKT - 1)),
                                    s.sPE if k == NKT - 1 else None)
                            mmv = s.sPE.n
                            s.W(A, s.sPE, mmv)
                            s.I(A, lambda e, r=r: e.activation(glt[r][:], pg[r][:, 0:TS], Act.Gelu_apprx_tanh), s.sAC)
                            s.W(V, s.sAC)
                            s.I(V, lambda e, j=j, th=th, r=r: e.tensor_mul(
                                itb[:, j * T + th * TS:j * T + (th + 1) * TS], glt[r][:], pu[r][:, 0:TS]), s.sDV)
                            ev_g[r] = s.sDV.n
                    mm_gu_done = s.sPE.n
                    it_done = s.sDV.n
                    s.W(TE, dWd, ld_d)
                    s.W(TE, s.sDV, it_done)
                    if b == 0:
                        s.W(V, dYp, yp_wait)
                    for th in range(NTS):
                        for h in range(16):
                            r = (th * 16 + h) % 2
                            s.W(TE, s.sDV, ev_d[r])
                            for jc in range(NB // 128):
                                s.I(TE, lambda e, jc=jc, h=h, th=th, r=r: e.matmul(
                                    pdn[r][:, 0:TS], wdb[:, jc * HID + h * 128:jc * HID + (h + 1) * 128],
                                    itb[:, jc * T + th * TS:jc * T + (th + 1) * TS],
                                    start=(jc == 0), stop=(jc == NB // 128 - 1)),
                                    s.sPE if jc == NB // 128 - 1 else None)
                            mmv = s.sPE.n
                            s.W(V, s.sPE, mmv)
                            if b == 0:
                                s.I(V, lambda e, h=h, th=th, r=r: e.tensor_copy(
                                    dab[:, h * T + th * TS:h * T + (th + 1) * TS], pdn[r][:, 0:TS]), s.sDV)
                            else:
                                s.I(V, lambda e, h=h, th=th, r=r: e.tensor_add(
                                    dab[:, h * T + th * TS:h * T + (th + 1) * TS],
                                    dab[:, h * T + th * TS:h * T + (th + 1) * TS], pdn[r][:, 0:TS]), s.sDV)
                            ev_d[r] = s.sDV.n
                    mm_dn_done = s.sPE.n
                dn_done = s.sDV.n
                s.W("gpsimd", s.sDV, dn_done)
                for h in range(16):
                    s.I("gpsimd", lambda e, h=h, bb=bb: e.dma_start(
                        s.d_yp[2048 * bb + 128 * h:2048 * bb + 128 * (h + 1), :],
                        dab[:, h * T:(h + 1) * T]), dYp, 16)
                yp_wait = dYp.n
        s.W("gpsimd", dYp)
        s.I("gpsimd", lambda e: e.collective_compute(
            "ReduceScatter", Alu.add, replica_groups=QUAD_GROUPS,
            ins=[s.d_yp[:]], outs=[s.d_yrs[:]]), s.sAG)
        self.sem_rs = s.sAG.n

    def ph_combine(self):
        s = self
        T = self.T
        V, A = "vector", "scalar"
        gt = [s.sb(f"cb_g{i}", (128, T), BF16) for i in range(2)]
        sre = [s.sb(f"cb_s{i}", (128, T), BF16) for i in range(2)]
        xre = [s.sb(f"cb_x{i}", (128, T), F32) for i in range(2)]
        yre = [s.sb(f"cb_y{i}", (128, T), BF16) for i in range(2)]
        ost = [s.sb(f"cb_o{i}", (128, T), F32) for i in range(2)]
        out_sem = [s.sem("outD0"), s.sem("outD1")]
        dCmb = [s.sem("dCmb0"), s.sem("dCmb1")]
        s.W("sync", s.dSst[0])
        s.W("sync", s.dSst[1])
        s.W("sync", s.sAG, self.sem_rs)
        for ho in range(16):
            r = ho % 2
            s.W("sync", s.sDV)
            s.I("sync", lambda e, ho=ho, r=r: e.dma_start(
                sre[r][:], s.d_ssp[128 * ho:128 * (ho + 1), :]), dCmb[r], 16)
            s.I("sync", lambda e, ho=ho, r=r: e.dma_start(
                xre[r][:], s.d_x[128 * ho:128 * (ho + 1), 3:3 + T]), dCmb[r], 16)
            s.I("sync", lambda e, ho=ho, r=r: e.dma_start(
                yre[r][:], s.d_yrs[128 * ho:128 * (ho + 1), :]), dCmb[r], 16)
            ldv = dCmb[r].n
            s.W(A, dCmb[r], ldv)
            s.I(A, lambda e, r=r: e.activation(
                gt[r][:], yre[r][:], Act.Silu), s.sAC)
            s.W(V, s.sAC)
            s.W(V, dCmb[r], ldv)
            s.W(V, out_sem[r])
            s.I(V, lambda e, r=r: e.tensor_mul(ost[r][:], sre[r][:], gt[r][:]))
            s.D(V)
            s.I(V, lambda e, r=r: e.tensor_add(ost[r][:], ost[r][:], xre[r][:]), s.sDV)
            s.W("scalar", s.sDV)
            s.I("scalar", lambda e, ho=ho, r=r: e.dma_start(
                s.d_out[128 * ho:128 * (ho + 1), :], ost[r][:]), out_sem[r], 16)
        s.W("scalar", out_sem[0])
        s.W("scalar", out_sem[1])


_PERM = _perm_hid()


def _stage_inputs(c, T, x, in_w, conv_w, conv_b, dt_bias, A_log, D, norm_w, out_w,
                  gW, gS, uW, uS, dW, dS_):
    P = _PERM
    seq, sl = c // 4, (c % 4) * T
    xin = np.zeros((HID, 3 + T), np.float32)
    xin[:, 3:] = x[seq, sl:sl + T, :].T[P]
    if sl > 0:
        xin[:, 0:3] = x[seq, sl - 3:sl, :].T[P]
    win = np.ascontiguousarray(in_w.T[P])
    wout = np.ascontiguousarray(out_w.T[:, P])
    cw = np.ascontiguousarray(conv_w[:, 0, :].reshape(NHT, 128, 4).transpose(1, 0, 2).reshape(128, NHT * 4))
    cb = np.ascontiguousarray(conv_b.reshape(NHT, 128).T)
    nw = np.ascontiguousarray(norm_w.reshape(NGT, 128).T)
    dcol = np.empty((128, NGT), np.float32)
    for t in range(NGT):
        dcol[0:64, t] = D[2 * t]
        dcol[64:128, t] = D[2 * t + 1]
    g8 = gW.view(np.uint8).reshape(16384, 1024, 4)[:, :, 0]
    u8 = uW.view(np.uint8).reshape(16384, 1024, 4)[:, :, 0]
    d8 = dW.view(np.uint8).reshape(HID, 8192, 4)[:, :, 0]
    tc, jc = c // 4, c % 4
    cg = np.ascontiguousarray(g8[4096 * jc:4096 * (jc + 1), 512 * tc:512 * (tc + 1)].T)
    cu = np.ascontiguousarray(u8[4096 * jc:4096 * (jc + 1), 512 * tc:512 * (tc + 1)].T)
    cd = np.ascontiguousarray(d8[P][:, 2048 * jc + 1024 * tc:2048 * jc + 1024 * (tc + 1)].T)
    sg = np.ascontiguousarray(gS[4096 * jc:4096 * (jc + 1), 32 * tc:32 * (tc + 1)].T)
    su = np.ascontiguousarray(uS[4096 * jc:4096 * (jc + 1), 32 * tc:32 * (tc + 1)].T)
    sd = np.ascontiguousarray(dS_[P][:, 128 * jc + 64 * tc:128 * jc + 64 * (tc + 1)].T)
    msumT = np.zeros((8, 8), np.float32)
    pmask = np.zeros((8, 1), np.float32)
    base = 4 * (c // 4)
    for r in range(base, c):
        pmask[r, 0] = 1.0
        for rp in range(r + 1, c):
            msumT[rp, r] = 1.0
    return dict(xin=xin, win=win, wout=wout, convw=cw, convb=cb,
                dtb=np.asarray(dt_bias, np.float32).reshape(64, 1),
                alog=np.asarray(A_log, np.float32).reshape(64, 1),
                dcol=dcol, nw=nw, cg=cg, cu=cu, cd=cd, sg=sg, su=su,
                sd=sd, msumT=msumT, pmask=pmask)


def build_nc(T=1024, debug=False):
    b = Builder(T, debug=debug)
    nc = b.build()
    return nc, b


def kernel(L0_attn_hidden_states, L0_attn_in_proj_w, L0_attn_conv_w, L0_attn_conv_b,
           L0_attn_dt_bias, L0_attn_A_log, L0_attn_D, L0_attn_norm_weight,
           L0_attn_out_proj_w, L0_ffn_gate_W_packed, L0_ffn_gate_scales,
           L0_ffn_up_W_packed, L0_ffn_up_scales, L0_ffn_down_W_packed,
           L0_ffn_down_scales):
    T = 1024
    x = np.asarray(L0_attn_hidden_states, np.float32)
    args = [x, np.asarray(L0_attn_in_proj_w, np.float32),
            np.asarray(L0_attn_conv_w, np.float32), np.asarray(L0_attn_conv_b, np.float32),
            np.asarray(L0_attn_dt_bias, np.float32), np.asarray(L0_attn_A_log, np.float32),
            np.asarray(L0_attn_D, np.float32), np.asarray(L0_attn_norm_weight, np.float32),
            np.asarray(L0_attn_out_proj_w, np.float32),
            np.asarray(L0_ffn_gate_W_packed, np.int32), np.asarray(L0_ffn_gate_scales, np.float32),
            np.asarray(L0_ffn_up_W_packed, np.int32), np.asarray(L0_ffn_up_scales, np.float32),
            np.asarray(L0_ffn_down_W_packed, np.int32), np.asarray(L0_ffn_down_scales, np.float32)]
    in_maps = [_stage_inputs(c, T, *args) for c in range(NCORE)]
    nc, _ = build_nc(T)
    res = run_bass_kernel_spmd(nc, in_maps, core_ids=list(range(NCORE)))
    out = np.empty((2, 4096, HID), np.float32)
    P = _PERM
    for c in range(NCORE):
        seq, sl = c // 4, (c % 4) * T
        out[seq, sl:sl + T][:, P] = np.asarray(res.results[c]["out"]).T
    return out


# version 15 (final)
# speedup vs baseline: 1.0045x; 1.0045x over previous
import contextlib
import numpy as np
import ml_dtypes

import concourse.bass as bass
import concourse.mybir as mybir
from concourse.bass_utils import run_bass_kernel_spmd
from concourse import library_config

BF = ml_dtypes.bfloat16
F32 = mybir.dt.float32
BF16 = mybir.dt.bfloat16
I16 = mybir.dt.int16
U8 = mybir.dt.uint8
Alu = mybir.AluOpType
Act = mybir.ActivationFunctionType

HID = 2048
PROJ = 8512
CONVD = 4352
INTER = 4096
NH = 64
HD = 64
NST = 128
LCH = 128
NKT = HID // 128
NGT = INTER // 128
NHT = CONVD // 128
OC = 67
NCORE = 8
QUAD_GROUPS = [[0, 1, 2, 3], [4, 5, 6, 7]]
PAIR_GROUPS = [[0, 4], [1, 5], [2, 6], [3, 7]]
EPS = 1e-5


def _perm_hid():
    perm = np.empty(HID, np.int64)
    for tb in range(8):
        for par in range(2):
            for p in range(128):
                perm[128 * (2 * tb + par) + p] = 256 * tb + 2 * p + par
    return perm


class Sem:
    __slots__ = ("h", "n", "_cm")

    def __init__(self, nc, name):
        self._cm = nc.semaphore(name)
        self.h = self._cm.__enter__()
        self.n = 0


class Builder:
    def __init__(self, T, debug=False):
        self.T = T
        self.NCH = T // LCH
        self.TSEG = min(512, T)
        self.NTS = T // self.TSEG
        self.debug = debug
        self.nc = bass.Bass("TRN2", target_bir_lowering=False)
        self.q = {"sync": [], "scalar": [], "vector": [], "tensor": [], "gpsimd": []}
        self._stack = contextlib.ExitStack()
        self.dbg_outs = {}
        self._phase_marks = []

    def mark(self, name):
        self._phase_marks.append((name, {k: len(v) for k, v in self.q.items()}))

    def sem(self, name):
        s = Sem(self.nc, name)
        self._sems.append(s)
        return s

    def sb(self, name, shape, dtype):
        return self._cur_stack.enter_context(self.nc.sbuf_tensor(name, list(shape), dtype))

    def ps(self, name, shape, dtype):
        return self._cur_stack.enter_context(self.nc.psum_tensor(name, list(shape), dtype))

    def scope(self):
        return contextlib.ExitStack()

    def _do(self, eng, fn):
        self.q[eng].append(fn)

    def W(self, eng, sem, val=None):
        v = sem.n if val is None else val
        assert v <= sem.n, f"wait on future sem value {v} > {sem.n}"
        if v <= 0:
            return
        self._do(eng, lambda e, h=sem.h, v=v: e.wait_ge(h, v))

    def I(self, eng, fn, sem=None, k=1):
        if sem is not None:
            sem.n += k

        def run(e, fn=fn, sem=sem, k=k):
            ins = fn(e)
            if sem is not None:
                ins.then_inc(sem.h, k)
        self._do(eng, run)

    def D(self, eng):
        self._do(eng, lambda e: e.drain())

    def dbg(self, name, ap_fn, shape, dtype):
        if not self.debug:
            return
        t = self.nc.dram_tensor("dbg_" + name, list(shape), dtype, kind="ExternalOutput").ap()
        self.dbg_outs[name] = "dbg_" + name
        for sem in [self.sDV, self.sAC, self.sPE, self.sGP]:
            self.W("scalar", sem)
        self.I("scalar", lambda e, t=t, f=ap_fn: e.dma_start(t, f()), self.sDP, 16)
        self.W("scalar", self.sDP)

    def dbgsnap(self, name, ap_fn, shape, dtype):
        if not self.debug:
            return
        t = self.nc.dram_tensor("dbg_" + name, list(shape), dtype, kind="ExternalOutput").ap()
        self.dbg_outs[name] = "dbg_" + name
        buf = self.sb("dbgbuf_" + name, shape, dtype)
        self.D("vector")
        self.I("vector", lambda e, b=buf, f=ap_fn: e.tensor_copy(b[:], f()), self.sDV)
        self.W("scalar", self.sDV)
        self.I("scalar", lambda e, t=t, b=buf: e.dma_start(t, b[:]), self.sDP, 16)
        self.W("scalar", self.sDP)

    def build(self):
        nc = self.nc
        T = self.T
        self._sems = []
        with self._stack as top:
            self._cur_stack = top
            self._declare_io()
            s = self
            s.sDP = self.sem("dP")
            s.sPE = self.sem("pe")
            s.sDV = self.sem("dv")
            s.sAC = self.sem("ac")
            s.sGP = self.sem("gp")
            s.sAG = self.sem("ag")
            self._alloc_global()
            self.mark("const")
            self.ph_const()
            with self.scope() as sc:
                self._cur_stack = sc
                self.mark("ln")
                self.ph_ln()
            with self.scope() as sc:
                self._cur_stack = sc
                self.mark("dequant")
                self.ph_dequant()
            self._alloc_mixer_globals()
            with self.scope() as sc:
                self._cur_stack = sc
                self.mark("inproj")
                self.ph_inproj()
            with self.scope() as sc:
                self._cur_stack = sc
                self.mark("mixer")
                self.ph_mixer()
            with self.scope() as sc:
                self._cur_stack = sc
                self.mark("correction")
                self.ph_correction()
            self._grp_state.close()
            with self.scope() as sc:
                self._cur_stack = sc
                self.mark("gating")
                self.ph_gating()
            self._grp_hsy.close()
            with self.scope() as sc:
                self._cur_stack = sc
                self.mark("ffn")
                self.ph_ffn()
            with self.scope() as sc:
                self._cur_stack = sc
                self.mark("combine")
                self.ph_combine()
            self._cur_stack = top
            marks = self._phase_marks + [("end", {k: len(v) for k, v in self.q.items()})]
            with nc.Block() as blk:
                for name in ["sync", "scalar", "vector", "tensor", "gpsimd"]:
                    ops = self.q[name]

                    def body(e, ops=ops, name=name):
                        for mi in range(len(marks) - 1):
                            pname, a = marks[mi][0], marks[mi][1][name]
                            b = marks[mi + 1][1][name]
                            if b <= a:
                                continue
                            with nc.named_scope(pname):
                                for f in ops[a:b]:
                                    f(e)
                        for f in ops[marks[-1][1][name]:]:
                            f(e)
                    getattr(blk, name)(body)
        return nc

    def _declare_io(self):
        nc = self.nc
        T = self.T
        ap = lambda n, s, d, k="ExternalInput": nc.dram_tensor(n, list(s), d, kind=k).ap()
        self.d_x = ap("xin", (HID, 3 + T), F32)
        self.d_win = ap("win", (HID, PROJ), F32)
        self.d_wout = ap("wout", (INTER, HID), F32)
        self.d_convw = ap("convw", (128, NHT * 4), F32)
        self.d_convb = ap("convb", (128, NHT), F32)
        self.d_dtb = ap("dtb", (64, 1), F32)
        self.d_alog = ap("alog", (64, 1), F32)
        self.d_dcol = ap("dcol", (128, NGT), F32)
        self.d_nw = ap("nw", (128, NGT), F32)
        self.d_cg = ap("cg", (512, 4096), U8)
        self.d_cu = ap("cu", (512, 4096), U8)
        self.d_cd = ap("cd", (1024, HID), U8)
        self.d_sg = ap("sg", (32, 4096), F32)
        self.d_su = ap("su", (32, 4096), F32)
        self.d_sd = ap("sd", (64, HID), F32)
        self.d_msumT = ap("msumT", (8, 8), F32)
        self.d_pmask = ap("pmask", (8, 1), F32)
        self.d_out = ap("out", (HID, T), F32, "ExternalOutput")
        ia = lambda n, s, d: nc.dram_tensor(n, list(s), d, kind="Internal").ap()
        self.d_gsp = ia("gate_spill", (INTER, T), BF16)
        self.d_ssp = ia("s_spill", (HID, T), BF16)
        self.d_wsh_g = ia("wsh_g", (1024, 4096), BF16)
        self.d_wsh_u = ia("wsh_u", (1024, 4096), BF16)
        self.d_wsh_d = ia("wsh_d", (2048, HID), BF16)
        self.d_wg = ia("wg_q", (HID, 4096), BF16)
        self.d_wu = ia("wu_q", (HID, 4096), BF16)
        self.d_wd = ia("wd_q", (4096, HID), BF16)
        self.d_xnsh = ia("xn_sh", (HID, T), BF16)
        self.d_xnq = ia("xn_q", (4 * HID, T), BF16)
        self.d_yp = ia("y_part", (4 * HID, T), BF16)
        self.d_yrs = ia("y_rs", (HID, T), BF16)
        self.d_stin = ia("st_in", (128, NH * HD), BF16)
        self.d_stout = ia("st_out", (128 * NCORE, NH * HD), BF16)
        self.d_afin = ia("af_in", (1, 64), F32)
        self.d_afout = ia("af_out", (NCORE, 64), F32)
        self.d_aL = ia("aL_d", (64, self.T), F32)
        self.d_rb = ia("rowb_d", (4, 4096), F32)
        self.d_w8 = ia("w8_d", (8, 64), F32)

    def _alloc_global(self):
        T = self.T
        self.idnB = self.sb("idnB", (128, 128), BF16)
        self.idnF = self.sb("idnF", (128, 128), F32)
        self.trimask = self.sb("trimask", (128, 128), F32)
        self.ones1 = self.sb("ones1", (128, 1), F32)
        self.onesT = self.sb("onesT", (64, T), F32)
        self.zc64 = self.sb("zc64", (64, 1), F32)
        self.dcol = self.sb("dcolS", (128, NGT), F32)
        self.dtb = self.sb("dtbS", (64, 1), F32)
        self.nA = self.sb("nA", (64, 1), F32)
        self.convb = self.sb("convbS", (128, NHT), F32)
        self.convw = self.sb("convwS", (128, NHT * 4), F32)
        self.nwcol = self.sb("nwS", (128, NGT), F32)
        self.xn = self.sb("xn", (128, NKT * T), BF16)
        self.xnh = self.sb("xnh", (128, NKT * 3), BF16)

    def _alloc_mixer_globals(self):
        T = self.T
        self._grp_hsy = contextlib.ExitStack()
        st = self._grp_hsy
        self.hsy = st.enter_context(self.nc.sbuf_tensor("hsy", [128, NGT * T], BF16))
        self._grp_state = contextlib.ExitStack()
        st = self._grp_state
        self.scanstack = st.enter_context(self.nc.sbuf_tensor("scanstack", [128, T], F32))
        self.aG = st.enter_context(self.nc.sbuf_tensor("aG", [64, T], F32))
        self.Bt_ = st.enter_context(self.nc.sbuf_tensor("Bbuf", [128, T], BF16))
        self.Ct_ = st.enter_context(self.nc.sbuf_tensor("Cbuf", [128, T], BF16))
        self.Sst = st.enter_context(self.nc.sbuf_tensor("Sstate", [128, NH * HD], F32))
        self.Sst16 = st.enter_context(self.nc.sbuf_tensor("Sstate16", [128, NH * HD], BF16))
        self.I("gpsimd", lambda e: e.memset(self.Sst[:], 0.0))
        self.I("gpsimd", lambda e: e.memset(self.Sst16[:], 0.0), self.sGP)

    def ph_const(self):
        s = self
        nc = self.nc
        s.I("gpsimd", lambda e: e.memset(s.idnB[:], 1.0))
        s.I("gpsimd", lambda e: e.memset(s.idnF[:], 1.0))
        s.I("gpsimd", lambda e: e.memset(s.onesT[:], 1.0))
        s.D("gpsimd")
        s.I("gpsimd", lambda e: e.affine_select(
            s.idnB[:], s.idnB[:], [[-1, 128]], Alu.is_equal, 0.0, channel_multiplier=1))
        s.D("gpsimd")
        s.I("gpsimd", lambda e: e.affine_select(
            s.idnF[:], s.idnF[:], [[-1, 128]], Alu.is_equal, 0.0, channel_multiplier=1))
        s.D("gpsimd")
        s.I("gpsimd", lambda e: e.memset(s.trimask[:], 1.0))
        s.D("gpsimd")
        s.I("gpsimd", lambda e: e.affine_select(
            s.trimask[:], s.trimask[:], [[1, 128]], Alu.is_ge, 0.0, channel_multiplier=-1))
        s.D("gpsimd")
        s.I("gpsimd", lambda e: e.memset(s.ones1[:], 1.0))
        s.I("gpsimd", lambda e: e.memset(s.zc64[:], 0.0), s.sGP)
        s.dCst = self.sem("dCst")
        for dst, src in [(s.dcol, s.d_dcol), (s.dtb, s.d_dtb), (s.convb, s.d_convb),
                         (s.convw, s.d_convw), (s.nwcol, s.d_nw)]:
            s.I("sync", lambda e, d=dst, sr=src: e.dma_start(d[:], sr[:]), s.dCst, 16)
        s.I("sync", lambda e: e.dma_start(s.nA[:], s.d_alog[:]), s.dCst, 16)
        s.W("scalar", s.dCst)
        s.I("scalar", lambda e: e.activation(s.nA[:], s.nA[:], Act.Exp))
        s.D("scalar")
        s.I("scalar", lambda e: e.mul(s.nA[:], s.nA[:], -1.0), s.sAC)
        s.D("scalar")

    def ph_ln(self):
        s = self
        for eng in ["tensor", "scalar", "vector"]:
            s.W(eng, s.sGP)
        T, TS, NTS = self.T, self.TSEG, self.NTS
        xbuf = s.sb("xbuf", (128, NKT * T), F32)
        xh = s.sb("xh", (128, NKT * 3), F32)
        xsq = [s.sb(f"xsq{i}", (128, TS), F32) for i in range(2)]
        srow0 = s.sb("srow0", (1, 3 + T), F32)
        srow1 = s.sb("srow1", (1, 3 + T), F32)
        mrow = s.sb("mrow", (1, 3 + T), F32)
        vrow = s.sb("vrow", (1, 3 + T), F32)
        inv = s.sb("invrow", (1, 3 + T), F32)
        nmi = s.sb("nmirow", (1, 3 + T), F32)
        invb = s.sb("invb", (128, T), F32)
        nmb = s.sb("nmb", (128, T), F32)
        invbH = s.sb("invbH", (128, 3), F32)
        nmbH = s.sb("nmbH", (128, 3), F32)
        xsq_mm = [0, 0]
        dX = s.sem("dX")
        with s.nc.psum_tensor("lnp0", [1, TS], F32) as p0, \
             s.nc.psum_tensor("lnp1", [1, TS], F32) as p1:
            for k in range(NKT):
                s.I("sync", lambda e, k=k: e.dma_start(
                    xbuf[:, k * T:(k + 1) * T], s.d_x[128 * k:128 * (k + 1), 3:3 + T]), dX, 16)
                s.I("sync", lambda e, k=k: e.dma_start(
                    xh[:, 3 * k:3 * k + 3], s.d_x[128 * k:128 * (k + 1), 0:3]), dX, 16)
            ldS = dX.n
            segs = [(3 + i * TS, TS, False) for i in range(NTS)] + [(0, 3, True)]
            for (c0, w, ishalo) in segs:
                for k in range(NKT):
                    src = (lambda k=k, c0=c0, w=w: xh[:, 3 * k:3 * k + w]) if ishalo \
                        else (lambda k=k, c0=c0, w=w: xbuf[:, k * T + (c0 - 3):k * T + (c0 - 3) + w])
                    s.W("tensor", dX, ldS)
                    s.I("tensor", lambda e, src=src, k=k, w=w: e.matmul(
                        p0[0:1, 0:w], s.ones1[:], src(), start=(k == 0), stop=(k == NKT - 1)))
                for k in range(NKT):
                    src = (lambda k=k, c0=c0, w=w: xh[:, 3 * k:3 * k + w]) if ishalo \
                        else (lambda k=k, c0=c0, w=w: xbuf[:, k * T + (c0 - 3):k * T + (c0 - 3) + w])
                    s.W("scalar", dX, ldS)
                    s.W("scalar", s.sPE, xsq_mm[k % 2])
                    s.I("scalar", lambda e, src=src, k=k, w=w: e.activation(
                        xsq[k % 2][:, 0:w], src(), Act.Square), s.sAC)
                    s.W("tensor", s.sAC)
                    s.I("tensor", lambda e, k=k, w=w: e.matmul(
                        p1[0:1, 0:w], s.ones1[:], xsq[k % 2][:, 0:w],
                        start=(k == 0), stop=(k == NKT - 1)), s.sPE)
                    xsq_mm[k % 2] = s.sPE.n
                s.W("vector", s.sPE)
                s.I("vector", lambda e, c0=c0, w=w: e.tensor_copy(srow0[0:1, c0:c0 + w], p0[0:1, 0:w]))
                s.I("vector", lambda e, c0=c0, w=w: e.tensor_copy(srow1[0:1, c0:c0 + w], p1[0:1, 0:w]), s.sDV)
                s.W("tensor", s.sDV)
        V, A = "vector", "scalar"
        s.D(V)
        s.I(V, lambda e: e.tensor_scalar_mul(mrow[:], srow0[:], 1.0 / HID))
        s.I(V, lambda e: e.tensor_scalar_mul(vrow[:], srow1[:], 1.0 / HID), s.sDV)
        s.D(V)
        s.W(A, s.sDV)
        s.I(A, lambda e: e.activation(inv[:], mrow[:], Act.Square), s.sAC)
        s.W(V, s.sAC)
        s.I(V, lambda e: e.tensor_sub(vrow[:], vrow[:], inv[:]))
        s.D(V)
        s.I(V, lambda e: e.tensor_scalar_add(vrow[:], vrow[:], EPS), s.sDV)
        s.D(V)
        s.W(A, s.sDV)
        s.I(A, lambda e: e.activation(inv[:], vrow[:], Act.Sqrt), s.sAC)
        s.W(V, s.sAC)
        s.I(V, lambda e: e.reciprocal(nmi[:], inv[:]))
        s.D(V)
        s.I(V, lambda e: e.tensor_mul(inv[:], nmi[:], nmi[:]))
        s.D(V)
        s.I(V, lambda e: e.tensor_mul(inv[:], inv[:], vrow[:]))
        s.D(V)
        s.I(V, lambda e: e.tensor_scalar(inv[:], inv[:], -0.5, 1.5, Alu.mult, Alu.add))
        s.D(V)
        s.I(V, lambda e: e.tensor_mul(inv[:], inv[:], nmi[:]))
        s.D(V)
        s.I(V, lambda e: e.scalar_tensor_tensor(
            nmi[:], mrow[:], -1.0, inv[:], Alu.mult, Alu.mult), s.sDV)
        dRb = s.sem("dRbLN")
        s.W("scalar", s.sDV)
        s.I("scalar", lambda e: e.dma_start(s.d_rb[0:1, 0:3 + T], inv[:]), dRb, 16)
        s.I("scalar", lambda e: e.dma_start(s.d_rb[1:2, 0:3 + T], nmi[:]), dRb, 16)
        s.W("sync", dRb)
        s.I("sync", lambda e: e.dma_start(invb[:], s.d_rb[0:1, 3:3 + T].broadcast_to([128, T])), dRb, 16)
        s.I("sync", lambda e: e.dma_start(nmb[:], s.d_rb[1:2, 3:3 + T].broadcast_to([128, T])), dRb, 16)
        s.I("sync", lambda e: e.dma_start(invbH[:], s.d_rb[0:1, 0:3].broadcast_to([128, 3])), dRb, 16)
        s.I("sync", lambda e: e.dma_start(nmbH[:], s.d_rb[1:2, 0:3].broadcast_to([128, 3])), dRb, 16)
        s.W(V, dRb)
        for k in range(NKT):
            s.I(V, lambda e, k=k: e.tensor_mul(xbuf[:, k * T:(k + 1) * T], xbuf[:, k * T:(k + 1) * T], invb[:]))
            s.I(V, lambda e, k=k: e.tensor_mul(xh[:, 3 * k:3 * k + 3], xh[:, 3 * k:3 * k + 3], invbH[:]))
        s.D(V)
        for k in range(NKT):
            s.I(V, lambda e, k=k: e.tensor_add(s.xn[:, k * T:(k + 1) * T], xbuf[:, k * T:(k + 1) * T], nmb[:]))
            s.I(V, lambda e, k=k: e.tensor_add(s.xnh[:, 3 * k:3 * k + 3], xh[:, 3 * k:3 * k + 3], nmbH[:]),
                s.sDV if k == NKT - 1 else None)
        self.sem_xn = s.sDV.n
        s.dbg("xn", lambda: s.xn[:, 0:self.T], (128, self.T), BF16)
        dXst = s.sem("dXst")
        s.W("scalar", s.sDV, self.sem_xn)
        for k in range(NKT):
            s.I("scalar", lambda e, k=k: e.dma_start(
                s.d_xnsh[128 * k:128 * (k + 1), :],
                s.xn[:, k * self.T:(k + 1) * self.T]), dXst, 16)
        self.ln_dXst = dXst

    def ph_dequant(self):
        s = self
        for eng in ["sync", "vector", "gpsimd", "scalar"]:
            s.W(eng, s.sDV, self.sem_xn)
        Q = 4096
        cbuf = [s.sb(f"dq_c{i}", (128, Q), U8) for i in range(2)]
        iu = [s.sb(f"dq_iu{i}", (128, Q // 4), mybir.dt.int32) for i in range(2)]
        iu1 = [s.sb(f"dq_iv{i}", (128, Q // 4), mybir.dt.int32) for i in range(2)]
        ipn = [s.sb(f"dq_pn{i}", (128, Q // 4), mybir.dt.int32) for i in range(2)]
        pn1 = [s.sb(f"dq_pm{i}", (128, Q // 4), mybir.dt.int32) for i in range(2)]
        sgn = [s.sb(f"dq_sn{i}", (128, Q // 4), mybir.dt.int32) for i in range(2)]
        crl = [s.sb(f"dq_cr{i}", (128, Q // 2), BF16) for i in range(2)]
        mbf = [s.sb(f"dq_mb{i}", (128, Q // 2), BF16) for i in range(2)]
        sct = [s.sb(f"dq_sc{i}", (128, Q), F32) for i in range(2)]
        wq = [s.sb(f"dq_w{i}", (128, 2 * Q), BF16) for i in range(2)]
        items = []
        for m, (csrc, ssrc, dsh) in enumerate([
                (s.d_cg, s.d_sg, s.d_wsh_g), (s.d_cu, s.d_su, s.d_wsh_u)]):
            for tbl in range(4):
                items.append(dict(
                    codes=lambda tbl=tbl, c=csrc: c[128 * tbl:128 * (tbl + 1), :],
                    srow=lambda jl, tbl=tbl, sc=ssrc: sc[8 * tbl + jl:8 * tbl + jl + 1, :],
                    dsh=dsh, re=256 * tbl, ro=256 * tbl + 128, c0=0, q=4096))
        for ti in range(8):
            items.append(dict(codes=lambda ti=ti: s.d_cd[128 * ti:128 * (ti + 1), :],
                              srow=lambda jl, ti=ti: s.d_sd[8 * ti + jl:8 * ti + jl + 1, :],
                              dsh=s.d_wsh_d, re=256 * ti, ro=256 * ti + 128, c0=0, q=2048))
        V, G = "vector", "gpsimd"
        I32 = mybir.dt.int32
        SGNMASK = 0x80008000 - (1 << 32)
        gp_done = [0, 0]
        g1n_prev = 0
        v4_prev = 0
        dCb = [s.sem("dCb0"), s.sem("dCb1")]
        dWq = [s.sem("dWq0"), s.sem("dWq1")]
        s.dq_dWq = dWq
        for n, it in enumerate(items):
            r = n % 2
            q = it["q"]
            s.W("sync", s.sGP, gp_done[r])
            s.I("sync", lambda e, it=it, r=r, q=q: e.dma_start(
                cbuf[r][:, 0:q], it["codes"]()), dCb[r], 16)
            for jl in range(8):
                s.I("sync", lambda e, it=it, jl=jl, r=r, q=q: e.dma_start(
                    sct[r][16 * jl:16 * (jl + 1), 0:q],
                    it["srow"](jl).broadcast_to([16, q])), dCb[r], 16)
            ld = dCb[r].n
            s.W(V, dCb[r], ld)
            s.W(V, s.sGP, gp_done[r])
            s.W(G, dCb[r], ld)
            for pl, (sh, pd, dlt) in enumerate([(0, 0, 0), (4, 1, 0), (8, 0, 1), (12, 1, 1)]):
                s.I(V, lambda e, r=r, sh=sh, q=q: e.tensor_scalar(
                    iu[r][:, 0:q // 4], cbuf[r][:, 0:q].bitcast(I32), sh, 0x00000007,
                    Alu.logical_shift_right, Alu.bitwise_and))
                s.I(V, lambda e, r=r, sh=sh, q=q: e.tensor_scalar(
                    iu1[r][:, 0:q // 4], cbuf[r][:, 0:q].bitcast(I32), sh, 0x00070000,
                    Alu.logical_shift_right, Alu.bitwise_and))
                s.D(V)
                s.I(V, lambda e, r=r, q=q: e.tensor_scalar(
                    ipn[r][:, 0:q // 4], iu[r][:, 0:q // 4], 64, 0x00003F00, Alu.mult, Alu.add))
                s.I(V, lambda e, r=r, q=q: e.tensor_scalar(
                    pn1[r][:, 0:q // 4], iu1[r][:, 0:q // 4], 64, 0x3F000000, Alu.mult, Alu.add))
                s.D(V)
                s.I(V, lambda e, r=r, q=q: e.tensor_tensor(
                    ipn[r][:, 0:q // 4], ipn[r][:, 0:q // 4], pn1[r][:, 0:q // 4], Alu.bitwise_or))
                s.I(V, lambda e, r=r, sh=sh, q=q: e.tensor_scalar(
                    sgn[r][:, 0:q // 4], cbuf[r][:, 0:q].bitcast(I32), 12 - sh, SGNMASK,
                    Alu.logical_shift_left, Alu.bitwise_and), s.sDV)
                dvv = s.sDV.n
                s.D(V)
                s.W("scalar", s.sGP, g1n_prev)
                s.W("scalar", s.sDV, dvv)
                s.I("scalar", lambda e, r=r, q=q: e.activation(
                    crl[r][:, 0:q // 2], ipn[r][:, 0:q // 4].bitcast(BF16), Act.Relu,
                    bias=1.0, scale=-1.0), s.sAC)
                acn = s.sAC.n
                s.W(G, s.sDV, v4_prev)
                s.W(G, s.sAC, acn)
                s.I(G, lambda e, r=r, q=q: e.tensor_tensor(
                    mbf[r][:, 0:q // 2], ipn[r][:, 0:q // 4].bitcast(BF16),
                    crl[r][:, 0:q // 2], Alu.subtract), s.sGP)
                g1n_prev = s.sGP.n
                s.W(V, s.sGP, g1n_prev)
                s.I(V, lambda e, r=r, q=q: e.tensor_tensor(
                    mbf[r][:, 0:q // 2].bitcast(I32), mbf[r][:, 0:q // 2].bitcast(I32),
                    sgn[r][:, 0:q // 4], Alu.bitwise_or), s.sDV)
                v4_prev = s.sDV.n
                s.D(V)
                if pl == 0:
                    s.W(G, dWq[r])
                s.W(G, s.sDV, v4_prev)
                s.I(G, lambda e, r=r, pd=pd, dlt=dlt, q=q: e.tensor_tensor(
                    wq[r][:, pd * q + dlt:pd * q + q:2], mbf[r][:, 0:q // 2],
                    sct[r][:, dlt:q:2], Alu.mult), s.sGP)
                s.D(G)
            gp_done[r] = s.sGP.n
            s.W("scalar", s.sGP, gp_done[r])
            s.I("scalar", lambda e, it=it, r=r, q=q: e.dma_start(
                it["dsh"][it["re"]:it["re"] + 128, it["c0"]:it["c0"] + q], wq[r][:, 0:q]), dWq[r], 16)
            s.I("scalar", lambda e, it=it, r=r, q=q: e.dma_start(
                it["dsh"][it["ro"]:it["ro"] + 128, it["c0"]:it["c0"] + q], wq[r][:, q:2 * q]), dWq[r], 16)

    def ph_inproj(self):
        s = self
        for eng in ["sync", "scalar", "vector", "tensor"]:
            s.W(eng, s.sGP)
            s.W(eng, s.dq_dWq[0])
            s.W(eng, s.dq_dWq[1])
        s.W("gpsimd", s.dq_dWq[0])
        s.W("gpsimd", s.dq_dWq[1])
        s.W("gpsimd", s.sDV)
        s.W("gpsimd", s.sAC)
        T, TS, NTS = self.T, self.TSEG, self.NTS
        GRP = 3
        wgrp = [s.sb(f"ip_wb{i}", (128, NKT * GRP * 128), BF16) for i in range(2)]
        hbcb = [s.sb(f"ip_hbc{i}", (128, 3 + T), BF16) for i in range(6)]
        cacc = [s.sb(f"ip_cacc{i}", (128, T), F32) for i in range(2)]
        gst = [s.sb(f"ip_gst{i}", (128, TS), BF16) for i in range(2)]
        dtraw = s.sb("dtraw", (64, 3 + T), F32)
        dtA = s.sb("dtA", (64, T), F32)
        groups = [list(range(g0, min(g0 + GRP, OC))) for g0 in range(0, OC, GRP)]
        V, A = "vector", "scalar"
        hbc_slot = {}
        slot_free = [0] * 6
        slot_next = [0]
        pend_conv = []
        dWgr = [s.sem("dWgr0"), s.sem("dWgr1")]
        s.dGst = [s.sem("dGst0"), s.sem("dGst1")]

        def emit_conv(oc, slot):
            th = oc - 32
            r2 = th % 2
            src = hbcb[slot]
            s.W(V, s.sAC)
            s.I(V, lambda e, src=src, th=th, r2=r2: e.tensor_scalar(
                cacc[r2][:], src[:, 0:T], s.convw[:, th * 4:th * 4 + 1], None, Alu.mult))
            s.D(V)
            for rr in range(1, 4):
                s.I(V, lambda e, src=src, th=th, r2=r2, rr=rr: e.scalar_tensor_tensor(
                    cacc[r2][:], src[:, rr:rr + T], s.convw[:, th * 4 + rr:th * 4 + rr + 1],
                    cacc[r2][:], Alu.mult, Alu.add), s.sDV if rr == 3 else None)
                s.D(V)
            cv = s.sDV.n
            if th < 32:
                dst = lambda th=th: s.hsy[:, th * T:(th + 1) * T]
            elif th == 32:
                dst = lambda: s.Bt_[:, 0:T]
            else:
                dst = lambda: s.Ct_[:, 0:T]
            s.W(A, s.sDV, cv)
            s.I(A, lambda e, dst=dst, th=th, r2=r2: e.activation(
                dst(), cacc[r2][:], Act.Silu, bias=s.convb[:, th:th + 1]), s.sAC)
            slot_free[slot] = s.sDV.n
            return cv

        with s.nc.psum_tensor("ipp", [128, GRP * TS], F32) as pp, \
             s.nc.psum_tensor("iph", [128, 512], F32) as ph:
            mm_grp = [0, 0]
            for gi, grp in enumerate(groups):
                o0 = grp[0]
                gw = sum(128 if o < 66 else 64 for o in grp)
                wb = gi % 2
                s.W("gpsimd", s.sPE, mm_grp[wb])
                for k in range(NKT):
                    s.I("gpsimd", lambda e, k=k, o0=o0, gw=gw, wb=wb: e.dma_start(
                        wgrp[wb][:, k * GRP * 128:k * GRP * 128 + gw],
                        s.d_win[128 * k:128 * (k + 1), 128 * o0:128 * o0 + gw]), dWgr[wb], 16)
                s.W("tensor", dWgr[wb])
                s.W("tensor", s.sDV, self.sem_xn)
                for th in range(NTS):
                    for oi, o in enumerate(grp):
                        ow = 128 if o < 66 else 64
                        last = (oi == len(grp) - 1)
                        for k in range(NKT):
                            s.I("tensor", lambda e, k=k, oi=oi, o=o, ow=ow, th=th, wb=wb: e.matmul(
                                pp[0:ow, oi * TS:(oi + 1) * TS],
                                wgrp[wb][:, k * GRP * 128 + oi * 128:k * GRP * 128 + oi * 128 + ow],
                                s.xn[:, k * T + th * TS:k * T + (th + 1) * TS],
                                start=(k == 0), stop=(k == NKT - 1)),
                                s.sPE if (last and k == NKT - 1) else None)
                    mmv = s.sPE.n
                    if th == 0:
                        halo_os = [oi for oi, o in enumerate(grp) if o >= 32]
                        for oi in halo_os:
                            o = grp[oi]
                            ow = 128 if o < 66 else 64
                            last = (oi == halo_os[-1])
                            for k in range(NKT):
                                s.I("tensor", lambda e, k=k, oi=oi, ow=ow, wb=wb: e.matmul(
                                    ph[0:ow, oi * 4:oi * 4 + 3],
                                    wgrp[wb][:, k * GRP * 128 + oi * 128:k * GRP * 128 + oi * 128 + ow],
                                    s.xnh[:, 3 * k:3 * k + 3],
                                    start=(k == 0), stop=(k == NKT - 1)),
                                    s.sPE if (last and k == NKT - 1) else None)
                    hmv = s.sPE.n
                    for oi, o in enumerate(grp):
                        if o < 32:
                            r = oi % 2
                            s.W(A, s.sPE, mmv)
                            s.W(A, s.dGst[r])
                            s.I(A, lambda e, oi=oi, r=r, th=th: e.activation(
                                gst[r][:], pp[:, oi * TS:(oi + 1) * TS], Act.Copy), s.sAC)
                            s.W("scalar", s.sAC)
                            s.I("scalar", lambda e, o=o, th=th, r=r: e.dma_start(
                                s.d_gsp[128 * o:128 * (o + 1), th * TS:(th + 1) * TS], gst[r][:]), s.dGst[r], 16)
                        elif o < 66:
                            if th == 0:
                                slot = slot_next[0] % 6
                                slot_next[0] += 1
                                hbc_slot[o] = slot
                                s.W(A, s.sDV, slot_free[slot])
                            slot = hbc_slot[o]
                            s.W(A, s.sPE, mmv)
                            s.I(A, lambda e, oi=oi, slot=slot, th=th: e.activation(
                                hbcb[slot][:, 3 + th * TS:3 + (th + 1) * TS],
                                pp[:, oi * TS:(oi + 1) * TS], Act.Copy), s.sAC)
                            if th == 0:
                                s.W(A, s.sPE, hmv)
                                s.I(A, lambda e, oi=oi, slot=slot: e.activation(
                                    hbcb[slot][:, 0:3], ph[:, oi * 4:oi * 4 + 3], Act.Copy), s.sAC)
                            if th == NTS - 1:
                                pend_conv.append((o, slot))
                        else:
                            s.W(V, s.sPE, mmv)
                            if th == 0:
                                s.I(V, lambda e, oi=oi, th=th: e.tensor_copy(
                                    dtraw[:, 3 + th * TS:3 + (th + 1) * TS], pp[0:64, oi * TS:(oi + 1) * TS]))
                                s.W(V, s.sPE, hmv)
                                s.I(V, lambda e, oi=oi: e.tensor_copy(
                                    dtraw[:, 0:3], ph[0:64, oi * 4:oi * 4 + 3]), s.sDV)
                            else:
                                s.I(V, lambda e, oi=oi, th=th: e.tensor_copy(
                                    dtraw[:, 3 + th * TS:3 + (th + 1) * TS], pp[0:64, oi * TS:(oi + 1) * TS]),
                                    s.sDV)
                    s.W("tensor", s.sAC)
                    s.W("tensor", s.sDV)
                mm_grp[wb] = s.sPE.n
                for (oc, slot) in pend_conv:
                    emit_conv(oc, slot)
                pend_conv = []
        s.dbg("Br", lambda: s.Bt_[:, 0:T], (128, T), BF16)
        s.dbg("Cr", lambda: s.Ct_[:, 0:T], (128, T), BF16)
        s.dbg("hs0", lambda: s.hsy[:, 0:T], (128, T), BF16)
        s.dbg("dtraw", lambda: dtraw[:], (64, 3 + T), F32)
        V, A = "vector", "scalar"
        ab = s.sb("dt_ab", (64, T), F32)
        s.D(V)
        s.I(V, lambda e: e.tensor_scalar(
            dtA[:], dtraw[:, 3:3 + T], s.dtb[:], 0.0, Alu.add, Alu.max), s.sDV)
        s.W(A, s.sDV)
        s.I(A, lambda e: e.activation(ab[:], dtraw[:, 3:3 + T], Act.Abs, bias=s.dtb[:]))
        s.D(A)
        s.I(A, lambda e: e.activation(ab[:], ab[:], Act.Exp, scale=-1.0))
        s.D(A)
        s.I(A, lambda e: e.activation(ab[:], ab[:], Act.Ln, bias=1.0), s.sAC)
        s.W(V, s.sAC)
        s.I(V, lambda e: e.tensor_add(s.scanstack[0:64, :], dtA[:], ab[:]))
        s.D(V)
        s.I(V, lambda e: e.tensor_scalar_mul(dtA[:], s.scanstack[0:64, :], s.nA[:]))
        s.D(V)
        s.I(V, lambda e: e.tensor_tensor_scan(
            s.aG[:], s.onesT[:], dtA[:], 0.0, Alu.mult, Alu.add))
        s.D(V)
        for c in range(self.NCH):
            cs = slice(c * LCH, (c + 1) * LCH)
            off = (lambda: s.zc64[:]) if c == 0 else (lambda c=c: s.aG[:, c * LCH - 1:c * LCH])
            s.I(V, lambda e, cs=cs, off=off: e.tensor_scalar(
                s.scanstack[64:128, cs], s.aG[:, cs], off(), None, Alu.subtract),
                s.sDV if c == self.NCH - 1 else None)
        self.sem_scan = s.sDV.n
        s.daL = s.sem("daL")
        s.W("scalar", s.sDV, self.sem_scan)
        s.I("scalar", lambda e: e.dma_start(s.d_aL[:], s.scanstack[64:128, :]), s.daL, 16)
        s.dbg("aG", lambda: s.aG[:], (64, T), F32)
        s.dbg("scanstack", lambda: s.scanstack[:], (128, T), F32)

    def ph_mixer(self):
        s = self
        for eng in ["vector", "scalar", "gpsimd", "tensor"]:
            s.W(eng, s.dGst[0])
            s.W(eng, s.dGst[1])
            s.W(eng, s.sAC)
        T = self.T
        V, A, G, TE = "vector", "scalar", "gpsimd", "tensor"
        scanT = s.sb("scanT", (128, 128), F32)
        lr = s.sb("lr", (128, 64), F32)
        lamE = s.sb("lamE", (128, 64), F32)
        wE = s.sb("wEx", (128, 64), F32)
        gamT = s.sb("gamT", (128, 64), F32)
        Btr = s.sb("Btr", (128, 128), BF16)
        cbm = s.sb("cbm", (128, 128), F32)
        xTc = s.sb("xTc", (128, NGT * 128), BF16)
        xTd = s.sb("xTd", (128, NGT * 128), BF16)
        bcA = [s.sb(f"bcA{i}", (128, 1024), F32) for i in range(2)]
        eMA = [s.sb(f"eMA{i}", (128, 1024), BF16) for i in range(2)]
        mhA = [s.sb(f"mhA{i}", (128, 1024), BF16) for i in range(2)]
        bwA = [s.sb(f"bwA{i}", (128, 1024), BF16) for i in range(2)]
        tsb = [s.sb(f"tsb{i}", (128, 512), BF16) for i in range(2)]
        t1s = s.sb("t1s", (128, 512), F32)
        dLr = s.sem("dLr")
        dBc = [s.sem("dBc0"), s.sem("dBc1")]
        with s.nc.psum_tensor("mxA0", [128, 512], F32) as pA0, \
             s.nc.psum_tensor("mxA1", [128, 512], F32) as pA1, \
             s.nc.psum_tensor("mxB0", [128, 512], F32) as pB0, \
             s.nc.psum_tensor("mxB1", [128, 512], F32) as pB1, \
             s.nc.psum_tensor("mxC0", [128, 512], F32) as pC0, \
             s.nc.psum_tensor("mxC1", [128, 512], F32) as pC1, \
             s.nc.psum_tensor("mxT0", [128, 1024], BF16) as pT0, \
             s.nc.psum_tensor("mxT1", [128, 1024], BF16) as pT1:
            pA, pB, pC, pT = [pA0, pA1], [pB0, pB1], [pC0, pC1], [pT0, pT1]
            evA = [0, 0]; evB = [0, 0]; evC = [0, 0]; evT = [0, 0]
            mmA = [0, 0]; mmB = [0, 0]; mmC = [0, 0]; mmT = [0, 0]
            emv_h = [0, 0]; mhv_h = [0, 0]
            pre_dv = [0]
            supd = {}
            s.W(TE, s.sDV, self.sem_scan)
            s.W(TE, s.sGP)
            for c in range(self.NCH):
                cs = slice(c * LCH, (c + 1) * LCH)
                pe_c0 = s.sPE.n
                dv_c0 = s.sDV.n
                ac_c0 = s.sAC.n
                s.W(TE, s.sDV, evB[0])
                s.I(TE, lambda e, cs=cs: e.matmul(
                    pB0[:, 0:128], s.scanstack[:, cs], s.idnF[:],
                    start=True, stop=True, is_transpose=True), s.sPE)
                mmB[0] = s.sPE.n
                s.W(V, s.sPE, mmB[0])
                s.I(V, lambda e: e.tensor_copy(scanT[:], pB0[:, 0:128]), s.sDV)
                s.D(V)
                evB[0] = s.sDV.n
                scT = s.sDV.n
                s.W(TE, s.sDV, evB[1])
                s.I(TE, lambda e, cs=cs: e.matmul(
                    pB1[:, 0:128], s.Bt_[:, cs], s.Ct_[:, cs], start=True, stop=True), s.sPE)
                mmB[1] = s.sPE.n
                s.W(V, s.sPE, mmB[1])
                s.W(V, s.sGP, mhv_h[1] if c > 0 else 0)
                s.W(V, s.sGP, mhv_h[0] if c > 0 else 0)
                s.I(V, lambda e: e.tensor_mul(cbm[:], pB1[:, 0:128], s.trimask[:]), s.sDV)
                evB[1] = s.sDV.n
                s.W(TE, s.sDV, evT[0])
                s.I(TE, lambda e, cs=cs: e.matmul(
                    pT0[:, 0:128], s.Bt_[:, cs], s.idnB[:],
                    start=True, stop=True, is_transpose=True), s.sPE)
                mmT[0] = s.sPE.n
                s.W(V, s.sPE, mmT[0])
                s.I(V, lambda e: e.tensor_copy(Btr[:], pT0[:, 0:128]), s.sDV)
                evT[0] = s.sDV.n
                s.W("sync", s.daL)
                s.W("sync", s.sAC, ac_c0)
                s.W("sync", s.sDV, dv_c0)
                ce = (c + 1) * LCH - 1
                s.I("sync", lambda e, ce=ce: e.dma_start(
                    lr[:], s.d_aL[:, ce:ce + 1].rearrange("h 1 -> 1 h").broadcast_to([128, 64]),
                    allow_slow_non_contiguous=True), dLr, 16)
                lrv = dLr.n
                s.W(A, s.sDV, dv_c0)
                s.W(A, dLr, lrv)
                s.I(A, lambda e: e.activation(lamE[:], lr[:], Act.Exp), s.sAC)
                s.W(V, dLr, lrv)
                s.I(V, lambda e: e.tensor_sub(wE[:], lr[:], scanT[:, 64:128]), s.sDV)
                wev = s.sDV.n
                s.W(A, s.sDV, wev)
                s.I(A, lambda e: e.activation(wE[:], wE[:], Act.Exp))
                s.D(A)
                s.I(A, lambda e: e.activation(gamT[:], scanT[:, 64:128], Act.Exp), s.sAC)
                pre_ac = s.sAC.n
                for b8 in range(4):
                    pt = b8 % 2
                    s.W(TE, s.sDV, evT[pt])
                    for i in range(8):
                        tau = b8 * 8 + i
                        s.I(TE, lambda e, tau=tau, i=i, pt=pt, c=c: e.matmul(
                            pT[pt][:, i * 128:(i + 1) * 128], s.hsy[:, tau * T + c * LCH:tau * T + (c + 1) * LCH],
                            s.idnB[:], start=True, stop=True, is_transpose=True),
                            s.sPE if i == 7 else None)
                    mmT[pt] = s.sPE.n
                    s.W(V, s.sPE, mmT[pt])
                    s.W(V, s.sPE, pe_c0)
                    s.I(V, lambda e, b8=b8, pt=pt: e.tensor_copy(
                        xTc[:, b8 * 1024:(b8 + 1) * 1024], pT[pt][:]), s.sDV)
                    evT[pt] = s.sDV.n
                s.D(V)
                s.I(V, lambda e: e.tensor_tensor(
                    xTd[:].rearrange("p (h d) -> p h d", d=HD), xTc[:].rearrange("p (h d) -> p h d", d=HD),
                    scanT[:, 0:64].unsqueeze(2).broadcast_to([128, 64, HD]), Alu.mult), s.sDV)
                s.D(V)
                pre_dv[0] = s.sDV.n
                for g in range(8):
                    r = g % 2
                    cg = c * 8 + g
                    s.W("sync", s.sDV, pre_dv[0])
                    s.W("sync", s.sAC, emv_h[r])
                    for i in range(8):
                        h = 8 * g + i
                        s.I("sync", lambda e, h=h, i=i, r=r, c=c: e.dma_start(
                            bcA[r][:, i * 128:(i + 1) * 128],
                            s.d_aL[h:h + 1, c * LCH:(c + 1) * LCH].broadcast_to([128, LCH])),
                            dBc[r], 16)
                    gbv = dBc[r].n
                    s.W(V, dBc[r], gbv)
                    s.I(V, lambda e, r=r, g=g: e.tensor_tensor(
                        bcA[r][:].rearrange("p (h t) -> p h t", t=128),
                        bcA[r][:].rearrange("p (h t) -> p h t", t=128),
                        scanT[:, 64 + 8 * g:64 + 8 * g + 8].unsqueeze(2).broadcast_to([128, 8, 128]),
                        Alu.subtract))
                    s.D(V)
                    s.I(V, lambda e, r=r: e.tensor_scalar_min(bcA[r][:], bcA[r][:], 0.0), s.sDV)
                    dcv = s.sDV.n
                    s.W(A, s.sDV, dcv)
                    s.W(A, s.sGP, mhv_h[r])
                    s.I(A, lambda e, r=r: e.activation(eMA[r][:], bcA[r][:], Act.Exp), s.sAC)
                    emv = s.sAC.n
                    emv_h[r] = emv
                    s.W(G, s.sAC, emv)
                    s.W(G, s.sPE, mmA[r])
                    s.I(G, lambda e, r=r: e.tensor_tensor(
                        mhA[r][:].rearrange("p (h t) -> p h t", t=128),
                        eMA[r][:].rearrange("p (h t) -> p h t", t=128),
                        cbm[:].unsqueeze(1).broadcast_to([128, 8, 128]), Alu.mult), s.sGP)
                    mhv = s.sGP.n
                    mhv_h[r] = mhv
                    s.W(V, s.sAC, pre_ac)
                    s.W(V, s.sPE, mmC[r])
                    s.I(V, lambda e, r=r, g=g: e.tensor_tensor(
                        bwA[r][:].rearrange("p (h n) -> p h n", n=128),
                        Btr[:].unsqueeze(1).broadcast_to([128, 8, 128]),
                        wE[:, 8 * g:8 * g + 8].unsqueeze(2).broadcast_to([128, 8, 128]),
                        Alu.mult), s.sDV)
                    bwv = s.sDV.n
                    s.W(TE, s.sDV, evB[r])
                    if c > 0:
                        s.W(TE, s.sDV, supd[(c - 1, g)])
                    s.I(TE, lambda e, r=r, g=g, cs=cs: e.matmul(
                        pB[r][:], s.Ct_[:, cs], s.Sst16[:, g * 512:(g + 1) * 512],
                        start=True, stop=True), s.sPE)
                    mmB[r] = s.sPE.n
                    s.W(V, s.sPE, mmB[r])
                    s.W(V, s.sPE, mmA[r])
                    s.I(V, lambda e, r=r, g=g: e.tensor_tensor(
                        tsb[r][:].rearrange("p (h d) -> p h d", d=HD),
                        pB[r][:].rearrange("p (h d) -> p h d", d=HD),
                        gamT[:, 8 * g:8 * g + 8].unsqueeze(2).broadcast_to([128, 8, HD]),
                        Alu.mult), s.sDV)
                    evB[r] = s.sDV.n
                    s.W(TE, s.sGP, mhv)
                    s.W(TE, s.sDV, evA[r])
                    for i in range(8):
                        h = 8 * g + i
                        s.I(TE, lambda e, r=r, h=h, i=i: e.matmul(
                            pA[r][64 * (i % 2):64 * (i % 2) + 64, (i // 2) * 128:(i // 2) * 128 + 128],
                            xTd[:, h * HD:(h + 1) * HD], mhA[r][:, i * 128:(i + 1) * 128],
                            start=True, stop=True),
                            s.sPE if i == 7 else None)
                    mmA[r] = s.sPE.n
                    pt2 = g % 2
                    s.W(TE, s.sDV, evB[r])
                    s.W(TE, s.sDV, evT[pt2])
                    for j in range(4):
                        s.I(TE, lambda e, pt2=pt2, r=r, j=j: e.matmul(
                            pT[pt2][:, j * 128:(j + 1) * 128],
                            tsb[r][:, j * 128:(j + 1) * 128], s.idnB[:],
                            start=True, stop=True, is_transpose=True),
                            s.sPE if j == 3 else None)
                    mmT[pt2] = s.sPE.n
                    s.W(V, s.sPE, mmA[r])
                    s.W(V, s.sPE, mmT[pt2])
                    for j in range(4):
                        tau = 4 * g + j
                        s.I(V, lambda e, r=r, tau=tau, j=j, c=c: e.scalar_tensor_tensor(
                            s.hsy[:, tau * T + c * LCH:tau * T + (c + 1) * LCH],
                            s.hsy[:, tau * T + c * LCH:tau * T + (c + 1) * LCH],
                            s.dcol[:, tau:tau + 1], pA[r][:, j * 128:(j + 1) * 128],
                            Alu.mult, Alu.add))
                    s.D(V)
                    for j in range(4):
                        tau = 4 * g + j
                        s.I(V, lambda e, pt2=pt2, tau=tau, j=j, c=c: e.tensor_add(
                            s.hsy[:, tau * T + c * LCH:tau * T + (c + 1) * LCH],
                            s.hsy[:, tau * T + c * LCH:tau * T + (c + 1) * LCH],
                            pT[pt2][:, j * 128:(j + 1) * 128]), s.sDV if j == 3 else None)
                    evA[r] = s.sDV.n
                    evT[pt2] = s.sDV.n
                    s.W(TE, s.sDV, bwv)
                    s.W(TE, s.sDV, evC[r])
                    for i in range(8):
                        h = 8 * g + i
                        s.I(TE, lambda e, r=r, h=h, i=i: e.matmul(
                            pC[r][:, i * HD:(i + 1) * HD],
                            bwA[r][:, i * 128:(i + 1) * 128], xTd[:, h * HD:(h + 1) * HD],
                            start=(i == 0), stop=(i == 7)), s.sPE if i == 7 else None)
                    mmC[r] = s.sPE.n
                    s.W(V, s.sPE, mmC[r])
                    s.I(V, lambda e, g=g: e.tensor_tensor(
                        t1s[:].rearrange("p (h d) -> p h d", d=HD),
                        s.Sst[:, g * 512:(g + 1) * 512].rearrange("p (h d) -> p h d", d=HD),
                        lamE[:, 8 * g:8 * g + 8].unsqueeze(2).broadcast_to([128, 8, HD]),
                        Alu.mult))
                    s.D(V)
                    s.I(V, lambda e, r=r, g=g: e.tensor_add(
                        s.Sst[:, g * 512:(g + 1) * 512], t1s[:], pC[r][:]), s.sDV)
                    evC[r] = s.sDV.n
                    s.D(V)
                    s.I(V, lambda e, g=g: e.tensor_copy(
                        s.Sst16[:, g * 512:(g + 1) * 512], s.Sst[:, g * 512:(g + 1) * 512]), s.sDV)
                    supd[(c, g)] = s.sDV.n
                    if c == 0 and g == 0:
                        s.dbgsnap("mh00", lambda: mhA[0][:, 0:128], (128, 128), BF16)
                        s.dbgsnap("xtd0", lambda: xTd[:, 0:128], (128, 128), BF16)
                        s.dbgsnap("tsb0", lambda: tsb[0][:, 0:128], (128, 128), F32)
                        s.dbgsnap("xtc0", lambda: xTc[:, 0:128], (128, 128), BF16)
                        s.dbgsnap("scT0", lambda: scanT[:], (128, 128), F32)
                if c == 0:
                    s.dbg("ych0", lambda: s.hsy[:, 0:LCH], (128, LCH), BF16)
        dStI = s.sem("dStI")
        s.mx_dStI = dStI
        s.W("scalar", s.sDV)
        s.I("scalar", lambda e: e.dma_start(s.d_stin[:], s.Sst16[:]), dStI, 16)
        with s.nc.psum_tensor("afp", [1, 64], F32) as pf:
            s.I(TE, lambda e: e.matmul(
                pf[:], s.aG[:, T - 1:T], s.idnF[0:64, 0:64],
                start=True, stop=True, is_transpose=True), s.sPE)
            afr = s.sb("afrow", (1, NH * HD), F32)
            s.I(V, lambda e: e.memset(afr[:], 0.0))
            s.D(V)
            s.W(V, s.sPE)
            s.I(V, lambda e: e.tensor_copy(afr[:, 0:64], pf[:]), s.sDV)
            s.W("scalar", s.sDV)
            s.I("scalar", lambda e: e.dma_start(s.d_afin[:], afr[:, 0:64]), dStI, 16)
        s.W("gpsimd", dStI)
        s.I("gpsimd", lambda e: e.collective_compute(
            "AllGather", Alu.bypass, replica_groups=[list(range(NCORE))],
            ins=[s.d_stin[:]], outs=[s.d_stout[:]]), s.sAG)
        s.I("gpsimd", lambda e: e.collective_compute(
            "AllGather", Alu.bypass, replica_groups=[list(range(NCORE))],
            ins=[s.d_afin[:]], outs=[s.d_afout[:]]), s.sAG)
        self.sem_ag4 = s.sAG.n
        s.W("gpsimd", s.ln_dXst)
        s.I("gpsimd", lambda e: e.collective_compute(
            "AllGather", Alu.bypass, replica_groups=QUAD_GROUPS,
            ins=[s.d_xnsh[:]], outs=[s.d_xnq[:]]), s.sAG)
        self.sem_agx = s.sAG.n
        s.W("gpsimd", s.dq_dWq[0])
        s.W("gpsimd", s.dq_dWq[1])
        for src_, dst in [(s.d_wsh_g, s.d_wg), (s.d_wsh_u, s.d_wu),
                          (s.d_wsh_d, s.d_wd)]:
            s.I("gpsimd", lambda e, src_=src_, dst=dst: e.collective_compute(
                "AllGather", Alu.bypass, replica_groups=PAIR_GROUPS,
                ins=[src_[:]], outs=[dst[:]]), s.sAG)
        self.sem_ag3 = s.sAG.n
        s.dbg("Sloc", lambda: s.Sst[:], (128, NH * HD), F32)
        s.dbg("ymix", lambda: s.hsy[:, 0:T], (128, T), BF16)

    def ph_correction(self):
        s = self
        for eng in ["vector", "scalar", "gpsimd", "tensor", "sync"]:
            s.W(eng, s.sPE)
            s.W(eng, s.sAC)
            s.W(eng, s.sGP)
            s.W(eng, s.mx_dStI)
        T = self.T
        V, A, G, TE = "vector", "scalar", "gpsimd", "tensor"
        Sin = s.sb("Sinit", (128, NH * HD), BF16)
        Sld = [s.sb(f"Sld{i}", (128, NH * HD), BF16) for i in range(2)]
        afr8 = s.sb("afr8", (8, 64), F32)
        msT = s.sb("msT", (8, 8), F32)
        pmk = s.sb("pmk", (8, 1), F32)
        w8 = s.sb("w8", (8, 64), F32)
        wb = s.sb("wb", (128, 64), F32)
        aGT = s.sb("aGT", (128, 64), F32)
        ggT = s.sb("ggT", (128, 64), F32)
        tmc = [s.sb(f"tmc{i}", (128, 512), F32) for i in range(2)]
        s.I(V, lambda e: e.memset(Sin[:], 0.0), s.sDV)
        dAfr = s.sem("dAfr")
        dSld = [s.sem("dSld0"), s.sem("dSld1")]
        s.W("sync", s.sAG, self.sem_ag4)
        s.I("sync", lambda e: e.dma_start(afr8[:], s.d_afout[:]), dAfr, 16)
        s.I("sync", lambda e: e.dma_start(msT[:], s.d_msumT[:]), dAfr, 16)
        s.I("sync", lambda e: e.dma_start(pmk[:], s.d_pmask[:]), dAfr, 16)
        ldv = dAfr.n
        with s.nc.psum_tensor("cwp", [8, 64], F32) as pw:
            s.W(TE, dAfr, ldv)
            s.I(TE, lambda e: e.matmul(pw[:], msT[:], afr8[:], start=True, stop=True), s.sPE)
            s.W(A, s.sPE)
            s.I(A, lambda e: e.activation(w8[:], pw[:], Act.Exp), s.sAC)
            s.W(V, s.sAC)
            s.I(V, lambda e: e.tensor_scalar_mul(w8[:], w8[:], pmk[:]), s.sDV)
            wsv = s.sDV.n
        dW8 = s.sem("dW8")
        dWb = s.sem("dWb")
        s.W("scalar", s.sDV, wsv)
        s.I("scalar", lambda e: e.dma_start(s.d_w8[:], w8[:]), dW8, 16)
        for r in range(NCORE):
            rr = r % 2
            s.W("sync", s.sDV)
            s.I("sync", lambda e, r=r, rr=rr: e.dma_start(
                Sld[rr][:], s.d_stout[128 * r:128 * r + 128, :]), dSld[rr], 16)
            ldr = dSld[rr].n
            s.W("sync", dW8)
            s.I("sync", lambda e, r=r: e.dma_start(
                wb[:], s.d_w8[r:r + 1, :].broadcast_to([128, 64])), dWb, 16)
            gv = dWb.n
            s.W(V, dSld[rr], ldr)
            s.W(V, dWb, gv)
            s.I(V, lambda e, rr=rr: e.tensor_tensor(
                Sld[rr][:].rearrange("p (h d) -> p h d", d=HD),
                Sld[rr][:].rearrange("p (h d) -> p h d", d=HD),
                wb[:].unsqueeze(2).broadcast_to([128, 64, HD]), Alu.mult))
            s.D(V)
            s.I(V, lambda e, rr=rr: e.tensor_add(Sin[:], Sin[:], Sld[rr][:]), s.sDV)
            s.D(V)
        sinv = s.sDV.n
        s.dbg("Sinit", lambda: Sin[:], (128, NH * HD), BF16)
        with s.nc.psum_tensor("cpD0", [128, 512], F32) as pD0, \
             s.nc.psum_tensor("cpD1", [128, 512], F32) as pD1, \
             s.nc.psum_tensor("cpE0", [128, 512], F32) as pE0, \
             s.nc.psum_tensor("cpE1", [128, 512], F32) as pE1, \
             s.nc.psum_tensor("cpG", [128, 64], F32) as pG:
            pD, pE = [pD0, pD1], [pE0, pE1]
            mmD = [0, 0]; evD = [0, 0]; mmE = [0, 0]; evE = [0, 0]
            aGTv = 0
            for c in range(self.NCH):
                cs = slice(c * LCH, (c + 1) * LCH)
                s.W(TE, s.sAC, aGTv)
                s.I(TE, lambda e, cs=cs: e.matmul(
                    pG[:], s.aG[:, cs], s.idnF[0:64, 0:64],
                    start=True, stop=True, is_transpose=True), s.sPE)
                mgv = s.sPE.n
                s.W(V, s.sPE, mgv)
                s.I(V, lambda e: e.tensor_copy(aGT[:], pG[:]), s.sDV)
                agv = s.sDV.n
                s.W(A, s.sDV, agv)
                s.I(A, lambda e: e.activation(ggT[:], aGT[:], Act.Exp), s.sAC)
                aGTv = s.sAC.n
                for g in range(8):
                    r = g % 2
                    s.W(TE, s.sDV, evD[r])
                    s.W(TE, s.sDV, sinv)
                    s.I(TE, lambda e, r=r, g=g, cs=cs: e.matmul(
                        pD[r][:], s.Ct_[:, cs], Sin[:, g * 512:(g + 1) * 512],
                        start=True, stop=True), s.sPE)
                    mmD[r] = s.sPE.n
                    s.W(V, s.sPE, mmD[r])
                    s.W(V, s.sPE, mmE[r])
                    s.W(V, s.sAC, aGTv)
                    s.I(V, lambda e, r=r, g=g: e.tensor_tensor(
                        tmc[r][:].rearrange("p (h d) -> p h d", d=HD),
                        pD[r][:].rearrange("p (h d) -> p h d", d=HD),
                        ggT[:, 8 * g:8 * g + 8].unsqueeze(2).broadcast_to([128, 8, HD]),
                        Alu.mult), s.sDV)
                    evD[r] = s.sDV.n
                    s.W(TE, s.sDV, evD[r])
                    s.W(TE, s.sDV, evE[r])
                    for j in range(4):
                        s.I(TE, lambda e, r=r, j=j: e.matmul(
                            pE[r][:, j * 128:(j + 1) * 128],
                            tmc[r][:, j * 128:(j + 1) * 128], s.idnF[:],
                            start=True, stop=True, is_transpose=True),
                            s.sPE if j == 3 else None)
                    mmE[r] = s.sPE.n
                    s.W(V, s.sPE, mmE[r])
                    for j in range(4):
                        tau = 4 * g + j
                        s.I(V, lambda e, r=r, tau=tau, j=j, c=c: e.tensor_add(
                            s.hsy[:, tau * T + c * LCH:tau * T + (c + 1) * LCH],
                            s.hsy[:, tau * T + c * LCH:tau * T + (c + 1) * LCH],
                            pE[r][:, j * 128:(j + 1) * 128]), s.sDV if j == 3 else None)
                    evE[r] = s.sDV.n
        s.dbg("ycorr", lambda: s.hsy[:, 0:T], (128, T), BF16)

    def ph_gating(self):
        s = self
        for eng in ["sync", "scalar", "vector", "tensor", "gpsimd"]:
            s.W(eng, s.sPE)
            s.W(eng, s.sDV)
        T, TS, NTS = self.T, self.TSEG, self.NTS
        V, A, G, TE = "vector", "scalar", "gpsimd", "tensor"
        _gsc = contextlib.ExitStack()
        _sb1 = lambda n, sh, dt_: _gsc.enter_context(s.nc.sbuf_tensor(n, list(sh), dt_))
        gb = _sb1("gateb", (128, NGT * TS), BF16)
        sg = _sb1("sgb", (128, NGT * TS), BF16)
        ygt = [_sb1(f"ygt{i}", (128, TS), F32) for i in range(2)]
        ysq = [_sb1(f"ysq{i}", (128, TS), F32) for i in range(2)]
        rr0 = _sb1("rr0", (1, TS), F32)
        rr1 = _sb1("rr1", (1, TS), F32)
        rr2 = _sb1("rr2", (1, TS), F32)
        rr3 = _sb1("rr3", (1, TS), F32)
        rsqb = _sb1("rsqb", (128, TS), F32)
        s.dRbG = s.sem("dRbG")
        dGb = s.sem("dGb")
        s.W("sync", s.dGst[0])
        s.W("sync", s.dGst[1])
        for th in range(NTS):
            with s.nc.psum_tensor(f"gstat{th}", [1, TS], F32) as pst:
                s.W("sync", s.sDV)
                for tau in range(NGT):
                    s.I("sync", lambda e, tau=tau, th=th: e.dma_start(
                        gb[:, tau * TS:(tau + 1) * TS],
                        s.d_gsp[128 * tau:128 * (tau + 1), th * TS:(th + 1) * TS]), dGb, 16)
                ldv = dGb.n
                for tau in range(NGT):
                    r = tau % 2
                    s.W(A, dGb, ldv)
                    s.W(A, s.sDV)
                    s.I(A, lambda e, tau=tau: e.activation(
                        sg[:, tau * TS:(tau + 1) * TS], gb[:, tau * TS:(tau + 1) * TS], Act.Silu), s.sAC)
                    s.W(V, s.sAC)
                    s.W(V, s.sPE)
                    s.I(V, lambda e, tau=tau, r=r, th=th: e.tensor_mul(
                        ygt[r][:], s.hsy[:, tau * T + th * TS:tau * T + (th + 1) * TS],
                        sg[:, tau * TS:(tau + 1) * TS]), s.sDV)
                    s.W(A, s.sDV)
                    s.I(A, lambda e, r=r: e.activation(ysq[r][:], ygt[r][:], Act.Square), s.sAC)
                    s.W(TE, s.sAC)
                    s.I(TE, lambda e, tau=tau, r=r: e.matmul(
                        pst[:], s.ones1[:], ysq[r][:], start=(tau == 0), stop=(tau == NGT - 1)), s.sPE)
                s.W(V, s.sPE)
                s.I(V, lambda e: e.tensor_copy(rr0[:], pst[:]), s.sDV)
                s.D(V)
            s.I(V, lambda e: e.tensor_scalar(rr1[:], rr0[:], 1.0 / INTER, EPS, Alu.mult, Alu.add), s.sDV)
            s.D(V)
            s.W(A, s.sDV)
            s.I(A, lambda e: e.activation(rr2[:], rr1[:], Act.Sqrt), s.sAC)
            s.W(V, s.sAC)
            s.I(V, lambda e: e.reciprocal(rr3[:], rr2[:]))
            s.D(V)
            s.I(V, lambda e: e.tensor_mul(rr2[:], rr3[:], rr3[:]))
            s.D(V)
            s.I(V, lambda e: e.tensor_mul(rr2[:], rr2[:], rr1[:]))
            s.D(V)
            s.I(V, lambda e: e.tensor_scalar(rr2[:], rr2[:], -0.5, 1.5, Alu.mult, Alu.add))
            s.D(V)
            s.I(V, lambda e: e.tensor_mul(rr3[:], rr3[:], rr2[:]), s.sDV)
            s.W("scalar", s.sDV)
            s.I("scalar", lambda e, th=th: e.dma_start(s.d_rb[2:3, 0:TS], rr3[:]), s.dRbG, 16)
            s.W("sync", s.dRbG)
            s.I("sync", lambda e, th=th: e.dma_start(
                rsqb[:], s.d_rb[2:3, 0:TS].broadcast_to([128, TS])), s.dRbG, 16)
            s.W(V, s.dRbG)
            for tau in range(NGT):
                r = tau % 2
                s.I(V, lambda e, tau=tau, r=r, th=th: e.tensor_mul(
                    ygt[r][:], s.hsy[:, tau * T + th * TS:tau * T + (th + 1) * TS],
                    sg[:, tau * TS:(tau + 1) * TS]))
                s.D(V)
                s.I(V, lambda e, tau=tau, r=r, th=th: e.scalar_tensor_tensor(
                    s.hsy[:, tau * T + th * TS:tau * T + (th + 1) * TS],
                    ygt[r][:], s.nwcol[:, tau:tau + 1], rsqb[:], Alu.mult, Alu.mult),
                    s.sDV if tau == NGT - 1 else None)
        yn_done = s.sDV.n
        s.dbg("yn", lambda: s.hsy[:, 0:T], (128, T), BF16)
        _gsc.close()
        for eng in ["sync", "scalar", "vector", "tensor"]:
            s.W(eng, s.sPE)
            s.W(eng, s.sDV)
            s.W(eng, s.sAC)
        HOG = 2
        wob = [s.sb(f"op_wb{i}", (128, NGT * HOG * 128), BF16) for i in range(2)]
        sst = [s.sb(f"op_s{i}", (128, TS), BF16) for i in range(2)]
        with s.nc.psum_tensor("opp0", [128, 512], F32) as q0, \
             s.nc.psum_tensor("opp1", [128, 512], F32) as q1:
            qq = [q0, q1]
            evq = [0, 0]
            dOw = [s.sem("dOw0"), s.sem("dOw1")]
            mm_hg = [0, 0]
            s.dSst = [s.sem("dSst0"), s.sem("dSst1")]
            s.W("gpsimd", s.sDV, yn_done)
            for hg in range(16 // HOG):
                wb = hg % 2
                s.W("gpsimd", s.sPE, mm_hg[wb])
                for ch in range(NGT):
                    s.I("gpsimd", lambda e, ch=ch, hg=hg, wb=wb: e.dma_start(
                        wob[wb][:, ch * HOG * 128:(ch + 1) * HOG * 128],
                        s.d_wout[128 * ch:128 * (ch + 1), HOG * 128 * hg:HOG * 128 * (hg + 1)]), dOw[wb], 16)
                s.W(TE, dOw[wb])
                s.W(TE, s.sDV, yn_done)
                for th in range(NTS):
                    for oi in range(HOG):
                        r = (th * HOG + oi) % 2
                        s.W(TE, s.sDV, evq[r])
                        for ch in range(NGT):
                            s.I(TE, lambda e, ch=ch, oi=oi, th=th, r=r, wb=wb: e.matmul(
                                qq[r][:, 0:TS], wob[wb][:, ch * HOG * 128 + oi * 128:ch * HOG * 128 + (oi + 1) * 128],
                                s.hsy[:, ch * T + th * TS:ch * T + (th + 1) * TS],
                                start=(ch == 0), stop=(ch == NGT - 1)),
                                s.sPE if ch == NGT - 1 else None)
                        mmq = s.sPE.n
                        s.W(V, s.sPE, mmq)
                        s.W(V, s.dSst[r])
                        s.I(V, lambda e, r=r: e.tensor_copy(sst[r][:], qq[r][:, 0:TS]), s.sDV)
                        evq[r] = s.sDV.n
                        ho = hg * HOG + oi
                        s.W("scalar", s.sDV, evq[r])
                        s.I("scalar", lambda e, ho=ho, th=th, r=r: e.dma_start(
                            s.d_ssp[128 * ho:128 * (ho + 1), th * TS:(th + 1) * TS], sst[r][:]), s.dSst[r], 16)
                mm_hg[wb] = s.sPE.n

    def ph_ffn(self):
        s = self
        for eng in ["sync", "scalar", "vector", "tensor"]:
            s.W(eng, s.sPE)
            s.W(eng, s.dSst[0])
            s.W(eng, s.dSst[1])
            if s.debug:
                s.W(eng, s.sDP)
        T, TS, NTS = self.T, self.TSEG, self.NTS
        V, A, TE = "vector", "scalar", "tensor"
        NB = 512
        NBLK = 4096 // NB
        wgb = s.sb("f_wg", (128, NKT * NB), BF16)
        wub = s.sb("f_wu", (128, NKT * NB), BF16)
        wdb = s.sb("f_wd", (128, (NB // 128) * HID), BF16)
        itb = s.sb("f_it", (128, (NB // 128) * T), BF16)
        glt = [s.sb(f"f_gl{i}", (128, TS), BF16) for i in range(2)]
        dab = s.sb("f_dac", (128, 16 * T), F32)
        xnb = s.sb("f_xnb", (128, NKT * T), BF16)
        dWg = s.sem("dWg")
        dWd = s.sem("dWd")
        dXnb = s.sem("dXnb")
        dYp = s.sem("dYp")
        s.W("sync", s.sAG, self.sem_agx)
        s.W("sync", s.sAG, self.sem_ag3)
        with s.nc.psum_tensor("fp0", [128, 512], F32) as f0, \
             s.nc.psum_tensor("fp1", [128, 512], F32) as f1, \
             s.nc.psum_tensor("fp2", [128, 512], F32) as f2, \
             s.nc.psum_tensor("fp3", [128, 512], F32) as f3, \
             s.nc.psum_tensor("fp4", [128, 512], F32) as f4, \
             s.nc.psum_tensor("fp5", [128, 512], F32) as f5:
            pg = [f0, f2]
            pu = [f1, f3]
            pdn = [f4, f5]
            ev_g = [0, 0]
            ev_d = [0, 0]
            mm_gu_done = 0
            mm_dn_done = 0
            yp_wait = 0
            for bb in range(4):
                s.W("sync", s.sPE, mm_gu_done)
                for k in range(NKT):
                    s.I("sync", lambda e, k=k, bb=bb: e.dma_start(
                        xnb[:, k * T:(k + 1) * T],
                        s.d_xnq[2048 * bb + 128 * k:2048 * bb + 128 * (k + 1), :]), dXnb, 16)
                ld_xn = dXnb.n
                for b in range(NBLK):
                    s.W("sync", s.sPE, mm_gu_done)
                    for k in range(NKT):
                        s.I("sync", lambda e, k=k, b=b: e.dma_start(
                            wgb[:, k * NB:(k + 1) * NB], s.d_wg[128 * k:128 * (k + 1), NB * b:NB * (b + 1)]), dWg, 16)
                        s.I("sync", lambda e, k=k, b=b: e.dma_start(
                            wub[:, k * NB:(k + 1) * NB], s.d_wu[128 * k:128 * (k + 1), NB * b:NB * (b + 1)]), dWg, 16)
                    ld_gu = dWg.n
                    s.W("sync", s.sPE, mm_dn_done)
                    for jc in range(NB // 128):
                        s.I("sync", lambda e, jc=jc, b=b: e.dma_start(
                            wdb[:, jc * HID:(jc + 1) * HID],
                            s.d_wd[NB * b + 128 * jc:NB * b + 128 * (jc + 1), :]), dWd, 16)
                    ld_d = dWd.n
                    s.W(TE, dWg, ld_gu)
                    if b == 0:
                        s.W(TE, dXnb, ld_xn)
                    for j in range(NB // 128):
                        par, w0 = j % 2, 256 * (j // 2)
                        for th in range(NTS):
                            r = (j * NTS + th) % 2
                            s.W(TE, s.sDV, ev_g[r])
                            for k in range(NKT):
                                s.I(TE, lambda e, k=k, j=j, th=th, r=r, par=par, w0=w0: e.matmul(
                                    pg[r][:, 0:TS],
                                    wgb[:, k * NB + w0 + par:k * NB + w0 + par + 255:2],
                                    xnb[:, k * T + th * TS:k * T + (th + 1) * TS],
                                    start=(k == 0), stop=(k == NKT - 1)))
                            for k in range(NKT):
                                s.I(TE, lambda e, k=k, j=j, th=th, r=r, par=par, w0=w0: e.matmul(
                                    pu[r][:, 0:TS],
                                    wub[:, k * NB + w0 + par:k * NB + w0 + par + 255:2],
                                    xnb[:, k * T + th * TS:k * T + (th + 1) * TS],
                                    start=(k == 0), stop=(k == NKT - 1)),
                                    s.sPE if k == NKT - 1 else None)
                            mmv = s.sPE.n
                            s.W(A, s.sPE, mmv)
                            s.I(A, lambda e, r=r: e.activation(glt[r][:], pg[r][:, 0:TS], Act.Gelu_apprx_tanh), s.sAC)
                            s.W(V, s.sAC)
                            s.I(V, lambda e, j=j, th=th, r=r: e.tensor_mul(
                                itb[:, j * T + th * TS:j * T + (th + 1) * TS], glt[r][:], pu[r][:, 0:TS]), s.sDV)
                            ev_g[r] = s.sDV.n
                    mm_gu_done = s.sPE.n
                    it_done = s.sDV.n
                    s.W(TE, dWd, ld_d)
                    s.W(TE, s.sDV, it_done)
                    if b == 0:
                        s.W(V, dYp, yp_wait)
                    for th in range(NTS):
                        for h in range(16):
                            r = (th * 16 + h) % 2
                            s.W(TE, s.sDV, ev_d[r])
                            for jc in range(NB // 128):
                                s.I(TE, lambda e, jc=jc, h=h, th=th, r=r: e.matmul(
                                    pdn[r][:, 0:TS], wdb[:, jc * HID + h * 128:jc * HID + (h + 1) * 128],
                                    itb[:, jc * T + th * TS:jc * T + (th + 1) * TS],
                                    start=(jc == 0), stop=(jc == NB // 128 - 1)),
                                    s.sPE if jc == NB // 128 - 1 else None)
                            mmv = s.sPE.n
                            s.W(V, s.sPE, mmv)
                            if b == 0:
                                s.I(V, lambda e, h=h, th=th, r=r: e.tensor_copy(
                                    dab[:, h * T + th * TS:h * T + (th + 1) * TS], pdn[r][:, 0:TS]), s.sDV)
                            else:
                                s.I(V, lambda e, h=h, th=th, r=r: e.tensor_add(
                                    dab[:, h * T + th * TS:h * T + (th + 1) * TS],
                                    dab[:, h * T + th * TS:h * T + (th + 1) * TS], pdn[r][:, 0:TS]), s.sDV)
                            ev_d[r] = s.sDV.n
                    mm_dn_done = s.sPE.n
                dn_done = s.sDV.n
                s.W("gpsimd", s.sDV, dn_done)
                for h in range(16):
                    s.I("gpsimd", lambda e, h=h, bb=bb: e.dma_start(
                        s.d_yp[2048 * bb + 128 * h:2048 * bb + 128 * (h + 1), :],
                        dab[:, h * T:(h + 1) * T]), dYp, 16)
                yp_wait = dYp.n
        s.W("gpsimd", dYp)
        s.I("gpsimd", lambda e: e.collective_compute(
            "ReduceScatter", Alu.add, replica_groups=QUAD_GROUPS,
            ins=[s.d_yp[:]], outs=[s.d_yrs[:]]), s.sAG)
        self.sem_rs = s.sAG.n

    def ph_combine(self):
        s = self
        T = self.T
        V, A = "vector", "scalar"
        gt = [s.sb(f"cb_g{i}", (128, T), BF16) for i in range(2)]
        sre = [s.sb(f"cb_s{i}", (128, T), BF16) for i in range(2)]
        xre = [s.sb(f"cb_x{i}", (128, T), F32) for i in range(2)]
        yre = [s.sb(f"cb_y{i}", (128, T), BF16) for i in range(2)]
        ost = [s.sb(f"cb_o{i}", (128, T), F32) for i in range(2)]
        out_sem = [s.sem("outD0"), s.sem("outD1")]
        dCmb = [s.sem("dCmb0"), s.sem("dCmb1")]
        s.W("sync", s.dSst[0])
        s.W("sync", s.dSst[1])
        s.W("sync", s.sAG, self.sem_rs)
        for ho in range(16):
            r = ho % 2
            s.W("sync", s.sDV)
            s.I("sync", lambda e, ho=ho, r=r: e.dma_start(
                sre[r][:], s.d_ssp[128 * ho:128 * (ho + 1), :]), dCmb[r], 16)
            s.I("sync", lambda e, ho=ho, r=r: e.dma_start(
                xre[r][:], s.d_x[128 * ho:128 * (ho + 1), 3:3 + T]), dCmb[r], 16)
            s.I("sync", lambda e, ho=ho, r=r: e.dma_start(
                yre[r][:], s.d_yrs[128 * ho:128 * (ho + 1), :]), dCmb[r], 16)
            ldv = dCmb[r].n
            s.W(A, dCmb[r], ldv)
            s.I(A, lambda e, r=r: e.activation(
                gt[r][:], yre[r][:], Act.Silu), s.sAC)
            s.W(V, s.sAC)
            s.W(V, dCmb[r], ldv)
            s.W(V, out_sem[r])
            s.I(V, lambda e, r=r: e.tensor_mul(ost[r][:], sre[r][:], gt[r][:]))
            s.D(V)
            s.I(V, lambda e, r=r: e.tensor_add(ost[r][:], ost[r][:], xre[r][:]), s.sDV)
            s.W("scalar", s.sDV)
            s.I("scalar", lambda e, ho=ho, r=r: e.dma_start(
                s.d_out[128 * ho:128 * (ho + 1), :], ost[r][:]), out_sem[r], 16)
        s.W("scalar", out_sem[0])
        s.W("scalar", out_sem[1])


_PERM = _perm_hid()


def _stage_inputs(c, T, x, in_w, conv_w, conv_b, dt_bias, A_log, D, norm_w, out_w,
                  gW, gS, uW, uS, dW, dS_):
    P = _PERM
    seq, sl = c // 4, (c % 4) * T
    xin = np.zeros((HID, 3 + T), np.float32)
    xin[:, 3:] = x[seq, sl:sl + T, :].T[P]
    if sl > 0:
        xin[:, 0:3] = x[seq, sl - 3:sl, :].T[P]
    win = np.ascontiguousarray(in_w.T[P])
    wout = np.ascontiguousarray(out_w.T[:, P])
    cw = np.ascontiguousarray(conv_w[:, 0, :].reshape(NHT, 128, 4).transpose(1, 0, 2).reshape(128, NHT * 4))
    cb = np.ascontiguousarray(conv_b.reshape(NHT, 128).T)
    nw = np.ascontiguousarray(norm_w.reshape(NGT, 128).T)
    dcol = np.empty((128, NGT), np.float32)
    for t in range(NGT):
        dcol[0:64, t] = D[2 * t]
        dcol[64:128, t] = D[2 * t + 1]
    g8 = gW.view(np.uint8).reshape(16384, 1024, 4)[:, :, 0]
    u8 = uW.view(np.uint8).reshape(16384, 1024, 4)[:, :, 0]
    d8 = dW.view(np.uint8).reshape(HID, 8192, 4)[:, :, 0]
    tc, jc = c // 4, c % 4
    cg = np.ascontiguousarray(g8[4096 * jc:4096 * (jc + 1), 512 * tc:512 * (tc + 1)].T)
    cu = np.ascontiguousarray(u8[4096 * jc:4096 * (jc + 1), 512 * tc:512 * (tc + 1)].T)
    cd = np.ascontiguousarray(d8[P][:, 2048 * jc + 1024 * tc:2048 * jc + 1024 * (tc + 1)].T)
    sg = np.ascontiguousarray(gS[4096 * jc:4096 * (jc + 1), 32 * tc:32 * (tc + 1)].T)
    su = np.ascontiguousarray(uS[4096 * jc:4096 * (jc + 1), 32 * tc:32 * (tc + 1)].T)
    sd = np.ascontiguousarray(dS_[P][:, 128 * jc + 64 * tc:128 * jc + 64 * (tc + 1)].T)
    msumT = np.zeros((8, 8), np.float32)
    pmask = np.zeros((8, 1), np.float32)
    base = 4 * (c // 4)
    for r in range(base, c):
        pmask[r, 0] = 1.0
        for rp in range(r + 1, c):
            msumT[rp, r] = 1.0
    return dict(xin=xin, win=win, wout=wout, convw=cw, convb=cb,
                dtb=np.asarray(dt_bias, np.float32).reshape(64, 1),
                alog=np.asarray(A_log, np.float32).reshape(64, 1),
                dcol=dcol, nw=nw, cg=cg, cu=cu, cd=cd, sg=sg, su=su,
                sd=sd, msumT=msumT, pmask=pmask)


def build_nc(T=1024, debug=False):
    b = Builder(T, debug=debug)
    nc = b.build()
    return nc, b


def kernel(L0_attn_hidden_states, L0_attn_in_proj_w, L0_attn_conv_w, L0_attn_conv_b,
           L0_attn_dt_bias, L0_attn_A_log, L0_attn_D, L0_attn_norm_weight,
           L0_attn_out_proj_w, L0_ffn_gate_W_packed, L0_ffn_gate_scales,
           L0_ffn_up_W_packed, L0_ffn_up_scales, L0_ffn_down_W_packed,
           L0_ffn_down_scales):
    T = 1024
    x = np.asarray(L0_attn_hidden_states, np.float32)
    args = [x, np.asarray(L0_attn_in_proj_w, np.float32),
            np.asarray(L0_attn_conv_w, np.float32), np.asarray(L0_attn_conv_b, np.float32),
            np.asarray(L0_attn_dt_bias, np.float32), np.asarray(L0_attn_A_log, np.float32),
            np.asarray(L0_attn_D, np.float32), np.asarray(L0_attn_norm_weight, np.float32),
            np.asarray(L0_attn_out_proj_w, np.float32),
            np.asarray(L0_ffn_gate_W_packed, np.int32), np.asarray(L0_ffn_gate_scales, np.float32),
            np.asarray(L0_ffn_up_W_packed, np.int32), np.asarray(L0_ffn_up_scales, np.float32),
            np.asarray(L0_ffn_down_W_packed, np.int32), np.asarray(L0_ffn_down_scales, np.float32)]
    in_maps = [_stage_inputs(c, T, *args) for c in range(NCORE)]
    nc, _ = build_nc(T)
    res = run_bass_kernel_spmd(nc, in_maps, core_ids=list(range(NCORE)))
    out = np.empty((2, 4096, HID), np.float32)
    P = _PERM
    for c in range(NCORE):
        seq, sl = c // 4, (c % 4) * T
        out[seq, sl:sl + T][:, P] = np.asarray(res.results[c]["out"]).T
    return out


# version 16
# speedup vs baseline: 1.0104x; 1.0059x over previous
import contextlib
import numpy as np
import ml_dtypes

import concourse.bass as bass
import concourse.mybir as mybir
from concourse.bass_utils import run_bass_kernel_spmd
from concourse import library_config

BF = ml_dtypes.bfloat16
F32 = mybir.dt.float32
BF16 = mybir.dt.bfloat16
I16 = mybir.dt.int16
U8 = mybir.dt.uint8
Alu = mybir.AluOpType
Act = mybir.ActivationFunctionType

HID = 2048
PROJ = 8512
CONVD = 4352
INTER = 4096
NH = 64
HD = 64
NST = 128
LCH = 128
NKT = HID // 128
NGT = INTER // 128
NHT = CONVD // 128
OC = 67
NCORE = 8
QUAD_GROUPS = [[0, 1, 2, 3], [4, 5, 6, 7]]
PAIR_GROUPS = [[0, 4], [1, 5], [2, 6], [3, 7]]
EPS = 1e-5


def _perm_hid():
    perm = np.empty(HID, np.int64)
    for tb in range(8):
        for par in range(2):
            for p in range(128):
                perm[128 * (2 * tb + par) + p] = 256 * tb + 2 * p + par
    return perm


class Sem:
    __slots__ = ("h", "n", "_cm")

    def __init__(self, nc, name):
        self._cm = nc.semaphore(name)
        self.h = self._cm.__enter__()
        self.n = 0


class Builder:
    def __init__(self, T, debug=False):
        self.T = T
        self.NCH = T // LCH
        self.TSEG = min(512, T)
        self.NTS = T // self.TSEG
        self.debug = debug
        self.nc = bass.Bass("TRN2", target_bir_lowering=False)
        self.q = {"sync": [], "scalar": [], "vector": [], "tensor": [], "gpsimd": []}
        self._stack = contextlib.ExitStack()
        self.dbg_outs = {}
        self._phase_marks = []

    def mark(self, name):
        self._phase_marks.append((name, {k: len(v) for k, v in self.q.items()}))

    def sem(self, name):
        s = Sem(self.nc, name)
        self._sems.append(s)
        return s

    def sb(self, name, shape, dtype):
        return self._cur_stack.enter_context(self.nc.sbuf_tensor(name, list(shape), dtype))

    def ps(self, name, shape, dtype):
        return self._cur_stack.enter_context(self.nc.psum_tensor(name, list(shape), dtype))

    def scope(self):
        return contextlib.ExitStack()

    def _do(self, eng, fn):
        self.q[eng].append(fn)

    def W(self, eng, sem, val=None):
        v = sem.n if val is None else val
        assert v <= sem.n, f"wait on future sem value {v} > {sem.n}"
        if v <= 0:
            return
        self._do(eng, lambda e, h=sem.h, v=v: e.wait_ge(h, v))

    def I(self, eng, fn, sem=None, k=1):
        if sem is not None:
            sem.n += k

        def run(e, fn=fn, sem=sem, k=k):
            ins = fn(e)
            if sem is not None:
                ins.then_inc(sem.h, k)
        self._do(eng, run)

    def D(self, eng):
        self._do(eng, lambda e: e.drain())

    def dbg(self, name, ap_fn, shape, dtype):
        if not self.debug:
            return
        t = self.nc.dram_tensor("dbg_" + name, list(shape), dtype, kind="ExternalOutput").ap()
        self.dbg_outs[name] = "dbg_" + name
        for sem in [self.sDV, self.sAC, self.sPE, self.sGP]:
            self.W("scalar", sem)
        self.I("scalar", lambda e, t=t, f=ap_fn: e.dma_start(t, f()), self.sDP, 16)
        self.W("scalar", self.sDP)

    def dbgsnap(self, name, ap_fn, shape, dtype):
        if not self.debug:
            return
        t = self.nc.dram_tensor("dbg_" + name, list(shape), dtype, kind="ExternalOutput").ap()
        self.dbg_outs[name] = "dbg_" + name
        buf = self.sb("dbgbuf_" + name, shape, dtype)
        self.D("vector")
        self.I("vector", lambda e, b=buf, f=ap_fn: e.tensor_copy(b[:], f()), self.sDV)
        self.W("scalar", self.sDV)
        self.I("scalar", lambda e, t=t, b=buf: e.dma_start(t, b[:]), self.sDP, 16)
        self.W("scalar", self.sDP)

    def build(self):
        nc = self.nc
        T = self.T
        self._sems = []
        with self._stack as top:
            self._cur_stack = top
            self._declare_io()
            s = self
            s.sDP = self.sem("dP")
            s.sPE = self.sem("pe")
            s.sDV = self.sem("dv")
            s.sAC = self.sem("ac")
            s.sGP = self.sem("gp")
            s.sAG = self.sem("ag")
            self._alloc_global()
            self.mark("const")
            self.ph_const()
            with self.scope() as sc:
                self._cur_stack = sc
                self.mark("ln")
                self.ph_ln()
            with self.scope() as sc:
                self._cur_stack = sc
                self.mark("dequant")
                self.ph_dequant()
            self._alloc_mixer_globals()
            with self.scope() as sc:
                self._cur_stack = sc
                self.mark("inproj")
                self.ph_inproj()
            with self.scope() as sc:
                self._cur_stack = sc
                self.mark("mixer")
                self.ph_mixer()
            with self.scope() as sc:
                self._cur_stack = sc
                self.mark("correction")
                self.ph_correction()
            self._grp_state.close()
            with self.scope() as sc:
                self._cur_stack = sc
                self.mark("gating")
                self.ph_gating()
            self._grp_hsy.close()
            with self.scope() as sc:
                self._cur_stack = sc
                self.mark("ffn")
                self.ph_ffn()
            with self.scope() as sc:
                self._cur_stack = sc
                self.mark("combine")
                self.ph_combine()
            self._cur_stack = top
            marks = self._phase_marks + [("end", {k: len(v) for k, v in self.q.items()})]
            with nc.Block() as blk:
                for name in ["sync", "scalar", "vector", "tensor", "gpsimd"]:
                    ops = self.q[name]

                    def body(e, ops=ops, name=name):
                        for mi in range(len(marks) - 1):
                            pname, a = marks[mi][0], marks[mi][1][name]
                            b = marks[mi + 1][1][name]
                            if b <= a:
                                continue
                            with nc.named_scope(pname):
                                for f in ops[a:b]:
                                    f(e)
                        for f in ops[marks[-1][1][name]:]:
                            f(e)
                    getattr(blk, name)(body)
        return nc

    def _declare_io(self):
        nc = self.nc
        T = self.T
        ap = lambda n, s, d, k="ExternalInput": nc.dram_tensor(n, list(s), d, kind=k).ap()
        self.d_x = ap("xin", (HID, 3 + T), F32)
        self.d_win = ap("win", (HID, PROJ), F32)
        self.d_wout = ap("wout", (INTER, HID), F32)
        self.d_convw = ap("convw", (128, NHT * 4), F32)
        self.d_convb = ap("convb", (128, NHT), F32)
        self.d_dtb = ap("dtb", (64, 1), F32)
        self.d_alog = ap("alog", (64, 1), F32)
        self.d_dcol = ap("dcol", (128, NGT), F32)
        self.d_nw = ap("nw", (128, NGT), F32)
        self.d_cg = ap("cg", (512, 4096), U8)
        self.d_cu = ap("cu", (512, 4096), U8)
        self.d_cd = ap("cd", (1024, HID), U8)
        self.d_sg = ap("sg", (32, 4096), F32)
        self.d_su = ap("su", (32, 4096), F32)
        self.d_sd = ap("sd", (64, HID), F32)
        self.d_msumT = ap("msumT", (8, 8), F32)
        self.d_pmask = ap("pmask", (8, 1), F32)
        self.d_out = ap("out", (HID, T), F32, "ExternalOutput")
        ia = lambda n, s, d: nc.dram_tensor(n, list(s), d, kind="Internal").ap()
        self.d_gsp = ia("gate_spill", (INTER, T), BF16)
        self.d_ssp = ia("s_spill", (HID, T), BF16)
        self.d_wsh_g = ia("wsh_g", (1024, 4096), BF16)
        self.d_wsh_u = ia("wsh_u", (1024, 4096), BF16)
        self.d_wsh_d = ia("wsh_d", (2048, HID), BF16)
        self.d_wg = ia("wg_q", (HID, 4096), BF16)
        self.d_wu = ia("wu_q", (HID, 4096), BF16)
        self.d_wd = ia("wd_q", (4096, HID), BF16)
        self.d_xnsh = ia("xn_sh", (HID, T), BF16)
        self.d_xnq = ia("xn_q", (4 * HID, T), BF16)
        self.d_yp = ia("y_part", (4 * HID, T), BF16)
        self.d_yrs = ia("y_rs", (HID, T), BF16)
        self.d_stin = ia("st_in", (128, NH * HD), BF16)
        self.d_stout = ia("st_out", (128 * NCORE, NH * HD), BF16)
        self.d_afin = ia("af_in", (1, 64), F32)
        self.d_afout = ia("af_out", (NCORE, 64), F32)
        self.d_aL = ia("aL_d", (64, self.T), F32)
        self.d_rb = ia("rowb_d", (4, 4096), F32)
        self.d_w8 = ia("w8_d", (8, 64), F32)

    def _alloc_global(self):
        T = self.T
        self.idnB = self.sb("idnB", (128, 128), BF16)
        self.idnF = self.sb("idnF", (128, 128), F32)
        self.trimask = self.sb("trimask", (128, 128), F32)
        self.ones1 = self.sb("ones1", (128, 1), F32)
        self.onesT = self.sb("onesT", (64, T), F32)
        self.zc64 = self.sb("zc64", (64, 1), F32)
        self.dcol = self.sb("dcolS", (128, NGT), F32)
        self.dtb = self.sb("dtbS", (64, 1), F32)
        self.nA = self.sb("nA", (64, 1), F32)
        self.convb = self.sb("convbS", (128, NHT), F32)
        self.convw = self.sb("convwS", (128, NHT * 4), F32)
        self.nwcol = self.sb("nwS", (128, NGT), F32)
        self.xn = self.sb("xn", (128, NKT * T), BF16)
        self.xnh = self.sb("xnh", (128, NKT * 3), BF16)

    def _alloc_mixer_globals(self):
        T = self.T
        self._grp_hsy = contextlib.ExitStack()
        st = self._grp_hsy
        self.hsy = st.enter_context(self.nc.sbuf_tensor("hsy", [128, NGT * T], BF16))
        self._grp_state = contextlib.ExitStack()
        st = self._grp_state
        self.scanstack = st.enter_context(self.nc.sbuf_tensor("scanstack", [128, T], F32))
        self.aG = st.enter_context(self.nc.sbuf_tensor("aG", [64, T], F32))
        self.Bt_ = st.enter_context(self.nc.sbuf_tensor("Bbuf", [128, T], BF16))
        self.Ct_ = st.enter_context(self.nc.sbuf_tensor("Cbuf", [128, T], BF16))
        self.Sst = st.enter_context(self.nc.sbuf_tensor("Sstate", [128, NH * HD], F32))
        self.Sst16 = st.enter_context(self.nc.sbuf_tensor("Sstate16", [128, NH * HD], BF16))
        self.I("gpsimd", lambda e: e.memset(self.Sst[:], 0.0))
        self.I("gpsimd", lambda e: e.memset(self.Sst16[:], 0.0), self.sGP)

    def ph_const(self):
        s = self
        nc = self.nc
        s.I("gpsimd", lambda e: e.memset(s.idnB[:], 1.0))
        s.I("gpsimd", lambda e: e.memset(s.idnF[:], 1.0))
        s.I("gpsimd", lambda e: e.memset(s.onesT[:], 1.0))
        s.D("gpsimd")
        s.I("gpsimd", lambda e: e.affine_select(
            s.idnB[:], s.idnB[:], [[-1, 128]], Alu.is_equal, 0.0, channel_multiplier=1))
        s.D("gpsimd")
        s.I("gpsimd", lambda e: e.affine_select(
            s.idnF[:], s.idnF[:], [[-1, 128]], Alu.is_equal, 0.0, channel_multiplier=1))
        s.D("gpsimd")
        s.I("gpsimd", lambda e: e.memset(s.trimask[:], 1.0))
        s.D("gpsimd")
        s.I("gpsimd", lambda e: e.affine_select(
            s.trimask[:], s.trimask[:], [[1, 128]], Alu.is_ge, 0.0, channel_multiplier=-1))
        s.D("gpsimd")
        s.I("gpsimd", lambda e: e.memset(s.ones1[:], 1.0))
        s.I("gpsimd", lambda e: e.memset(s.zc64[:], 0.0), s.sGP)
        s.dCst = self.sem("dCst")
        for dst, src in [(s.dcol, s.d_dcol), (s.dtb, s.d_dtb), (s.convb, s.d_convb),
                         (s.convw, s.d_convw), (s.nwcol, s.d_nw)]:
            s.I("sync", lambda e, d=dst, sr=src: e.dma_start(d[:], sr[:]), s.dCst, 16)
        s.I("sync", lambda e: e.dma_start(s.nA[:], s.d_alog[:]), s.dCst, 16)
        s.W("scalar", s.dCst)
        s.I("scalar", lambda e: e.activation(s.nA[:], s.nA[:], Act.Exp))
        s.D("scalar")
        s.I("scalar", lambda e: e.mul(s.nA[:], s.nA[:], -1.0), s.sAC)
        s.D("scalar")

    def ph_ln(self):
        s = self
        for eng in ["tensor", "scalar", "vector"]:
            s.W(eng, s.sGP)
        T, TS, NTS = self.T, self.TSEG, self.NTS
        xbuf = s.sb("xbuf", (128, NKT * T), F32)
        xh = s.sb("xh", (128, NKT * 3), F32)
        xsq = [s.sb(f"xsq{i}", (128, TS), F32) for i in range(2)]
        srow0 = s.sb("srow0", (1, 3 + T), F32)
        srow1 = s.sb("srow1", (1, 3 + T), F32)
        mrow = s.sb("mrow", (1, 3 + T), F32)
        vrow = s.sb("vrow", (1, 3 + T), F32)
        inv = s.sb("invrow", (1, 3 + T), F32)
        nmi = s.sb("nmirow", (1, 3 + T), F32)
        invb = s.sb("invb", (128, T), F32)
        nmb = s.sb("nmb", (128, T), F32)
        invbH = s.sb("invbH", (128, 3), F32)
        nmbH = s.sb("nmbH", (128, 3), F32)
        xsq_mm = [0, 0]
        dX = s.sem("dX")
        with s.nc.psum_tensor("lnp0", [1, TS], F32) as p0, \
             s.nc.psum_tensor("lnp1", [1, TS], F32) as p1:
            for k in range(NKT):
                s.I("sync", lambda e, k=k: e.dma_start(
                    xbuf[:, k * T:(k + 1) * T], s.d_x[128 * k:128 * (k + 1), 3:3 + T]), dX, 16)
                s.I("sync", lambda e, k=k: e.dma_start(
                    xh[:, 3 * k:3 * k + 3], s.d_x[128 * k:128 * (k + 1), 0:3]), dX, 16)
            ldS = dX.n
            segs = [(3 + i * TS, TS, False) for i in range(NTS)] + [(0, 3, True)]
            for (c0, w, ishalo) in segs:
                for k in range(NKT):
                    src = (lambda k=k, c0=c0, w=w: xh[:, 3 * k:3 * k + w]) if ishalo \
                        else (lambda k=k, c0=c0, w=w: xbuf[:, k * T + (c0 - 3):k * T + (c0 - 3) + w])
                    s.W("tensor", dX, ldS)
                    s.I("tensor", lambda e, src=src, k=k, w=w: e.matmul(
                        p0[0:1, 0:w], s.ones1[:], src(), start=(k == 0), stop=(k == NKT - 1)))
                for k in range(NKT):
                    src = (lambda k=k, c0=c0, w=w: xh[:, 3 * k:3 * k + w]) if ishalo \
                        else (lambda k=k, c0=c0, w=w: xbuf[:, k * T + (c0 - 3):k * T + (c0 - 3) + w])
                    s.W("scalar", dX, ldS)
                    s.W("scalar", s.sPE, xsq_mm[k % 2])
                    s.I("scalar", lambda e, src=src, k=k, w=w: e.activation(
                        xsq[k % 2][:, 0:w], src(), Act.Square), s.sAC)
                    s.W("tensor", s.sAC)
                    s.I("tensor", lambda e, k=k, w=w: e.matmul(
                        p1[0:1, 0:w], s.ones1[:], xsq[k % 2][:, 0:w],
                        start=(k == 0), stop=(k == NKT - 1)), s.sPE)
                    xsq_mm[k % 2] = s.sPE.n
                s.W("vector", s.sPE)
                s.I("vector", lambda e, c0=c0, w=w: e.tensor_copy(srow0[0:1, c0:c0 + w], p0[0:1, 0:w]))
                s.I("vector", lambda e, c0=c0, w=w: e.tensor_copy(srow1[0:1, c0:c0 + w], p1[0:1, 0:w]), s.sDV)
                s.W("tensor", s.sDV)
        V, A = "vector", "scalar"
        s.D(V)
        s.I(V, lambda e: e.tensor_scalar_mul(mrow[:], srow0[:], 1.0 / HID))
        s.I(V, lambda e: e.tensor_scalar_mul(vrow[:], srow1[:], 1.0 / HID), s.sDV)
        s.D(V)
        s.W(A, s.sDV)
        s.I(A, lambda e: e.activation(inv[:], mrow[:], Act.Square), s.sAC)
        s.W(V, s.sAC)
        s.I(V, lambda e: e.tensor_sub(vrow[:], vrow[:], inv[:]))
        s.D(V)
        s.I(V, lambda e: e.tensor_scalar_add(vrow[:], vrow[:], EPS), s.sDV)
        s.D(V)
        s.W(A, s.sDV)
        s.I(A, lambda e: e.activation(inv[:], vrow[:], Act.Sqrt), s.sAC)
        s.W(V, s.sAC)
        s.I(V, lambda e: e.reciprocal(nmi[:], inv[:]))
        s.D(V)
        s.I(V, lambda e: e.tensor_mul(inv[:], nmi[:], nmi[:]))
        s.D(V)
        s.I(V, lambda e: e.tensor_mul(inv[:], inv[:], vrow[:]))
        s.D(V)
        s.I(V, lambda e: e.tensor_scalar(inv[:], inv[:], -0.5, 1.5, Alu.mult, Alu.add))
        s.D(V)
        s.I(V, lambda e: e.tensor_mul(inv[:], inv[:], nmi[:]))
        s.D(V)
        s.I(V, lambda e: e.scalar_tensor_tensor(
            nmi[:], mrow[:], -1.0, inv[:], Alu.mult, Alu.mult), s.sDV)
        dRb = s.sem("dRbLN")
        s.W("scalar", s.sDV)
        s.I("scalar", lambda e: e.dma_start(s.d_rb[0:1, 0:3 + T], inv[:]), dRb, 16)
        s.I("scalar", lambda e: e.dma_start(s.d_rb[1:2, 0:3 + T], nmi[:]), dRb, 16)
        s.W("sync", dRb)
        s.I("sync", lambda e: e.dma_start(invb[:], s.d_rb[0:1, 3:3 + T].broadcast_to([128, T])), dRb, 16)
        s.I("sync", lambda e: e.dma_start(nmb[:], s.d_rb[1:2, 3:3 + T].broadcast_to([128, T])), dRb, 16)
        s.I("sync", lambda e: e.dma_start(invbH[:], s.d_rb[0:1, 0:3].broadcast_to([128, 3])), dRb, 16)
        s.I("sync", lambda e: e.dma_start(nmbH[:], s.d_rb[1:2, 0:3].broadcast_to([128, 3])), dRb, 16)
        s.W(V, dRb)
        for k in range(NKT):
            s.I(V, lambda e, k=k: e.tensor_mul(xbuf[:, k * T:(k + 1) * T], xbuf[:, k * T:(k + 1) * T], invb[:]))
            s.I(V, lambda e, k=k: e.tensor_mul(xh[:, 3 * k:3 * k + 3], xh[:, 3 * k:3 * k + 3], invbH[:]))
        s.D(V)
        for k in range(NKT):
            s.I(V, lambda e, k=k: e.tensor_add(s.xn[:, k * T:(k + 1) * T], xbuf[:, k * T:(k + 1) * T], nmb[:]))
            s.I(V, lambda e, k=k: e.tensor_add(s.xnh[:, 3 * k:3 * k + 3], xh[:, 3 * k:3 * k + 3], nmbH[:]),
                s.sDV if k == NKT - 1 else None)
        self.sem_xn = s.sDV.n
        s.dbg("xn", lambda: s.xn[:, 0:self.T], (128, self.T), BF16)
        dXst = s.sem("dXst")
        s.W("scalar", s.sDV, self.sem_xn)
        for k in range(NKT):
            s.I("scalar", lambda e, k=k: e.dma_start(
                s.d_xnsh[128 * k:128 * (k + 1), :],
                s.xn[:, k * self.T:(k + 1) * self.T]), dXst, 16)
        self.ln_dXst = dXst

    def ph_dequant(self):
        s = self
        for eng in ["sync", "vector", "gpsimd", "scalar"]:
            s.W(eng, s.sDV, self.sem_xn)
        Q = 4096
        cbuf = [s.sb(f"dq_c{i}", (128, Q), U8) for i in range(2)]
        iu = [s.sb(f"dq_iu{i}", (128, Q // 4), mybir.dt.int32) for i in range(2)]
        iu1 = [s.sb(f"dq_iv{i}", (128, Q // 4), mybir.dt.int32) for i in range(2)]
        ipn = [s.sb(f"dq_pn{i}", (128, Q // 4), mybir.dt.int32) for i in range(2)]
        pn1 = [s.sb(f"dq_pm{i}", (128, Q // 4), mybir.dt.int32) for i in range(2)]
        sgn = [s.sb(f"dq_sn{i}", (128, Q // 4), mybir.dt.int32) for i in range(2)]
        crl = [s.sb(f"dq_cr{i}", (128, Q // 2), BF16) for i in range(2)]
        mbf = [s.sb(f"dq_mb{i}", (128, Q // 2), BF16) for i in range(2)]
        sct = [s.sb(f"dq_sc{i}", (128, Q), F32) for i in range(2)]
        wq = [s.sb(f"dq_w{i}", (128, 2 * Q), BF16) for i in range(2)]
        items = []
        for m, (csrc, ssrc, dsh) in enumerate([
                (s.d_cg, s.d_sg, s.d_wsh_g), (s.d_cu, s.d_su, s.d_wsh_u)]):
            for tbl in range(4):
                items.append(dict(
                    codes=lambda tbl=tbl, c=csrc: c[128 * tbl:128 * (tbl + 1), :],
                    srow=lambda jl, tbl=tbl, sc=ssrc: sc[8 * tbl + jl:8 * tbl + jl + 1, :],
                    dsh=dsh, re=256 * tbl, ro=256 * tbl + 128, c0=0, q=4096))
        for ti in range(8):
            items.append(dict(codes=lambda ti=ti: s.d_cd[128 * ti:128 * (ti + 1), :],
                              srow=lambda jl, ti=ti: s.d_sd[8 * ti + jl:8 * ti + jl + 1, :],
                              dsh=s.d_wsh_d, re=256 * ti, ro=256 * ti + 128, c0=0, q=2048))
        V, G = "vector", "gpsimd"
        I32 = mybir.dt.int32
        SGNMASK = 0x80008000 - (1 << 32)
        gp_done = [0, 0]
        g1n_prev = 0
        v4_prev = 0
        dCb = [s.sem("dCb0"), s.sem("dCb1")]
        dWq = [s.sem("dWq0"), s.sem("dWq1")]
        s.dq_dWq = dWq
        for n, it in enumerate(items):
            r = n % 2
            q = it["q"]
            s.W("sync", s.sGP, gp_done[r])
            s.I("sync", lambda e, it=it, r=r, q=q: e.dma_start(
                cbuf[r][:, 0:q], it["codes"]()), dCb[r], 16)
            for jl in range(8):
                s.I("sync", lambda e, it=it, jl=jl, r=r, q=q: e.dma_start(
                    sct[r][16 * jl:16 * (jl + 1), 0:q],
                    it["srow"](jl).broadcast_to([16, q])), dCb[r], 16)
            ld = dCb[r].n
            s.W(V, dCb[r], ld)
            s.W(V, s.sGP, gp_done[r])
            s.W(G, dCb[r], ld)
            for pl, (sh, pd, dlt) in enumerate([(0, 0, 0), (4, 1, 0), (8, 0, 1), (12, 1, 1)]):
                s.I(V, lambda e, r=r, sh=sh, q=q: e.tensor_scalar(
                    iu[r][:, 0:q // 4], cbuf[r][:, 0:q].bitcast(I32), sh, 0x00000007,
                    Alu.logical_shift_right, Alu.bitwise_and))
                s.I(V, lambda e, r=r, sh=sh, q=q: e.tensor_scalar(
                    iu1[r][:, 0:q // 4], cbuf[r][:, 0:q].bitcast(I32), sh, 0x00070000,
                    Alu.logical_shift_right, Alu.bitwise_and))
                s.D(V)
                s.I(V, lambda e, r=r, q=q: e.tensor_scalar(
                    ipn[r][:, 0:q // 4], iu[r][:, 0:q // 4], 64, 0x00003F00, Alu.mult, Alu.add))
                s.I(V, lambda e, r=r, q=q: e.tensor_scalar(
                    pn1[r][:, 0:q // 4], iu1[r][:, 0:q // 4], 64, 0x3F000000, Alu.mult, Alu.add))
                s.D(V)
                s.I(V, lambda e, r=r, q=q: e.tensor_tensor(
                    ipn[r][:, 0:q // 4], ipn[r][:, 0:q // 4], pn1[r][:, 0:q // 4], Alu.bitwise_or))
                s.I(V, lambda e, r=r, sh=sh, q=q: e.tensor_scalar(
                    sgn[r][:, 0:q // 4], cbuf[r][:, 0:q].bitcast(I32), 12 - sh, SGNMASK,
                    Alu.logical_shift_left, Alu.bitwise_and), s.sDV)
                dvv = s.sDV.n
                s.D(V)
                s.W("scalar", s.sGP, g1n_prev)
                s.W("scalar", s.sDV, dvv)
                s.I("scalar", lambda e, r=r, q=q: e.activation(
                    crl[r][:, 0:q // 2], ipn[r][:, 0:q // 4].bitcast(BF16), Act.Relu,
                    bias=1.0, scale=-1.0), s.sAC)
                acn = s.sAC.n
                s.W(G, s.sDV, v4_prev)
                s.W(G, s.sAC, acn)
                s.I(G, lambda e, r=r, q=q: e.tensor_tensor(
                    mbf[r][:, 0:q // 2], ipn[r][:, 0:q // 4].bitcast(BF16),
                    crl[r][:, 0:q // 2], Alu.subtract), s.sGP)
                g1n_prev = s.sGP.n
                s.W(V, s.sGP, g1n_prev)
                s.I(V, lambda e, r=r, q=q: e.tensor_tensor(
                    mbf[r][:, 0:q // 2].bitcast(I32), mbf[r][:, 0:q // 2].bitcast(I32),
                    sgn[r][:, 0:q // 4], Alu.bitwise_or), s.sDV)
                v4_prev = s.sDV.n
                s.D(V)
                if pl == 0:
                    s.W(G, dWq[r])
                s.W(G, s.sDV, v4_prev)
                s.I(G, lambda e, r=r, pd=pd, dlt=dlt, q=q: e.tensor_tensor(
                    wq[r][:, pd * q + dlt:pd * q + q:2], mbf[r][:, 0:q // 2],
                    sct[r][:, dlt:q:2], Alu.mult), s.sGP)
                s.D(G)
            gp_done[r] = s.sGP.n
            s.W("scalar", s.sGP, gp_done[r])
            s.I("scalar", lambda e, it=it, r=r, q=q: e.dma_start(
                it["dsh"][it["re"]:it["re"] + 128, it["c0"]:it["c0"] + q], wq[r][:, 0:q]), dWq[r], 16)
            s.I("scalar", lambda e, it=it, r=r, q=q: e.dma_start(
                it["dsh"][it["ro"]:it["ro"] + 128, it["c0"]:it["c0"] + q], wq[r][:, q:2 * q]), dWq[r], 16)

    def ph_inproj(self):
        s = self
        for eng in ["sync", "scalar", "vector", "tensor"]:
            s.W(eng, s.sGP)
            s.W(eng, s.dq_dWq[0])
            s.W(eng, s.dq_dWq[1])
        s.W("gpsimd", s.dq_dWq[0])
        s.W("gpsimd", s.dq_dWq[1])
        s.W("gpsimd", s.sDV)
        s.W("gpsimd", s.sAC)
        T, TS, NTS = self.T, self.TSEG, self.NTS
        GRP = 4
        wgrp = [s.sb(f"ip_wb{i}", (128, NKT * GRP * 128), BF16) for i in range(2)]
        hbcb = [s.sb(f"ip_hbc{i}", (128, 3 + T), BF16) for i in range(6)]
        cacc = [s.sb(f"ip_cacc{i}", (128, T), F32) for i in range(2)]
        gst = [s.sb(f"ip_gst{i}", (128, TS), BF16) for i in range(2)]
        dtraw = s.sb("dtraw", (64, 3 + T), F32)
        dtA = s.sb("dtA", (64, T), F32)
        groups = [list(range(g0, min(g0 + GRP, OC))) for g0 in range(0, OC, GRP)]
        V, A = "vector", "scalar"
        hbc_slot = {}
        slot_free = [0] * 6
        slot_next = [0]
        pend_conv = []
        dWgr = [s.sem("dWgr0"), s.sem("dWgr1")]
        s.dGst = [s.sem("dGst0"), s.sem("dGst1")]

        def emit_conv(oc, slot):
            th = oc - 32
            r2 = th % 2
            src = hbcb[slot]
            s.W(V, s.sAC)
            s.I(V, lambda e, src=src, th=th, r2=r2: e.tensor_scalar(
                cacc[r2][:], src[:, 0:T], s.convw[:, th * 4:th * 4 + 1], None, Alu.mult))
            s.D(V)
            for rr in range(1, 4):
                s.I(V, lambda e, src=src, th=th, r2=r2, rr=rr: e.scalar_tensor_tensor(
                    cacc[r2][:], src[:, rr:rr + T], s.convw[:, th * 4 + rr:th * 4 + rr + 1],
                    cacc[r2][:], Alu.mult, Alu.add), s.sDV if rr == 3 else None)
                s.D(V)
            cv = s.sDV.n
            if th < 32:
                dst = lambda th=th: s.hsy[:, th * T:(th + 1) * T]
            elif th == 32:
                dst = lambda: s.Bt_[:, 0:T]
            else:
                dst = lambda: s.Ct_[:, 0:T]
            s.W(A, s.sDV, cv)
            s.I(A, lambda e, dst=dst, th=th, r2=r2: e.activation(
                dst(), cacc[r2][:], Act.Silu, bias=s.convb[:, th:th + 1]), s.sAC)
            slot_free[slot] = s.sDV.n
            return cv

        with s.nc.psum_tensor("ipp", [128, GRP * TS], F32) as pp, \
             s.nc.psum_tensor("iph", [128, 512], F32) as ph:
            mm_grp = [0, 0]
            for gi, grp in enumerate(groups):
                o0 = grp[0]
                gw = sum(128 if o < 66 else 64 for o in grp)
                wb = gi % 2
                s.W("gpsimd", s.sPE, mm_grp[wb])
                for k in range(NKT):
                    s.I("gpsimd", lambda e, k=k, o0=o0, gw=gw, wb=wb: e.dma_start(
                        wgrp[wb][:, k * GRP * 128:k * GRP * 128 + gw],
                        s.d_win[128 * k:128 * (k + 1), 128 * o0:128 * o0 + gw]), dWgr[wb], 16)
                s.W("tensor", dWgr[wb])
                s.W("tensor", s.sDV, self.sem_xn)
                for th in range(NTS):
                    for oi, o in enumerate(grp):
                        ow = 128 if o < 66 else 64
                        last = (oi == len(grp) - 1)
                        for k in range(NKT):
                            s.I("tensor", lambda e, k=k, oi=oi, o=o, ow=ow, th=th, wb=wb: e.matmul(
                                pp[0:ow, oi * TS:(oi + 1) * TS],
                                wgrp[wb][:, k * GRP * 128 + oi * 128:k * GRP * 128 + oi * 128 + ow],
                                s.xn[:, k * T + th * TS:k * T + (th + 1) * TS],
                                start=(k == 0), stop=(k == NKT - 1)),
                                s.sPE if (last and k == NKT - 1) else None)
                    mmv = s.sPE.n
                    if th == 0:
                        halo_os = [oi for oi, o in enumerate(grp) if o >= 32]
                        for oi in halo_os:
                            o = grp[oi]
                            ow = 128 if o < 66 else 64
                            last = (oi == halo_os[-1])
                            for k in range(NKT):
                                s.I("tensor", lambda e, k=k, oi=oi, ow=ow, wb=wb: e.matmul(
                                    ph[0:ow, oi * 4:oi * 4 + 3],
                                    wgrp[wb][:, k * GRP * 128 + oi * 128:k * GRP * 128 + oi * 128 + ow],
                                    s.xnh[:, 3 * k:3 * k + 3],
                                    start=(k == 0), stop=(k == NKT - 1)),
                                    s.sPE if (last and k == NKT - 1) else None)
                    hmv = s.sPE.n
                    for oi, o in enumerate(grp):
                        if o < 32:
                            r = oi % 2
                            s.W(A, s.sPE, mmv)
                            s.W(A, s.dGst[r])
                            s.I(A, lambda e, oi=oi, r=r, th=th: e.activation(
                                gst[r][:], pp[:, oi * TS:(oi + 1) * TS], Act.Copy), s.sAC)
                            s.W("scalar", s.sAC)
                            s.I("scalar", lambda e, o=o, th=th, r=r: e.dma_start(
                                s.d_gsp[128 * o:128 * (o + 1), th * TS:(th + 1) * TS], gst[r][:]), s.dGst[r], 16)
                        elif o < 66:
                            if th == 0:
                                slot = slot_next[0] % 6
                                slot_next[0] += 1
                                hbc_slot[o] = slot
                                s.W(A, s.sDV, slot_free[slot])
                            slot = hbc_slot[o]
                            s.W(A, s.sPE, mmv)
                            s.I(A, lambda e, oi=oi, slot=slot, th=th: e.activation(
                                hbcb[slot][:, 3 + th * TS:3 + (th + 1) * TS],
                                pp[:, oi * TS:(oi + 1) * TS], Act.Copy), s.sAC)
                            if th == 0:
                                s.W(A, s.sPE, hmv)
                                s.I(A, lambda e, oi=oi, slot=slot: e.activation(
                                    hbcb[slot][:, 0:3], ph[:, oi * 4:oi * 4 + 3], Act.Copy), s.sAC)
                            if th == NTS - 1:
                                pend_conv.append((o, slot))
                        else:
                            s.W(V, s.sPE, mmv)
                            if th == 0:
                                s.I(V, lambda e, oi=oi, th=th: e.tensor_copy(
                                    dtraw[:, 3 + th * TS:3 + (th + 1) * TS], pp[0:64, oi * TS:(oi + 1) * TS]))
                                s.W(V, s.sPE, hmv)
                                s.I(V, lambda e, oi=oi: e.tensor_copy(
                                    dtraw[:, 0:3], ph[0:64, oi * 4:oi * 4 + 3]), s.sDV)
                            else:
                                s.I(V, lambda e, oi=oi, th=th: e.tensor_copy(
                                    dtraw[:, 3 + th * TS:3 + (th + 1) * TS], pp[0:64, oi * TS:(oi + 1) * TS]),
                                    s.sDV)
                    s.W("tensor", s.sAC)
                    s.W("tensor", s.sDV)
                mm_grp[wb] = s.sPE.n
                for (oc, slot) in pend_conv:
                    emit_conv(oc, slot)
                pend_conv = []
        s.dbg("Br", lambda: s.Bt_[:, 0:T], (128, T), BF16)
        s.dbg("Cr", lambda: s.Ct_[:, 0:T], (128, T), BF16)
        s.dbg("hs0", lambda: s.hsy[:, 0:T], (128, T), BF16)
        s.dbg("dtraw", lambda: dtraw[:], (64, 3 + T), F32)
        V, A = "vector", "scalar"
        ab = s.sb("dt_ab", (64, T), F32)
        s.D(V)
        s.I(V, lambda e: e.tensor_scalar(
            dtA[:], dtraw[:, 3:3 + T], s.dtb[:], 0.0, Alu.add, Alu.max), s.sDV)
        s.W(A, s.sDV)
        s.I(A, lambda e: e.activation(ab[:], dtraw[:, 3:3 + T], Act.Abs, bias=s.dtb[:]))
        s.D(A)
        s.I(A, lambda e: e.activation(ab[:], ab[:], Act.Exp, scale=-1.0))
        s.D(A)
        s.I(A, lambda e: e.activation(ab[:], ab[:], Act.Ln, bias=1.0), s.sAC)
        s.W(V, s.sAC)
        s.I(V, lambda e: e.tensor_add(s.scanstack[0:64, :], dtA[:], ab[:]))
        s.D(V)
        s.I(V, lambda e: e.tensor_scalar_mul(dtA[:], s.scanstack[0:64, :], s.nA[:]))
        s.D(V)
        s.I(V, lambda e: e.tensor_tensor_scan(
            s.aG[:], s.onesT[:], dtA[:], 0.0, Alu.mult, Alu.add))
        s.D(V)
        for c in range(self.NCH):
            cs = slice(c * LCH, (c + 1) * LCH)
            off = (lambda: s.zc64[:]) if c == 0 else (lambda c=c: s.aG[:, c * LCH - 1:c * LCH])
            s.I(V, lambda e, cs=cs, off=off: e.tensor_scalar(
                s.scanstack[64:128, cs], s.aG[:, cs], off(), None, Alu.subtract),
                s.sDV if c == self.NCH - 1 else None)
        self.sem_scan = s.sDV.n
        s.daL = s.sem("daL")
        s.W("scalar", s.sDV, self.sem_scan)
        s.I("scalar", lambda e: e.dma_start(s.d_aL[:], s.scanstack[64:128, :]), s.daL, 16)
        s.dbg("aG", lambda: s.aG[:], (64, T), F32)
        s.dbg("scanstack", lambda: s.scanstack[:], (128, T), F32)

    def ph_mixer(self):
        s = self
        for eng in ["vector", "scalar", "gpsimd", "tensor"]:
            s.W(eng, s.dGst[0])
            s.W(eng, s.dGst[1])
            s.W(eng, s.sAC)
        T = self.T
        V, A, G, TE = "vector", "scalar", "gpsimd", "tensor"
        scanT = s.sb("scanT", (128, 128), F32)
        lr = s.sb("lr", (128, 64), F32)
        lamE = s.sb("lamE", (128, 64), F32)
        wE = s.sb("wEx", (128, 64), F32)
        gamT = s.sb("gamT", (128, 64), F32)
        Btr = s.sb("Btr", (128, 128), BF16)
        cbm = s.sb("cbm", (128, 128), F32)
        xTc = s.sb("xTc", (128, NGT * 128), BF16)
        xTd = s.sb("xTd", (128, NGT * 128), BF16)
        bcA = [s.sb(f"bcA{i}", (128, 1024), F32) for i in range(2)]
        eMA = [s.sb(f"eMA{i}", (128, 1024), BF16) for i in range(2)]
        mhA = [s.sb(f"mhA{i}", (128, 1024), BF16) for i in range(2)]
        bwA = [s.sb(f"bwA{i}", (128, 1024), BF16) for i in range(2)]
        tsb = [s.sb(f"tsb{i}", (128, 512), BF16) for i in range(2)]
        t1s = s.sb("t1s", (128, 512), F32)
        dLr = s.sem("dLr")
        dBc = [s.sem("dBc0"), s.sem("dBc1")]
        with s.nc.psum_tensor("mxA0", [128, 512], F32) as pA0, \
             s.nc.psum_tensor("mxA1", [128, 512], F32) as pA1, \
             s.nc.psum_tensor("mxB0", [128, 512], F32) as pB0, \
             s.nc.psum_tensor("mxB1", [128, 512], F32) as pB1, \
             s.nc.psum_tensor("mxC0", [128, 512], F32) as pC0, \
             s.nc.psum_tensor("mxC1", [128, 512], F32) as pC1, \
             s.nc.psum_tensor("mxT0", [128, 1024], BF16) as pT0, \
             s.nc.psum_tensor("mxT1", [128, 1024], BF16) as pT1:
            pA, pB, pC, pT = [pA0, pA1], [pB0, pB1], [pC0, pC1], [pT0, pT1]
            evA = [0, 0]; evB = [0, 0]; evC = [0, 0]; evT = [0, 0]
            mmA = [0, 0]; mmB = [0, 0]; mmC = [0, 0]; mmT = [0, 0]
            emv_h = [0, 0]; mhv_h = [0, 0]
            pre_dv = [0]
            supd = {}
            s.W(TE, s.sDV, self.sem_scan)
            s.W(TE, s.sGP)
            for c in range(self.NCH):
                cs = slice(c * LCH, (c + 1) * LCH)
                pe_c0 = s.sPE.n
                dv_c0 = s.sDV.n
                ac_c0 = s.sAC.n
                s.W(TE, s.sDV, evB[0])
                s.I(TE, lambda e, cs=cs: e.matmul(
                    pB0[:, 0:128], s.scanstack[:, cs], s.idnF[:],
                    start=True, stop=True, is_transpose=True), s.sPE)
                mmB[0] = s.sPE.n
                s.W(V, s.sPE, mmB[0])
                s.I(V, lambda e: e.tensor_copy(scanT[:], pB0[:, 0:128]), s.sDV)
                s.D(V)
                evB[0] = s.sDV.n
                scT = s.sDV.n
                s.W(TE, s.sDV, evB[1])
                s.I(TE, lambda e, cs=cs: e.matmul(
                    pB1[:, 0:128], s.Bt_[:, cs], s.Ct_[:, cs], start=True, stop=True), s.sPE)
                mmB[1] = s.sPE.n
                s.W(V, s.sPE, mmB[1])
                s.W(V, s.sGP, mhv_h[1] if c > 0 else 0)
                s.W(V, s.sGP, mhv_h[0] if c > 0 else 0)
                s.I(V, lambda e: e.tensor_mul(cbm[:], pB1[:, 0:128], s.trimask[:]), s.sDV)
                evB[1] = s.sDV.n
                s.W(TE, s.sDV, evT[0])
                s.I(TE, lambda e, cs=cs: e.matmul(
                    pT0[:, 0:128], s.Bt_[:, cs], s.idnB[:],
                    start=True, stop=True, is_transpose=True), s.sPE)
                mmT[0] = s.sPE.n
                s.W(V, s.sPE, mmT[0])
                s.I(V, lambda e: e.tensor_copy(Btr[:], pT0[:, 0:128]), s.sDV)
                evT[0] = s.sDV.n
                s.W("sync", s.daL)
                s.W("sync", s.sAC, ac_c0)
                s.W("sync", s.sDV, dv_c0)
                ce = (c + 1) * LCH - 1
                s.I("sync", lambda e, ce=ce: e.dma_start(
                    lr[:], s.d_aL[:, ce:ce + 1].rearrange("h 1 -> 1 h").broadcast_to([128, 64]),
                    allow_slow_non_contiguous=True), dLr, 16)
                lrv = dLr.n
                s.W(A, s.sDV, dv_c0)
                s.W(A, dLr, lrv)
                s.I(A, lambda e: e.activation(lamE[:], lr[:], Act.Exp), s.sAC)
                s.W(V, dLr, lrv)
                s.I(V, lambda e: e.tensor_sub(wE[:], lr[:], scanT[:, 64:128]), s.sDV)
                wev = s.sDV.n
                s.W(A, s.sDV, wev)
                s.I(A, lambda e: e.activation(wE[:], wE[:], Act.Exp))
                s.D(A)
                s.I(A, lambda e: e.activation(gamT[:], scanT[:, 64:128], Act.Exp), s.sAC)
                pre_ac = s.sAC.n
                for b8 in range(4):
                    pt = b8 % 2
                    s.W(TE, s.sDV, evT[pt])
                    for i in range(8):
                        tau = b8 * 8 + i
                        s.I(TE, lambda e, tau=tau, i=i, pt=pt, c=c: e.matmul(
                            pT[pt][:, i * 128:(i + 1) * 128], s.hsy[:, tau * T + c * LCH:tau * T + (c + 1) * LCH],
                            s.idnB[:], start=True, stop=True, is_transpose=True),
                            s.sPE if i == 7 else None)
                    mmT[pt] = s.sPE.n
                    s.W(V, s.sPE, mmT[pt])
                    s.W(V, s.sPE, pe_c0)
                    s.I(V, lambda e, b8=b8, pt=pt: e.tensor_copy(
                        xTc[:, b8 * 1024:(b8 + 1) * 1024], pT[pt][:]), s.sDV)
                    evT[pt] = s.sDV.n
                s.D(V)
                s.I(V, lambda e: e.tensor_tensor(
                    xTd[:].rearrange("p (h d) -> p h d", d=HD), xTc[:].rearrange("p (h d) -> p h d", d=HD),
                    scanT[:, 0:64].unsqueeze(2).broadcast_to([128, 64, HD]), Alu.mult), s.sDV)
                s.D(V)
                pre_dv[0] = s.sDV.n
                for g in range(8):
                    r = g % 2
                    cg = c * 8 + g
                    s.W("sync", s.sDV, pre_dv[0])
                    s.W("sync", s.sAC, emv_h[r])
                    for i in range(8):
                        h = 8 * g + i
                        s.I("sync", lambda e, h=h, i=i, r=r, c=c: e.dma_start(
                            bcA[r][:, i * 128:(i + 1) * 128],
                            s.d_aL[h:h + 1, c * LCH:(c + 1) * LCH].broadcast_to([128, LCH])),
                            dBc[r], 16)
                    gbv = dBc[r].n
                    s.W(V, dBc[r], gbv)
                    s.I(V, lambda e, r=r, g=g: e.tensor_tensor(
                        bcA[r][:].rearrange("p (h t) -> p h t", t=128),
                        bcA[r][:].rearrange("p (h t) -> p h t", t=128),
                        scanT[:, 64 + 8 * g:64 + 8 * g + 8].unsqueeze(2).broadcast_to([128, 8, 128]),
                        Alu.subtract))
                    s.D(V)
                    s.I(V, lambda e, r=r: e.tensor_scalar_min(bcA[r][:], bcA[r][:], 0.0), s.sDV)
                    dcv = s.sDV.n
                    s.W(A, s.sDV, dcv)
                    s.W(A, s.sGP, mhv_h[r])
                    s.I(A, lambda e, r=r: e.activation(eMA[r][:], bcA[r][:], Act.Exp), s.sAC)
                    emv = s.sAC.n
                    emv_h[r] = emv
                    s.W(G, s.sAC, emv)
                    s.W(G, s.sPE, mmA[r])
                    s.I(G, lambda e, r=r: e.tensor_tensor(
                        mhA[r][:].rearrange("p (h t) -> p h t", t=128),
                        eMA[r][:].rearrange("p (h t) -> p h t", t=128),
                        cbm[:].unsqueeze(1).broadcast_to([128, 8, 128]), Alu.mult), s.sGP)
                    mhv = s.sGP.n
                    mhv_h[r] = mhv
                    s.W(V, s.sAC, pre_ac)
                    s.W(V, s.sPE, mmC[r])
                    s.I(V, lambda e, r=r, g=g: e.tensor_tensor(
                        bwA[r][:].rearrange("p (h n) -> p h n", n=128),
                        Btr[:].unsqueeze(1).broadcast_to([128, 8, 128]),
                        wE[:, 8 * g:8 * g + 8].unsqueeze(2).broadcast_to([128, 8, 128]),
                        Alu.mult), s.sDV)
                    bwv = s.sDV.n
                    s.W(TE, s.sDV, evB[r])
                    if c > 0:
                        s.W(TE, s.sDV, supd[(c - 1, g)])
                    s.I(TE, lambda e, r=r, g=g, cs=cs: e.matmul(
                        pB[r][:], s.Ct_[:, cs], s.Sst16[:, g * 512:(g + 1) * 512],
                        start=True, stop=True), s.sPE)
                    mmB[r] = s.sPE.n
                    s.W(V, s.sPE, mmB[r])
                    s.W(V, s.sPE, mmA[r])
                    s.I(V, lambda e, r=r, g=g: e.tensor_tensor(
                        tsb[r][:].rearrange("p (h d) -> p h d", d=HD),
                        pB[r][:].rearrange("p (h d) -> p h d", d=HD),
                        gamT[:, 8 * g:8 * g + 8].unsqueeze(2).broadcast_to([128, 8, HD]),
                        Alu.mult), s.sDV)
                    evB[r] = s.sDV.n
                    s.W(TE, s.sGP, mhv)
                    s.W(TE, s.sDV, evA[r])
                    for i in range(8):
                        h = 8 * g + i
                        s.I(TE, lambda e, r=r, h=h, i=i: e.matmul(
                            pA[r][64 * (i % 2):64 * (i % 2) + 64, (i // 2) * 128:(i // 2) * 128 + 128],
                            xTd[:, h * HD:(h + 1) * HD], mhA[r][:, i * 128:(i + 1) * 128],
                            start=True, stop=True),
                            s.sPE if i == 7 else None)
                    mmA[r] = s.sPE.n
                    pt2 = g % 2
                    s.W(TE, s.sDV, evB[r])
                    s.W(TE, s.sDV, evT[pt2])
                    for j in range(4):
                        s.I(TE, lambda e, pt2=pt2, r=r, j=j: e.matmul(
                            pT[pt2][:, j * 128:(j + 1) * 128],
                            tsb[r][:, j * 128:(j + 1) * 128], s.idnB[:],
                            start=True, stop=True, is_transpose=True),
                            s.sPE if j == 3 else None)
                    mmT[pt2] = s.sPE.n
                    s.W(V, s.sPE, mmA[r])
                    s.W(V, s.sPE, mmT[pt2])
                    for j in range(4):
                        tau = 4 * g + j
                        s.I(V, lambda e, r=r, tau=tau, j=j, c=c: e.scalar_tensor_tensor(
                            s.hsy[:, tau * T + c * LCH:tau * T + (c + 1) * LCH],
                            s.hsy[:, tau * T + c * LCH:tau * T + (c + 1) * LCH],
                            s.dcol[:, tau:tau + 1], pA[r][:, j * 128:(j + 1) * 128],
                            Alu.mult, Alu.add))
                    s.D(V)
                    for j in range(4):
                        tau = 4 * g + j
                        s.I(V, lambda e, pt2=pt2, tau=tau, j=j, c=c: e.tensor_add(
                            s.hsy[:, tau * T + c * LCH:tau * T + (c + 1) * LCH],
                            s.hsy[:, tau * T + c * LCH:tau * T + (c + 1) * LCH],
                            pT[pt2][:, j * 128:(j + 1) * 128]), s.sDV if j == 3 else None)
                    evA[r] = s.sDV.n
                    evT[pt2] = s.sDV.n
                    s.W(TE, s.sDV, bwv)
                    s.W(TE, s.sDV, evC[r])
                    for i in range(8):
                        h = 8 * g + i
                        s.I(TE, lambda e, r=r, h=h, i=i: e.matmul(
                            pC[r][:, i * HD:(i + 1) * HD],
                            bwA[r][:, i * 128:(i + 1) * 128], xTd[:, h * HD:(h + 1) * HD],
                            start=(i == 0), stop=(i == 7)), s.sPE if i == 7 else None)
                    mmC[r] = s.sPE.n
                    s.W(V, s.sPE, mmC[r])
                    s.I(V, lambda e, g=g: e.tensor_tensor(
                        t1s[:].rearrange("p (h d) -> p h d", d=HD),
                        s.Sst[:, g * 512:(g + 1) * 512].rearrange("p (h d) -> p h d", d=HD),
                        lamE[:, 8 * g:8 * g + 8].unsqueeze(2).broadcast_to([128, 8, HD]),
                        Alu.mult))
                    s.D(V)
                    s.I(V, lambda e, r=r, g=g: e.tensor_add(
                        s.Sst[:, g * 512:(g + 1) * 512], t1s[:], pC[r][:]), s.sDV)
                    evC[r] = s.sDV.n
                    s.D(V)
                    s.I(V, lambda e, g=g: e.tensor_copy(
                        s.Sst16[:, g * 512:(g + 1) * 512], s.Sst[:, g * 512:(g + 1) * 512]), s.sDV)
                    supd[(c, g)] = s.sDV.n
                    if c == 0 and g == 0:
                        s.dbgsnap("mh00", lambda: mhA[0][:, 0:128], (128, 128), BF16)
                        s.dbgsnap("xtd0", lambda: xTd[:, 0:128], (128, 128), BF16)
                        s.dbgsnap("tsb0", lambda: tsb[0][:, 0:128], (128, 128), F32)
                        s.dbgsnap("xtc0", lambda: xTc[:, 0:128], (128, 128), BF16)
                        s.dbgsnap("scT0", lambda: scanT[:], (128, 128), F32)
                if c == 0:
                    s.dbg("ych0", lambda: s.hsy[:, 0:LCH], (128, LCH), BF16)
        dStI = s.sem("dStI")
        s.mx_dStI = dStI
        s.W("scalar", s.sDV)
        s.I("scalar", lambda e: e.dma_start(s.d_stin[:], s.Sst16[:]), dStI, 16)
        with s.nc.psum_tensor("afp", [1, 64], F32) as pf:
            s.I(TE, lambda e: e.matmul(
                pf[:], s.aG[:, T - 1:T], s.idnF[0:64, 0:64],
                start=True, stop=True, is_transpose=True), s.sPE)
            afr = s.sb("afrow", (1, NH * HD), F32)
            s.I(V, lambda e: e.memset(afr[:], 0.0))
            s.D(V)
            s.W(V, s.sPE)
            s.I(V, lambda e: e.tensor_copy(afr[:, 0:64], pf[:]), s.sDV)
            s.W("scalar", s.sDV)
            s.I("scalar", lambda e: e.dma_start(s.d_afin[:], afr[:, 0:64]), dStI, 16)
        s.W("gpsimd", dStI)
        s.I("gpsimd", lambda e: e.collective_compute(
            "AllGather", Alu.bypass, replica_groups=[list(range(NCORE))],
            ins=[s.d_stin[:]], outs=[s.d_stout[:]]), s.sAG)
        s.I("gpsimd", lambda e: e.collective_compute(
            "AllGather", Alu.bypass, replica_groups=[list(range(NCORE))],
            ins=[s.d_afin[:]], outs=[s.d_afout[:]]), s.sAG)
        self.sem_ag4 = s.sAG.n
        s.W("gpsimd", s.ln_dXst)
        s.I("gpsimd", lambda e: e.collective_compute(
            "AllGather", Alu.bypass, replica_groups=QUAD_GROUPS,
            ins=[s.d_xnsh[:]], outs=[s.d_xnq[:]]), s.sAG)
        self.sem_agx = s.sAG.n
        s.W("gpsimd", s.dq_dWq[0])
        s.W("gpsimd", s.dq_dWq[1])
        for src_, dst in [(s.d_wsh_g, s.d_wg), (s.d_wsh_u, s.d_wu),
                          (s.d_wsh_d, s.d_wd)]:
            s.I("gpsimd", lambda e, src_=src_, dst=dst: e.collective_compute(
                "AllGather", Alu.bypass, replica_groups=PAIR_GROUPS,
                ins=[src_[:]], outs=[dst[:]]), s.sAG)
        self.sem_ag3 = s.sAG.n
        s.dbg("Sloc", lambda: s.Sst[:], (128, NH * HD), F32)
        s.dbg("ymix", lambda: s.hsy[:, 0:T], (128, T), BF16)

    def ph_correction(self):
        s = self
        for eng in ["vector", "scalar", "gpsimd", "tensor", "sync"]:
            s.W(eng, s.sPE)
            s.W(eng, s.sAC)
            s.W(eng, s.sGP)
            s.W(eng, s.mx_dStI)
        T = self.T
        V, A, G, TE = "vector", "scalar", "gpsimd", "tensor"
        Sin = s.sb("Sinit", (128, NH * HD), BF16)
        Sld = [s.sb(f"Sld{i}", (128, NH * HD), BF16) for i in range(2)]
        afr8 = s.sb("afr8", (8, 64), F32)
        msT = s.sb("msT", (8, 8), F32)
        pmk = s.sb("pmk", (8, 1), F32)
        w8 = s.sb("w8", (8, 64), F32)
        wb = s.sb("wb", (128, 64), F32)
        aGT = s.sb("aGT", (128, 64), F32)
        ggT = s.sb("ggT", (128, 64), F32)
        tmc = [s.sb(f"tmc{i}", (128, 512), F32) for i in range(2)]
        s.I(V, lambda e: e.memset(Sin[:], 0.0), s.sDV)
        dAfr = s.sem("dAfr")
        dSld = [s.sem("dSld0"), s.sem("dSld1")]
        s.W("sync", s.sAG, self.sem_ag4)
        s.I("sync", lambda e: e.dma_start(afr8[:], s.d_afout[:]), dAfr, 16)
        s.I("sync", lambda e: e.dma_start(msT[:], s.d_msumT[:]), dAfr, 16)
        s.I("sync", lambda e: e.dma_start(pmk[:], s.d_pmask[:]), dAfr, 16)
        ldv = dAfr.n
        with s.nc.psum_tensor("cwp", [8, 64], F32) as pw:
            s.W(TE, dAfr, ldv)
            s.I(TE, lambda e: e.matmul(pw[:], msT[:], afr8[:], start=True, stop=True), s.sPE)
            s.W(A, s.sPE)
            s.I(A, lambda e: e.activation(w8[:], pw[:], Act.Exp), s.sAC)
            s.W(V, s.sAC)
            s.I(V, lambda e: e.tensor_scalar_mul(w8[:], w8[:], pmk[:]), s.sDV)
            wsv = s.sDV.n
        dW8 = s.sem("dW8")
        dWb = s.sem("dWb")
        s.W("scalar", s.sDV, wsv)
        s.I("scalar", lambda e: e.dma_start(s.d_w8[:], w8[:]), dW8, 16)
        for r in range(NCORE):
            rr = r % 2
            s.W("sync", s.sDV)
            s.I("sync", lambda e, r=r, rr=rr: e.dma_start(
                Sld[rr][:], s.d_stout[128 * r:128 * r + 128, :]), dSld[rr], 16)
            ldr = dSld[rr].n
            s.W("sync", dW8)
            s.I("sync", lambda e, r=r: e.dma_start(
                wb[:], s.d_w8[r:r + 1, :].broadcast_to([128, 64])), dWb, 16)
            gv = dWb.n
            s.W(V, dSld[rr], ldr)
            s.W(V, dWb, gv)
            s.I(V, lambda e, rr=rr: e.tensor_tensor(
                Sld[rr][:].rearrange("p (h d) -> p h d", d=HD),
                Sld[rr][:].rearrange("p (h d) -> p h d", d=HD),
                wb[:].unsqueeze(2).broadcast_to([128, 64, HD]), Alu.mult))
            s.D(V)
            s.I(V, lambda e, rr=rr: e.tensor_add(Sin[:], Sin[:], Sld[rr][:]), s.sDV)
            s.D(V)
        sinv = s.sDV.n
        s.dbg("Sinit", lambda: Sin[:], (128, NH * HD), BF16)
        with s.nc.psum_tensor("cpD0", [128, 512], F32) as pD0, \
             s.nc.psum_tensor("cpD1", [128, 512], F32) as pD1, \
             s.nc.psum_tensor("cpE0", [128, 512], F32) as pE0, \
             s.nc.psum_tensor("cpE1", [128, 512], F32) as pE1, \
             s.nc.psum_tensor("cpG", [128, 64], F32) as pG:
            pD, pE = [pD0, pD1], [pE0, pE1]
            mmD = [0, 0]; evD = [0, 0]; mmE = [0, 0]; evE = [0, 0]
            aGTv = 0
            for c in range(self.NCH):
                cs = slice(c * LCH, (c + 1) * LCH)
                s.W(TE, s.sAC, aGTv)
                s.I(TE, lambda e, cs=cs: e.matmul(
                    pG[:], s.aG[:, cs], s.idnF[0:64, 0:64],
                    start=True, stop=True, is_transpose=True), s.sPE)
                mgv = s.sPE.n
                s.W(V, s.sPE, mgv)
                s.I(V, lambda e: e.tensor_copy(aGT[:], pG[:]), s.sDV)
                agv = s.sDV.n
                s.W(A, s.sDV, agv)
                s.I(A, lambda e: e.activation(ggT[:], aGT[:], Act.Exp), s.sAC)
                aGTv = s.sAC.n
                for g in range(8):
                    r = g % 2
                    s.W(TE, s.sDV, evD[r])
                    s.W(TE, s.sDV, sinv)
                    s.I(TE, lambda e, r=r, g=g, cs=cs: e.matmul(
                        pD[r][:], s.Ct_[:, cs], Sin[:, g * 512:(g + 1) * 512],
                        start=True, stop=True), s.sPE)
                    mmD[r] = s.sPE.n
                    s.W(V, s.sPE, mmD[r])
                    s.W(V, s.sPE, mmE[r])
                    s.W(V, s.sAC, aGTv)
                    s.I(V, lambda e, r=r, g=g: e.tensor_tensor(
                        tmc[r][:].rearrange("p (h d) -> p h d", d=HD),
                        pD[r][:].rearrange("p (h d) -> p h d", d=HD),
                        ggT[:, 8 * g:8 * g + 8].unsqueeze(2).broadcast_to([128, 8, HD]),
                        Alu.mult), s.sDV)
                    evD[r] = s.sDV.n
                    s.W(TE, s.sDV, evD[r])
                    s.W(TE, s.sDV, evE[r])
                    for j in range(4):
                        s.I(TE, lambda e, r=r, j=j: e.matmul(
                            pE[r][:, j * 128:(j + 1) * 128],
                            tmc[r][:, j * 128:(j + 1) * 128], s.idnF[:],
                            start=True, stop=True, is_transpose=True),
                            s.sPE if j == 3 else None)
                    mmE[r] = s.sPE.n
                    s.W(V, s.sPE, mmE[r])
                    for j in range(4):
                        tau = 4 * g + j
                        s.I(V, lambda e, r=r, tau=tau, j=j, c=c: e.tensor_add(
                            s.hsy[:, tau * T + c * LCH:tau * T + (c + 1) * LCH],
                            s.hsy[:, tau * T + c * LCH:tau * T + (c + 1) * LCH],
                            pE[r][:, j * 128:(j + 1) * 128]), s.sDV if j == 3 else None)
                    evE[r] = s.sDV.n
        s.dbg("ycorr", lambda: s.hsy[:, 0:T], (128, T), BF16)

    def ph_gating(self):
        s = self
        for eng in ["sync", "scalar", "vector", "tensor", "gpsimd"]:
            s.W(eng, s.sPE)
            s.W(eng, s.sDV)
        T, TS, NTS = self.T, self.TSEG, self.NTS
        V, A, G, TE = "vector", "scalar", "gpsimd", "tensor"
        _gsc = contextlib.ExitStack()
        _sb1 = lambda n, sh, dt_: _gsc.enter_context(s.nc.sbuf_tensor(n, list(sh), dt_))
        gb = _sb1("gateb", (128, NGT * TS), BF16)
        sg = _sb1("sgb", (128, NGT * TS), BF16)
        ygt = [_sb1(f"ygt{i}", (128, TS), F32) for i in range(2)]
        ysq = [_sb1(f"ysq{i}", (128, TS), F32) for i in range(2)]
        rr0 = _sb1("rr0", (1, TS), F32)
        rr1 = _sb1("rr1", (1, TS), F32)
        rr2 = _sb1("rr2", (1, TS), F32)
        rr3 = _sb1("rr3", (1, TS), F32)
        rsqb = _sb1("rsqb", (128, TS), F32)
        s.dRbG = s.sem("dRbG")
        dGb = s.sem("dGb")
        s.W("sync", s.dGst[0])
        s.W("sync", s.dGst[1])
        for th in range(NTS):
            with s.nc.psum_tensor(f"gstat{th}", [1, TS], F32) as pst:
                s.W("sync", s.sDV)
                for tau in range(NGT):
                    s.I("sync", lambda e, tau=tau, th=th: e.dma_start(
                        gb[:, tau * TS:(tau + 1) * TS],
                        s.d_gsp[128 * tau:128 * (tau + 1), th * TS:(th + 1) * TS]), dGb, 16)
                ldv = dGb.n
                for tau in range(NGT):
                    r = tau % 2
                    s.W(A, dGb, ldv)
                    s.W(A, s.sDV)
                    s.I(A, lambda e, tau=tau: e.activation(
                        sg[:, tau * TS:(tau + 1) * TS], gb[:, tau * TS:(tau + 1) * TS], Act.Silu), s.sAC)
                    s.W(V, s.sAC)
                    s.W(V, s.sPE)
                    s.I(V, lambda e, tau=tau, r=r, th=th: e.tensor_mul(
                        ygt[r][:], s.hsy[:, tau * T + th * TS:tau * T + (th + 1) * TS],
                        sg[:, tau * TS:(tau + 1) * TS]), s.sDV)
                    s.W(A, s.sDV)
                    s.I(A, lambda e, r=r: e.activation(ysq[r][:], ygt[r][:], Act.Square), s.sAC)
                    s.W(TE, s.sAC)
                    s.I(TE, lambda e, tau=tau, r=r: e.matmul(
                        pst[:], s.ones1[:], ysq[r][:], start=(tau == 0), stop=(tau == NGT - 1)), s.sPE)
                s.W(V, s.sPE)
                s.I(V, lambda e: e.tensor_copy(rr0[:], pst[:]), s.sDV)
                s.D(V)
            s.I(V, lambda e: e.tensor_scalar(rr1[:], rr0[:], 1.0 / INTER, EPS, Alu.mult, Alu.add), s.sDV)
            s.D(V)
            s.W(A, s.sDV)
            s.I(A, lambda e: e.activation(rr2[:], rr1[:], Act.Sqrt), s.sAC)
            s.W(V, s.sAC)
            s.I(V, lambda e: e.reciprocal(rr3[:], rr2[:]))
            s.D(V)
            s.I(V, lambda e: e.tensor_mul(rr2[:], rr3[:], rr3[:]))
            s.D(V)
            s.I(V, lambda e: e.tensor_mul(rr2[:], rr2[:], rr1[:]))
            s.D(V)
            s.I(V, lambda e: e.tensor_scalar(rr2[:], rr2[:], -0.5, 1.5, Alu.mult, Alu.add))
            s.D(V)
            s.I(V, lambda e: e.tensor_mul(rr3[:], rr3[:], rr2[:]), s.sDV)
            s.W("scalar", s.sDV)
            s.I("scalar", lambda e, th=th: e.dma_start(s.d_rb[2:3, 0:TS], rr3[:]), s.dRbG, 16)
            s.W("sync", s.dRbG)
            s.I("sync", lambda e, th=th: e.dma_start(
                rsqb[:], s.d_rb[2:3, 0:TS].broadcast_to([128, TS])), s.dRbG, 16)
            s.W(V, s.dRbG)
            for tau in range(NGT):
                r = tau % 2
                s.I(V, lambda e, tau=tau, r=r, th=th: e.tensor_mul(
                    ygt[r][:], s.hsy[:, tau * T + th * TS:tau * T + (th + 1) * TS],
                    sg[:, tau * TS:(tau + 1) * TS]))
                s.D(V)
                s.I(V, lambda e, tau=tau, r=r, th=th: e.scalar_tensor_tensor(
                    s.hsy[:, tau * T + th * TS:tau * T + (th + 1) * TS],
                    ygt[r][:], s.nwcol[:, tau:tau + 1], rsqb[:], Alu.mult, Alu.mult),
                    s.sDV if tau == NGT - 1 else None)
        yn_done = s.sDV.n
        s.dbg("yn", lambda: s.hsy[:, 0:T], (128, T), BF16)
        _gsc.close()
        for eng in ["sync", "scalar", "vector", "tensor"]:
            s.W(eng, s.sPE)
            s.W(eng, s.sDV)
            s.W(eng, s.sAC)
        HOG = 2
        wob = [s.sb(f"op_wb{i}", (128, NGT * HOG * 128), BF16) for i in range(2)]
        sst = [s.sb(f"op_s{i}", (128, TS), BF16) for i in range(2)]
        with s.nc.psum_tensor("opp0", [128, 512], F32) as q0, \
             s.nc.psum_tensor("opp1", [128, 512], F32) as q1:
            qq = [q0, q1]
            evq = [0, 0]
            dOw = [s.sem("dOw0"), s.sem("dOw1")]
            mm_hg = [0, 0]
            s.dSst = [s.sem("dSst0"), s.sem("dSst1")]
            s.W("gpsimd", s.sDV, yn_done)
            for hg in range(16 // HOG):
                wb = hg % 2
                s.W("gpsimd", s.sPE, mm_hg[wb])
                for ch in range(NGT):
                    s.I("gpsimd", lambda e, ch=ch, hg=hg, wb=wb: e.dma_start(
                        wob[wb][:, ch * HOG * 128:(ch + 1) * HOG * 128],
                        s.d_wout[128 * ch:128 * (ch + 1), HOG * 128 * hg:HOG * 128 * (hg + 1)]), dOw[wb], 16)
                s.W(TE, dOw[wb])
                s.W(TE, s.sDV, yn_done)
                for th in range(NTS):
                    for oi in range(HOG):
                        r = (th * HOG + oi) % 2
                        s.W(TE, s.sDV, evq[r])
                        for ch in range(NGT):
                            s.I(TE, lambda e, ch=ch, oi=oi, th=th, r=r, wb=wb: e.matmul(
                                qq[r][:, 0:TS], wob[wb][:, ch * HOG * 128 + oi * 128:ch * HOG * 128 + (oi + 1) * 128],
                                s.hsy[:, ch * T + th * TS:ch * T + (th + 1) * TS],
                                start=(ch == 0), stop=(ch == NGT - 1)),
                                s.sPE if ch == NGT - 1 else None)
                        mmq = s.sPE.n
                        s.W(V, s.sPE, mmq)
                        s.W(V, s.dSst[r])
                        s.I(V, lambda e, r=r: e.tensor_copy(sst[r][:], qq[r][:, 0:TS]), s.sDV)
                        evq[r] = s.sDV.n
                        ho = hg * HOG + oi
                        s.W("scalar", s.sDV, evq[r])
                        s.I("scalar", lambda e, ho=ho, th=th, r=r: e.dma_start(
                            s.d_ssp[128 * ho:128 * (ho + 1), th * TS:(th + 1) * TS], sst[r][:]), s.dSst[r], 16)
                mm_hg[wb] = s.sPE.n

    def ph_ffn(self):
        s = self
        for eng in ["sync", "scalar", "vector", "tensor"]:
            s.W(eng, s.sPE)
            s.W(eng, s.dSst[0])
            s.W(eng, s.dSst[1])
            if s.debug:
                s.W(eng, s.sDP)
        T, TS, NTS = self.T, self.TSEG, self.NTS
        V, A, TE = "vector", "scalar", "tensor"
        NB = 512
        NBLK = 4096 // NB
        wgb = s.sb("f_wg", (128, NKT * NB), BF16)
        wub = s.sb("f_wu", (128, NKT * NB), BF16)
        wdb = s.sb("f_wd", (128, (NB // 128) * HID), BF16)
        itb = s.sb("f_it", (128, (NB // 128) * T), BF16)
        glt = [s.sb(f"f_gl{i}", (128, TS), BF16) for i in range(2)]
        dab = s.sb("f_dac", (128, 16 * T), F32)
        xnb = s.sb("f_xnb", (128, NKT * T), BF16)
        dWg = s.sem("dWg")
        dWd = s.sem("dWd")
        dXnb = s.sem("dXnb")
        dYp = s.sem("dYp")
        s.W("sync", s.sAG, self.sem_agx)
        s.W("sync", s.sAG, self.sem_ag3)
        with s.nc.psum_tensor("fp0", [128, 512], F32) as f0, \
             s.nc.psum_tensor("fp1", [128, 512], F32) as f1, \
             s.nc.psum_tensor("fp2", [128, 512], F32) as f2, \
             s.nc.psum_tensor("fp3", [128, 512], F32) as f3, \
             s.nc.psum_tensor("fp4", [128, 512], F32) as f4, \
             s.nc.psum_tensor("fp5", [128, 512], F32) as f5:
            pg = [f0, f2]
            pu = [f1, f3]
            pdn = [f4, f5]
            ev_g = [0, 0]
            ev_d = [0, 0]
            mm_gu_done = 0
            mm_dn_done = 0
            yp_wait = 0
            for bb in range(4):
                s.W("sync", s.sPE, mm_gu_done)
                for k in range(NKT):
                    s.I("sync", lambda e, k=k, bb=bb: e.dma_start(
                        xnb[:, k * T:(k + 1) * T],
                        s.d_xnq[2048 * bb + 128 * k:2048 * bb + 128 * (k + 1), :]), dXnb, 16)
                ld_xn = dXnb.n
                for b in range(NBLK):
                    s.W("sync", s.sPE, mm_gu_done)
                    for k in range(NKT):
                        s.I("sync", lambda e, k=k, b=b: e.dma_start(
                            wgb[:, k * NB:(k + 1) * NB], s.d_wg[128 * k:128 * (k + 1), NB * b:NB * (b + 1)]), dWg, 16)
                        s.I("sync", lambda e, k=k, b=b: e.dma_start(
                            wub[:, k * NB:(k + 1) * NB], s.d_wu[128 * k:128 * (k + 1), NB * b:NB * (b + 1)]), dWg, 16)
                    ld_gu = dWg.n
                    s.W("sync", s.sPE, mm_dn_done)
                    for jc in range(NB // 128):
                        s.I("sync", lambda e, jc=jc, b=b: e.dma_start(
                            wdb[:, jc * HID:(jc + 1) * HID],
                            s.d_wd[NB * b + 128 * jc:NB * b + 128 * (jc + 1), :]), dWd, 16)
                    ld_d = dWd.n
                    s.W(TE, dWg, ld_gu)
                    if b == 0:
                        s.W(TE, dXnb, ld_xn)
                    for j in range(NB // 128):
                        par, w0 = j % 2, 256 * (j // 2)
                        for th in range(NTS):
                            r = (j * NTS + th) % 2
                            s.W(TE, s.sDV, ev_g[r])
                            for k in range(NKT):
                                s.I(TE, lambda e, k=k, j=j, th=th, r=r, par=par, w0=w0: e.matmul(
                                    pg[r][:, 0:TS],
                                    wgb[:, k * NB + w0 + par:k * NB + w0 + par + 255:2],
                                    xnb[:, k * T + th * TS:k * T + (th + 1) * TS],
                                    start=(k == 0), stop=(k == NKT - 1)))
                            for k in range(NKT):
                                s.I(TE, lambda e, k=k, j=j, th=th, r=r, par=par, w0=w0: e.matmul(
                                    pu[r][:, 0:TS],
                                    wub[:, k * NB + w0 + par:k * NB + w0 + par + 255:2],
                                    xnb[:, k * T + th * TS:k * T + (th + 1) * TS],
                                    start=(k == 0), stop=(k == NKT - 1)),
                                    s.sPE if k == NKT - 1 else None)
                            mmv = s.sPE.n
                            s.W(A, s.sPE, mmv)
                            s.I(A, lambda e, r=r: e.activation(glt[r][:], pg[r][:, 0:TS], Act.Gelu_apprx_tanh), s.sAC)
                            s.W(V, s.sAC)
                            s.I(V, lambda e, j=j, th=th, r=r: e.tensor_mul(
                                itb[:, j * T + th * TS:j * T + (th + 1) * TS], glt[r][:], pu[r][:, 0:TS]), s.sDV)
                            ev_g[r] = s.sDV.n
                    mm_gu_done = s.sPE.n
                    it_done = s.sDV.n
                    s.W(TE, dWd, ld_d)
                    s.W(TE, s.sDV, it_done)
                    if b == 0:
                        s.W(V, dYp, yp_wait)
                    for th in range(NTS):
                        for h in range(16):
                            r = (th * 16 + h) % 2
                            s.W(TE, s.sDV, ev_d[r])
                            for jc in range(NB // 128):
                                s.I(TE, lambda e, jc=jc, h=h, th=th, r=r: e.matmul(
                                    pdn[r][:, 0:TS], wdb[:, jc * HID + h * 128:jc * HID + (h + 1) * 128],
                                    itb[:, jc * T + th * TS:jc * T + (th + 1) * TS],
                                    start=(jc == 0), stop=(jc == NB // 128 - 1)),
                                    s.sPE if jc == NB // 128 - 1 else None)
                            mmv = s.sPE.n
                            s.W(V, s.sPE, mmv)
                            if b == 0:
                                s.I(V, lambda e, h=h, th=th, r=r: e.tensor_copy(
                                    dab[:, h * T + th * TS:h * T + (th + 1) * TS], pdn[r][:, 0:TS]), s.sDV)
                            else:
                                s.I(V, lambda e, h=h, th=th, r=r: e.tensor_add(
                                    dab[:, h * T + th * TS:h * T + (th + 1) * TS],
                                    dab[:, h * T + th * TS:h * T + (th + 1) * TS], pdn[r][:, 0:TS]), s.sDV)
                            ev_d[r] = s.sDV.n
                    mm_dn_done = s.sPE.n
                dn_done = s.sDV.n
                s.W("gpsimd", s.sDV, dn_done)
                for h in range(16):
                    s.I("gpsimd", lambda e, h=h, bb=bb: e.dma_start(
                        s.d_yp[2048 * bb + 128 * h:2048 * bb + 128 * (h + 1), :],
                        dab[:, h * T:(h + 1) * T]), dYp, 16)
                yp_wait = dYp.n
        s.W("gpsimd", dYp)
        s.I("gpsimd", lambda e: e.collective_compute(
            "ReduceScatter", Alu.add, replica_groups=QUAD_GROUPS,
            ins=[s.d_yp[:]], outs=[s.d_yrs[:]]), s.sAG)
        self.sem_rs = s.sAG.n

    def ph_combine(self):
        s = self
        T = self.T
        V, A = "vector", "scalar"
        gt = [s.sb(f"cb_g{i}", (128, T), BF16) for i in range(2)]
        sre = [s.sb(f"cb_s{i}", (128, T), BF16) for i in range(2)]
        xre = [s.sb(f"cb_x{i}", (128, T), F32) for i in range(2)]
        yre = [s.sb(f"cb_y{i}", (128, T), BF16) for i in range(2)]
        ost = [s.sb(f"cb_o{i}", (128, T), F32) for i in range(2)]
        out_sem = [s.sem("outD0"), s.sem("outD1")]
        dCmb = [s.sem("dCmb0"), s.sem("dCmb1")]
        s.W("sync", s.dSst[0])
        s.W("sync", s.dSst[1])
        s.W("sync", s.sAG, self.sem_rs)
        for ho in range(16):
            r = ho % 2
            s.W("sync", s.sDV)
            s.I("sync", lambda e, ho=ho, r=r: e.dma_start(
                sre[r][:], s.d_ssp[128 * ho:128 * (ho + 1), :]), dCmb[r], 16)
            s.I("sync", lambda e, ho=ho, r=r: e.dma_start(
                xre[r][:], s.d_x[128 * ho:128 * (ho + 1), 3:3 + T]), dCmb[r], 16)
            s.I("sync", lambda e, ho=ho, r=r: e.dma_start(
                yre[r][:], s.d_yrs[128 * ho:128 * (ho + 1), :]), dCmb[r], 16)
            ldv = dCmb[r].n
            s.W(A, dCmb[r], ldv)
            s.I(A, lambda e, r=r: e.activation(
                gt[r][:], yre[r][:], Act.Silu), s.sAC)
            s.W(V, s.sAC)
            s.W(V, dCmb[r], ldv)
            s.W(V, out_sem[r])
            s.I(V, lambda e, r=r: e.tensor_mul(ost[r][:], sre[r][:], gt[r][:]))
            s.D(V)
            s.I(V, lambda e, r=r: e.tensor_add(ost[r][:], ost[r][:], xre[r][:]), s.sDV)
            s.W("scalar", s.sDV)
            s.I("scalar", lambda e, ho=ho, r=r: e.dma_start(
                s.d_out[128 * ho:128 * (ho + 1), :], ost[r][:]), out_sem[r], 16)
        s.W("scalar", out_sem[0])
        s.W("scalar", out_sem[1])


_PERM = _perm_hid()


def _stage_inputs(c, T, x, in_w, conv_w, conv_b, dt_bias, A_log, D, norm_w, out_w,
                  gW, gS, uW, uS, dW, dS_):
    P = _PERM
    seq, sl = c // 4, (c % 4) * T
    xin = np.zeros((HID, 3 + T), np.float32)
    xin[:, 3:] = x[seq, sl:sl + T, :].T[P]
    if sl > 0:
        xin[:, 0:3] = x[seq, sl - 3:sl, :].T[P]
    win = np.ascontiguousarray(in_w.T[P])
    wout = np.ascontiguousarray(out_w.T[:, P])
    cw = np.ascontiguousarray(conv_w[:, 0, :].reshape(NHT, 128, 4).transpose(1, 0, 2).reshape(128, NHT * 4))
    cb = np.ascontiguousarray(conv_b.reshape(NHT, 128).T)
    nw = np.ascontiguousarray(norm_w.reshape(NGT, 128).T)
    dcol = np.empty((128, NGT), np.float32)
    for t in range(NGT):
        dcol[0:64, t] = D[2 * t]
        dcol[64:128, t] = D[2 * t + 1]
    g8 = gW.view(np.uint8).reshape(16384, 1024, 4)[:, :, 0]
    u8 = uW.view(np.uint8).reshape(16384, 1024, 4)[:, :, 0]
    d8 = dW.view(np.uint8).reshape(HID, 8192, 4)[:, :, 0]
    tc, jc = c // 4, c % 4
    cg = np.ascontiguousarray(g8[4096 * jc:4096 * (jc + 1), 512 * tc:512 * (tc + 1)].T)
    cu = np.ascontiguousarray(u8[4096 * jc:4096 * (jc + 1), 512 * tc:512 * (tc + 1)].T)
    cd = np.ascontiguousarray(d8[P][:, 2048 * jc + 1024 * tc:2048 * jc + 1024 * (tc + 1)].T)
    sg = np.ascontiguousarray(gS[4096 * jc:4096 * (jc + 1), 32 * tc:32 * (tc + 1)].T)
    su = np.ascontiguousarray(uS[4096 * jc:4096 * (jc + 1), 32 * tc:32 * (tc + 1)].T)
    sd = np.ascontiguousarray(dS_[P][:, 128 * jc + 64 * tc:128 * jc + 64 * (tc + 1)].T)
    msumT = np.zeros((8, 8), np.float32)
    pmask = np.zeros((8, 1), np.float32)
    base = 4 * (c // 4)
    for r in range(base, c):
        pmask[r, 0] = 1.0
        for rp in range(r + 1, c):
            msumT[rp, r] = 1.0
    return dict(xin=xin, win=win, wout=wout, convw=cw, convb=cb,
                dtb=np.asarray(dt_bias, np.float32).reshape(64, 1),
                alog=np.asarray(A_log, np.float32).reshape(64, 1),
                dcol=dcol, nw=nw, cg=cg, cu=cu, cd=cd, sg=sg, su=su,
                sd=sd, msumT=msumT, pmask=pmask)


def build_nc(T=1024, debug=False):
    b = Builder(T, debug=debug)
    nc = b.build()
    return nc, b


def kernel(L0_attn_hidden_states, L0_attn_in_proj_w, L0_attn_conv_w, L0_attn_conv_b,
           L0_attn_dt_bias, L0_attn_A_log, L0_attn_D, L0_attn_norm_weight,
           L0_attn_out_proj_w, L0_ffn_gate_W_packed, L0_ffn_gate_scales,
           L0_ffn_up_W_packed, L0_ffn_up_scales, L0_ffn_down_W_packed,
           L0_ffn_down_scales):
    T = 1024
    x = np.asarray(L0_attn_hidden_states, np.float32)
    args = [x, np.asarray(L0_attn_in_proj_w, np.float32),
            np.asarray(L0_attn_conv_w, np.float32), np.asarray(L0_attn_conv_b, np.float32),
            np.asarray(L0_attn_dt_bias, np.float32), np.asarray(L0_attn_A_log, np.float32),
            np.asarray(L0_attn_D, np.float32), np.asarray(L0_attn_norm_weight, np.float32),
            np.asarray(L0_attn_out_proj_w, np.float32),
            np.asarray(L0_ffn_gate_W_packed, np.int32), np.asarray(L0_ffn_gate_scales, np.float32),
            np.asarray(L0_ffn_up_W_packed, np.int32), np.asarray(L0_ffn_up_scales, np.float32),
            np.asarray(L0_ffn_down_W_packed, np.int32), np.asarray(L0_ffn_down_scales, np.float32)]
    in_maps = [_stage_inputs(c, T, *args) for c in range(NCORE)]
    nc, _ = build_nc(T)
    res = run_bass_kernel_spmd(nc, in_maps, core_ids=list(range(NCORE)))
    out = np.empty((2, 4096, HID), np.float32)
    P = _PERM
    for c in range(NCORE):
        seq, sl = c // 4, (c % 4) * T
        out[seq, sl:sl + T][:, P] = np.asarray(res.results[c]["out"]).T
    return out
